# version 17
# baseline (speedup 1.0000x reference)
import numpy as np
import concourse.bass as bass
import concourse.mybir as mybir
from concourse.bass_utils import run_bass_kernel_spmd

F32 = mybir.dt.float32
BF16 = mybir.dt.bfloat16
AF = mybir.ActivationFunctionType
ALU = mybir.AluOpType
AX = mybir.AxisListType.X

L = 4096
D = 1024
DIN = 2048
NST = 16
RDT = 64
MAGIC = 12582912.0
NT128 = L // 128
NCORE = 8

IDENT_NP = np.eye(128, dtype=np.float32)


def bcast_row(dram, row, n, part=128):
    return bass.AP(dram, row * n, [[0, part], [1, n]])



def emit_quant_dve(vector, src, dst_bf16, t, nblk=32, blk=32):
    v3 = lambda h: h[:, :].rearrange("p (a c) -> p a c", c=blk)
    sc_b = lambda h: bass.AP(h, 0, [[nblk, 128], [1, nblk], [0, blk]])
    V = vector
    V.reduce_max(t["sc0"][:, :], v3(src), axis=AX, apply_absolute_value=True)
    V.drain()
    V.tensor_scalar(t["sc"][:, :], t["sc0"][:, :], 1.0 / 7.5, 1e-30,
                    ALU.mult, ALU.max)
    V.drain()
    V.reciprocal(t["si"][:, :], t["sc"][:, :])
    V.drain()
    V.tensor_tensor(v3(t["v"]), v3(src), sc_b(t["si"]), ALU.mult)
    V.drain()
    V.tensor_tensor(t["q4"][:, :], t["v"][:, :], t["v"][:, :], ALU.mult)
    V.drain()
    V.tensor_scalar(t["g2"][:, :], t["q4"][:, :], 4.0, None, ALU.is_ge)
    V.tensor_scalar(t["g4"][:, :], t["q4"][:, :], 16.0, None, ALU.is_ge)
    V.drain()
    V.tensor_scalar(t["w1"][:, :], t["g2"][:, :], -4.0, 8.0, ALU.mult, ALU.add)
    V.tensor_scalar(t["sn"][:, :], t["g2"][:, :], -0.125, -0.125,
                    ALU.mult, ALU.add)
    V.drain()
    V.scalar_tensor_tensor(t["w1"][:, :], t["g4"][:, :], -2.0, t["w1"][:, :],
                           ALU.mult, ALU.add)
    V.scalar_tensor_tensor(t["sn"][:, :], t["g4"][:, :], -0.25, t["sn"][:, :],
                           ALU.mult, ALU.add)
    V.drain()
    V.tensor_tensor(t["tt"][:, :], t["v"][:, :], t["w1"][:, :], ALU.mult)
    V.drain()
    V.tensor_scalar(t["z"][:, :], t["tt"][:, :], -1.0, 0.5, ALU.mult, ALU.add)
    V.drain()
    V.tensor_scalar(t["rr"][:, :], t["z"][:, :], MAGIC, MAGIC,
                    ALU.add, ALU.subtract)
    V.drain()
    V.tensor_tensor(t["gt"][:, :], t["rr"][:, :], t["z"][:, :], ALU.is_gt)
    V.drain()
    V.tensor_tensor(t["ff"][:, :], t["rr"][:, :], t["gt"][:, :], ALU.subtract)
    V.drain()
    V.tensor_tensor(t["tt"][:, :], t["ff"][:, :], t["sn"][:, :], ALU.mult)
    V.drain()
    return V.tensor_tensor(v3(dst_bf16), v3(t["tt"]), sc_b(t["sc"]), ALU.mult)


def emit_ln_stats_dve(v, xt, st, j=0):
    F = xt.shape[1]
    h = F // 2
    v.bn_stats(st["stat12"][:, 0:6], xt[:, 0:h])
    v.bn_stats(st["stat12"][:, 6:12], xt[:, h:F])
    v.drain()
    return v.bn_aggr(st["mv"][:, 2 * j:2 * j + 2], st["stat12"][:, :])


def emit_ln_stats_act(sc, st, j=0):
    sc.activation(st["negm"][:, j:j + 1], st["mv"][:, 2 * j:2 * j + 1],
                  AF.Copy, scale=-1.0)
    sc.activation(st["lnv"][:, :], st["mv"][:, 2 * j + 1:2 * j + 2], AF.Ln,
                  bias=st["eps"][:, :1])
    sc.drain()
    return sc.activation(st["r"][:, j:j + 1], st["lnv"][:, :], AF.Exp,
                         scale=-0.5)


def alloc_quant_scratch(nc, pfx="", es=None):
    if es is None:
        mk = lambda n, sh: nc.alloc_sbuf_tensor(n, sh, F32)
    else:
        mk = lambda n, sh: es.enter_context(nc.sbuf_tensor(n, sh, F32))
    t = {k: mk(pfx + "q_" + k, [128, D])
         for k in ["v", "q4", "g2", "g4", "w1", "sn", "tt",
                   "z", "rr", "gt", "ff"]}
    for k in ["sc0", "sc", "si"]:
        t[k] = mk(pfx + "q_" + k, [128, 32])
    return t


def dve_fence(v, dummy=None):
    return v.drain()


def act_fence(sc, dummy=None):
    return sc.drain()



class SemPool:
    def __init__(self, nc):
        self.nc = nc
        self.sems = {}
        self.name_of = {}
        self.base = {}
        self.cur = {}

    def get(self, name):
        if name not in self.sems:
            s = self.nc.alloc_semaphore(name=name)
            self.sems[name] = s
            self.name_of[id(s)] = name
            self.base[name] = 0
        return self.sems[name]

    def begin(self):
        self.cur = {}

    def end(self):
        for n, v in self.cur.items():
            self.base[n] += v
        self.cur = {}

    def note_inc(self, sem, n):
        nm = self.name_of[id(sem)]
        self.cur[nm] = self.cur.get(nm, 0) + n

    def base_of(self, sem):
        return self.base[self.name_of[id(sem)]]

    def abs_total(self, name):
        return self.base.get(name, 0) + self.cur.get(name, 0)


class InstrWrap:
    def __init__(self, instr, pool):
        self._i, self._p = instr, pool

    def then_inc(self, sem, n):
        self._p.note_inc(sem, n)
        r = self._i.then_inc(sem, n)
        return InstrWrap(r, self._p) if hasattr(r, "then_inc") else r

    def __getattr__(self, k):
        return getattr(self._i, k)


class EngWrap:
    def __init__(self, eng, pool):
        self._e, self._p = eng, pool

    def wait_ge(self, sem, v):
        r = self._e.wait_ge(sem, v + self._p.base_of(sem))
        return InstrWrap(r, self._p) if hasattr(r, "then_inc") else r

    def wait_abs(self, sem, v):
        r = self._e.wait_ge(sem, v)
        return InstrWrap(r, self._p) if hasattr(r, "then_inc") else r

    def __getattr__(self, k):
        a = getattr(self._e, k)
        if not callable(a):
            return a

        def call(*args, **kw):
            r = a(*args, **kw)
            return InstrWrap(r, self._p) if hasattr(r, "then_inc") else r
        return call


class BlockWrap:
    ENGS = ("sync", "gpsimd", "vector", "scalar", "tensor")

    def __init__(self, block, pool, gates):
        self._b, self._p, self._g = block, pool, gates
        self._gated = set()

    def _mk(self, engname):
        dec = getattr(self._b, engname)
        pool, gates, gated = self._p, self._g, self._gated

        def deco(fn):
            def inner(eng):
                w = EngWrap(eng, pool)
                if engname not in gated:
                    gated.add(engname)
                    for s, v in gates:
                        w.wait_abs(s, v)
                return fn(w)
            return dec(inner)
        return deco

    @property
    def sync(self):
        return self._mk("sync")

    @property
    def gpsimd(self):
        return self._mk("gpsimd")

    @property
    def vector(self):
        return self._mk("vector")

    @property
    def scalar(self):
        return self._mk("scalar")

    @property
    def tensor(self):
        return self._mk("tensor")



DEBUG_L0FFN = False


def emit_l0ffn(nc, block, pool, pfx, gates, io):
    from contextlib import ExitStack
    es = ExitStack()
    sbt = lambda n, sh, dt: es.enter_context(nc.sbuf_tensor(pfx + n, sh, dt))
    pst = lambda n, sh, dt: es.enter_context(nc.psum_tensor(pfx + n, sh, dt))
    block = BlockWrap(block, pool, gates)
    pool.begin()
    dbg = {}
    x_in, x_out = io["x_in"], io["x_out"]
    w_in, b_in, ident = io["w"], io["b"], io["ident"]

    wr = [sbt(f"wr{j}", [128, D], F32) for j in range(2)]
    wq = sbt("wq", [128, D], BF16)
    wT = sbt("wT", [128, 8 * D], BF16)
    identb = sbt("identb", [128, 128], BF16)
    xt = [sbt(f"xt{j}", [128, D], F32) for j in range(2)]
    xn = sbt("xn", [128, D], F32)
    xq = sbt("xq", [128, D], BF16)
    xfm = sbt("xfm", [128, D], BF16)
    biasb = sbt("biasb", [128, D], F32)
    ty = sbt("ty", [128, 512], F32)
    xo = [sbt(f"xo{j}", [128, D], F32) for j in range(2)]
    t = alloc_quant_scratch(nc, pfx, es)
    st = {k: sbt("s_" + k, [128, 1], F32) for k in ["lnv", "eps"]}
    st["negm"] = sbt("s_negm", [128, 2], F32)
    st["r"] = sbt("s_r", [128, 2], F32)
    st["mv"] = sbt("s_mv", [128, 4], F32)
    st["stat12"] = sbt("s_stat12", [128, 12], F32)
    fv = sbt("fv", [128, 1], F32)
    fa = sbt("fa", [128, 1], F32)
    dbq = sbt("dbq", [128, D], F32)
    dbs = sbt("dbs", [128, 4], F32)

    ptr = pst("ptr", [128, 1024], BF16)
    py = [pst(f"py{j}", [128, 512], F32) for j in range(2)]

    sem = {n: pool.get(n) for n in
           ["ldw0", "ldw1", "ldi", "ldb", "wq_rdy", "wtr", "wev", "ldx0", "ldx1",
            "v2a", "a2v", "xq_rdy", "xtr", "xev", "mm", "ep", "so0", "so1",
            "cst",
            "dbg", "dbd", "mvr"]}

    ptr3 = bass.AP(ptr, 0, [[1024, 128], [128, 8], [1, 128]])

    if True:
        @block.gpsimd
        def _(g):
            g.dma_start(out=identb[:, :], in_=ident[:, :]).then_inc(sem["ldi"], 16)
            for k in range(8):
                if k >= 2:
                    g.wait_ge(sem["wq_rdy"], k - 1)
                g.dma_start(out=wr[k % 2][:, :],
                            in_=w_in[k * 128:(k + 1) * 128, :]).then_inc(
                                sem[f"ldw{k % 2}"], 16)

        @block.sync
        def _(s):
            s.dma_start(out=biasb[:, :], in_=bcast_row(b_in, 0, D)).then_inc(
                sem["ldb"], 16)
            for i in range(NT128):
                if i >= 2:
                    s.wait_ge(sem["ep"], i - 1)
                s.dma_start(out=xt[i % 2][:, :],
                            in_=x_in[i * 128:(i + 1) * 128, :]).then_inc(
                                sem[f"ldx{i % 2}"], 16)
                if i >= 1:
                    s.wait_ge(sem["ep"], i)
                    s.dma_start(out=x_out[(i - 1) * 128:i * 128, :],
                                in_=xo[(i - 1) % 2][:, :]).then_inc(
                                    sem[f"so{(i - 1) % 2}"], 16)
            s.wait_ge(sem["ep"], NT128)
            s.dma_start(out=x_out[(NT128 - 1) * 128:, :],
                        in_=xo[(NT128 - 1) % 2][:, :]).then_inc(
                            sem[f"so{(NT128 - 1) % 2}"], 16)
            s.wait_ge(sem["so0"], 16 * (NT128 // 2))
            s.wait_ge(sem["so1"], 16 * (NT128 // 2))

        @block.scalar
        def _(sc):
            sc.wait_ge(sem["cst"], 1)
            for i in range(NT128):
                sc.wait_ge(sem["mvr"], i + 1)
                emit_ln_stats_act(sc, st, j=i % 2)
                act_fence(sc).then_inc(sem["a2v"], 1)

        @block.vector
        def _(v):
            v.memset(st["eps"][:, :], 1e-5).then_inc(sem["cst"], 1)
            for k in range(8):
                v.wait_ge(sem[f"ldw{k % 2}"], 16 * (k // 2 + 1))
                emit_quant_dve(v, wr[k % 2], wq, t)
                dve_fence(v).then_inc(sem["wq_rdy"], 1)
                v.wait_ge(sem["wtr"], k + 1)
                v.tensor_copy(
                    bass.AP(wT, k * 128, [[8 * D, 128], [1024, 8], [1, 128]]),
                    ptr3)
                dve_fence(v).then_inc(sem["wev"], 1)
            for i in range(NT128):
                v.wait_ge(sem[f"ldx{i % 2}"], 16 * (i // 2 + 1))
                xc = xt[i % 2]
                if i >= 2:
                    v.wait_ge(sem["a2v"], i - 1)
                emit_ln_stats_dve(v, xc, st, j=i % 2)
                dve_fence(v).then_inc(sem["mvr"], 1)
                v.wait_ge(sem["a2v"], i + 1)
                if DEBUG_L0FFN and i >= 1:
                    v.wait_ge(sem["dbd"], 48 * i)
                v.tensor_scalar(xn[:, :], xc[:, :],
                                st["negm"][:, i % 2:i % 2 + 1],
                                st["r"][:, i % 2:i % 2 + 1],
                                ALU.add, ALU.mult)
                v.drain()
                emit_quant_dve(v, xn, xq, t)
                if DEBUG_L0FFN:
                    v.tensor_copy(dbq[:, :], xq[:, :])
                    v.tensor_copy(dbs[:, 0:1], st["negm"][:, i % 2:i % 2 + 1])
                    v.tensor_copy(dbs[:, 1:2], st["r"][:, i % 2:i % 2 + 1])
                    v.tensor_copy(dbs[:, 2:4], st["mv"][:, 2*(i%2):2*(i%2)+2])
                dve_fence(v).then_inc(sem["xq_rdy"], 1)
                if DEBUG_L0FFN:
                    v.tensor_copy(dbq[:, 0:1], dbq[:, 0:1]).then_inc(
                        sem["dbg"], 1)
                v.wait_ge(sem["xtr"], i + 1)
                v.tensor_copy(xfm[:, :], ptr[:, :])
                dve_fence(v).then_inc(sem["xev"], 1)
                if i == 0:
                    v.wait_ge(sem["ldb"], 16)
                if i >= 2:
                    v.wait_ge(sem[f"so{i % 2}"], 16 * (i // 2))
                for h in range(2):
                    v.wait_ge(sem["mm"], 2 * i + h + 1)
                    v.tensor_tensor(ty[:, :], py[h][:, :],
                                    biasb[:, h * 512:(h + 1) * 512], ALU.add)
                    v.drain()
                    v.scalar_tensor_tensor(
                        xo[i % 2][:, h * 512:(h + 1) * 512], ty[:, :], 0.5,
                        xc[:, h * 512:(h + 1) * 512], ALU.mult, ALU.add)
                    v.drain()
                dve_fence(v).then_inc(sem["ep"], 1)

        @block.tensor
        def _(te):
            te.wait_ge(sem["ldi"], 16)
            for k in range(8):
                te.wait_ge(sem["wq_rdy"], k + 1)
                if k >= 1:
                    te.wait_ge(sem["wev"], k)
                for c in range(8):
                    mm = te.transpose(ptr[:, c * 128:(c + 1) * 128],
                                      wq[:, c * 128:(c + 1) * 128],
                                      identb[:, :])
                mm.then_inc(sem["wtr"], 1)
            for i in range(NT128):
                te.wait_ge(sem["xq_rdy"], i + 1)
                te.wait_ge(sem["wev"], 8) if i == 0 else te.wait_ge(
                    sem["xev"], i)
                for c in range(8):
                    mm = te.transpose(ptr[:, c * 128:(c + 1) * 128],
                                      xq[:, c * 128:(c + 1) * 128],
                                      identb[:, :])
                mm.then_inc(sem["xtr"], 1)
                te.wait_ge(sem["xev"], i + 1)
                if i > 0:
                    te.wait_ge(sem["ep"], i)
                for h in range(2):
                    for c in range(8):
                        mm = te.matmul(py[h][:, :],
                                       xfm[:, c * 128:(c + 1) * 128],
                                       wT[:, c * 1024 + h * 512:
                                          c * 1024 + (h + 1) * 512],
                                       start=(c == 0), stop=(c == 7))
                    mm.then_inc(sem["mm"], 1)

        if DEBUG_L0FFN:
            @block.gpsimd
            def _(g):
                for i in range(NT128):
                    g.wait_ge(sem["dbg"], i + 1)
                    g.dma_start(out=dbg["xn"][i * 128:(i + 1) * 128, :],
                                in_=xn[:, :]).then_inc(sem["dbd"], 16)
                    g.dma_start(out=dbg["xq"][i * 128:(i + 1) * 128, :],
                                in_=dbq[:, :]).then_inc(sem["dbd"], 16)
                    g.dma_start(out=dbg["st"].ap()[i, :, :],
                                in_=dbs[:, :]).then_inc(sem["dbd"], 16)
                g.wait_ge(sem["wev"], 8)
                g.dma_start(out=dbg["wT"][:, :], in_=wT[:, :]).then_inc(
                    sem["dbd"], 16)
                g.wait_ge(sem["dbd"], 16 * (3 * NT128 + 1))

    es.close()
    pool.end()
    return [(sem["so0"], pool.base["so0"]), (sem["so1"], pool.base["so1"])]


def build_l0ffn():
    nc = bass.Bass("TRN2", target_bir_lowering=False)
    io = {
        "x_in": nc.dram_tensor("x_in", [L, D], F32, kind="ExternalInput"),
        "w": nc.dram_tensor("w", [D, D], F32, kind="ExternalInput"),
        "b": nc.dram_tensor("b", [D], F32, kind="ExternalInput"),
        "ident": nc.dram_tensor("ident", [128, 128], F32,
                                kind="ExternalInput"),
        "x_out": nc.dram_tensor("x_out", [L, D], F32, kind="ExternalOutput"),
    }
    pool = SemPool(nc)
    with nc.Block() as block:
        emit_l0ffn(nc, block, pool, "", [], io)
    return nc




from contextlib import ExitStack

TC = 256
NCH = L // TC
T5 = 512
NC5 = L // T5


def emit_mamba(nc, block, pool, pfx, gates, io, v1):
    es = ExitStack()
    sbt = lambda *a: es.enter_context(nc.sbuf_tensor(*a))
    block = BlockWrap(block, pool, gates)
    pool.begin()
    DI = DIN
    NDC = DI // 128
    x_in = io["x_in"]
    w_proj = io["w_proj"]
    conv_w = io["conv_w"]
    conv_b = io["conv_b"]
    wx_d, wx_B, wx_C = io["wx_d"], io["wx_B"], io["wx_C"]
    w_dt, b_dt = io["w_dt"], io["b_dt"]
    a_log, d_vec, w_out = io["a_log"], io["d_vec"], io["w_out"]
    ident = io["ident"]
    if v1:
        ln_w, ln_b = io["ln_w"], io["ln_b"]
        in_b, out_b = io["in_b"], io["out_b"]
    x_out = io["x_out"]

    upre_h = nc.dram_tensor(pfx + "upre_h", [DI, L + 4], F32 if v1 else BF16)
    res_h = nc.dram_tensor(pfx + "res_h", [DI, L], F32)
    u_h = nc.dram_tensor(pfx + "u_h", [DI, L], F32)
    del_h = nc.dram_tensor(pfx + "del_h", [DI, L], F32)
    bc_h = nc.dram_tensor(pfx + "bc_h", [2 * NST, L], F32)
    y_h = nc.dram_tensor(pfx + "y_h", [DI, L], F32)
    hid_h = nc.dram_tensor(pfx + "hid_h", [L, D], F32) if v1 else None

    def fmdc(dram, dc, col0, tlen, row0=0, width=L):
        return bass.AP(dram, (row0 + dc * 128) * width + col0,
                       [[width, 128], [1, tlen]])

    def fm_dma(eng, dram, col0, tlen, sbuf, seg, s_sem, ndc=NDC, row0=0,
               store=False, width=L):
        for dc in range(ndc):
            sb = sbuf[:, dc * seg:dc * seg + tlen]
            dr = fmdc(dram, dc, col0, tlen, row0, width)
            if store:
                eng.dma_start(out=dr, in_=sb).then_inc(s_sem, 16)
            else:
                eng.dma_start(out=sb, in_=dr).then_inc(s_sem, 16)

    identb = sbt(pfx + "identb", [128, 128], BF16)
    a_sb = sbt(pfx + "a_sb", [128, NDC * NST], F32)
    dcol = sbt(pfx + "dcol", [128, NDC], F32)
    bdtcol = sbt(pfx + "bdtcol", [128, NDC], F32)
    cbcol = sbt(pfx + "cbcol", [128, NDC], F32)
    wdt_sb = sbt(pfx + "wdt_sb", [RDT, DI], BF16)
    wxd_sb = sbt(pfx + "wxd_sb", [128, NDC * RDT], BF16)
    wxB_sb = sbt(pfx + "wxB_sb", [128, NDC * NST], BF16)
    wxC_sb = sbt(pfx + "wxC_sb", [128, NDC * NST], BF16)
    rf_sb = sbt(pfx + "rf_sb", [128, NDC], F32)
    rft = sbt(pfx + "rft", [128, 1], F32)
    carry = sbt(pfx + "carry", [128, NDC * NST], F32)
    st = {k: sbt(pfx + "s_" + k, [128, 1], F32)
          for k in ["lnv", "eps"]}
    st["negm"] = sbt(pfx + "s_negm", [128, 4], F32)
    st["r"] = sbt(pfx + "s_r", [128, 4], F32)
    st["mv"] = sbt(pfx + "s_mv", [128, 8], F32)
    st["stat12"] = sbt(pfx + "s_stat12", [128, 12], F32)
    zpad4 = sbt(pfx + "zpad4", [128, 4], F32 if v1 else BF16)
    if v1:
        inbcol = sbt(pfx + "inbcol", [128, 2 * NDC], F32)
        cw2col = sbt(pfx + "cw2col", [128, NDC * 4], F32)
    pLN = ExitStack()
    if v1:
        lnw_b = pLN.enter_context(nc.sbuf_tensor(pfx + "lnw_b", [128, D], F32))
        lnb_b = pLN.enter_context(nc.sbuf_tensor(pfx + "lnb_b", [128, D], F32))

    sem = {n: pool.get(n) for n in [
        "ldc", "ldw1",
        "ldx0", "ldx1", "unr", "mvr", "a2v", "sqr", "tr1", "rir", "mm1", "ubr",
        "spre", "sres", "shid",
        "cwld", "upld0", "upld1", "mmsc", "scf", "mmh", "su0", "su1",
        "uev2", "zp", "zpd",
        "ubld0", "ubld1", "hev", "rcr", "spa", "dtmm", "spl", "dcons",
        "dstgr", "sdel0", "sdel1", "sbc0", "sbc1", "spa3",
        "ld4", "ld4p0", "ld4p1", "p4a", "p4c", "p4y", "vdone",
        "sy0", "sy1", "sy2", "sy3",
        "ldw5", "ld5a", "ld5b", "s5a", "ygr", "ymm", "p5e", "ep5", "so5",
    ]}

    if True:
        @block.gpsimd
        def _(g):
            g.dma_start(out=identb[:, :], in_=ident[:, :]).then_inc(sem["ldc"], 16)
            g.dma_start(out=wdt_sb[:, :], in_=w_dt[:, :]).then_inc(sem["ldc"], 16)
            for dc in range(NDC):
                g.dma_start(out=wxd_sb[:, dc * RDT:(dc + 1) * RDT],
                            in_=wx_d[dc * 128:(dc + 1) * 128, :]).then_inc(
                                sem["ldc"], 16)
                g.dma_start(out=wxB_sb[:, dc * NST:(dc + 1) * NST],
                            in_=wx_B[dc * 128:(dc + 1) * 128, :]).then_inc(
                                sem["ldc"], 16)
                g.dma_start(out=wxC_sb[:, dc * NST:(dc + 1) * NST],
                            in_=wx_C[dc * 128:(dc + 1) * 128, :]).then_inc(
                                sem["ldc"], 16)
                g.dma_start(out=a_sb[:, dc * NST:(dc + 1) * NST],
                            in_=a_log[dc * 128:(dc + 1) * 128, :]).then_inc(
                                sem["ldc"], 16)
                g.dma_start(out=dcol[:, dc:dc + 1],
                            in_=bass.AP(d_vec, dc * 128, [[1, 128], [1, 1]])
                            ).then_inc(sem["ldc"], 16)
                g.dma_start(out=bdtcol[:, dc:dc + 1],
                            in_=bass.AP(b_dt, dc * 128, [[1, 128], [1, 1]])
                            ).then_inc(sem["ldc"], 16)
                g.dma_start(out=cbcol[:, dc:dc + 1],
                            in_=bass.AP(conv_b, dc * 128, [[1, 128], [1, 1]])
                            ).then_inc(sem["ldc"], 16)
                if v1:
                    g.dma_start(out=cw2col[:, dc * 4:(dc + 1) * 4],
                                in_=conv_w[dc * 128:(dc + 1) * 128, :]
                                ).then_inc(sem["ldc"], 16)
            if v1:
                g.dma_start(out=lnw_b[:, :], in_=bcast_row(ln_w, 0, D)
                            ).then_inc(sem["ldc"], 16)
                g.dma_start(out=lnb_b[:, :], in_=bcast_row(ln_b, 0, D)
                            ).then_inc(sem["ldc"], 16)
                for h in range(2 * NDC):
                    g.dma_start(out=inbcol[:, h:h + 1],
                                in_=bass.AP(in_b, h * 128, [[1, 128], [1, 1]])
                                ).then_inc(sem["ldc"], 16)

        NLDC = 16 * (2 + NDC * (8 if v1 else 7) + ((2 + 2 * NDC) if v1 else 0))

        @block.sync
        def _(s):
            s.wait_ge(sem["zp"], 1)
            for dc in range(NDC):
                s.dma_start(out=fmdc(upre_h, dc, 0, 4, width=L + 4),
                            in_=zpad4[:, :]).then_inc(sem["zpd"], 16)

        @block.vector
        def _(v):
            v.memset(st["eps"][:, :], 1e-5)
            v.memset(carry[:, :], 0.0)
            v.memset(zpad4[:, :], 0.0)
            v.drain().then_inc(sem["zp"], 1)
            v.wait_ge(sem["ldc"], NLDC)
            v.drain().then_inc(sem["unr"], 1)

        @block.scalar
        def _(sc):
            sc.wait_ge(sem["unr"], 1)
            sc.activation(a_sb[:, :], a_sb[:, :], AF.Exp, scale=1.0)
            sc.drain().then_inc(sem["a2v"], 1)

        @block.vector
        def _(v):
            v.wait_ge(sem["a2v"], 1)
            v.tensor_scalar(a_sb[:, :], a_sb[:, :], -1.0, None, ALU.mult)
            v.drain().then_inc(sem["unr"], 1)

        MVR0, A2V0, VP10 = 0, 1, 2
        APT = 2 if v1 else 1

        _sid = nc.enter_named_scope(pfx + "p1", False)[0]
        p1 = ExitStack()
        wproj_sb = [p1.enter_context(
            nc.sbuf_tensor(pfx + f"wp{dc}", [128, 2 * DI], BF16)) for dc in range(8)]
        xt = [p1.enter_context(nc.sbuf_tensor(pfx + f"xt{j}", [128, D], F32))
              for j in range(2)]
        hidt = p1.enter_context(nc.sbuf_tensor(pfx + "hidt", [128, D], F32))
        hnt = p1.enter_context(nc.sbuf_tensor(pfx + "hnt", [128, D], F32))
        xnb = p1.enter_context(nc.sbuf_tensor(pfx + "sqr", [128, D], BF16))
        xfmb = p1.enter_context(nc.sbuf_tensor(pfx + "xfmb", [128, 8 * T5], BF16))
        ustg = p1.enter_context(nc.sbuf_tensor(pfx + "ustg", [128, NDC * T5],
                                               F32 if v1 else BF16))
        rstg = p1.enter_context(nc.sbuf_tensor(pfx + "rstg", [128, NDC * T5], F32))
        ptr = p1.enter_context(nc.psum_tensor(pfx + "ptr", [128, 1024], BF16))
        pmm = [p1.enter_context(nc.psum_tensor(pfx + f"pmm{j}", [128, T5], F32))
               for j in range(2)]
        ptr3 = bass.AP(ptr, 0, [[1024, 128], [128, 8], [1, 128]])

        @block.gpsimd
        def _(g):
            for dc in range(8):
                g.dma_start(out=wproj_sb[dc][:, :],
                            in_=w_proj[dc * 128:(dc + 1) * 128, :]).then_inc(
                                sem["ldw1"], 16)
            for tt in range(NC5):
                if v1:
                    for q in range(4):
                        i = 4 * tt + q
                        g.wait_ge(sem["unr"], VP10 + i + 1)
                        g.dma_start(out=hid_h[i * 128:(i + 1) * 128, :],
                                    in_=hidt[:, :]).then_inc(sem["shid"], 16)
                g.wait_ge(sem["ubr"], tt + 1)
                fm_dma(g, upre_h, tt * T5 + 4, T5, ustg, T5, sem["spre"],
                       store=True, width=L + 4)
                fm_dma(g, res_h, tt * T5, T5, rstg, T5, sem["sres"],
                       store=True)

        @block.sync
        def _(s):
            for i in range(NT128):
                if i >= 2:
                    s.wait_ge(sem["unr"], VP10 + i - 1)
                s.dma_start(out=xt[i % 2][:, :],
                            in_=x_in[i * 128:(i + 1) * 128, :]).then_inc(
                                sem[f"ldx{i % 2}"], 16)

        @block.scalar
        def _(sc):
            for tt in range(NC5):
                for q in range(4):
                    i = 4 * tt + q
                    sc.wait_ge(sem["mvr"], MVR0 + APT * i + 1)
                    emit_ln_stats_act(sc, st, j=i % 2)
                    act_fence(sc).then_inc(sem["a2v"], 1)
                    if v1:
                        sc.wait_ge(sem["mvr"], MVR0 + 2 * i + 2)
                        emit_ln_stats_act(sc, st, j=2 + i % 2)
                        act_fence(sc).then_inc(sem["a2v"], 1)
                if tt >= 1:
                    sc.wait_ge(sem["spre"], 16 * NDC * tt)
                    sc.wait_ge(sem["sres"], 16 * NDC * tt)
                for oc in range(2 * NDC):
                    sc.wait_ge(sem["mm1"], 32 * tt + oc + 1)
                    dst = (ustg[:, oc * T5:(oc + 1) * T5] if oc < NDC else
                           rstg[:, (oc - NDC) * T5:(oc - NDC + 1) * T5])
                    if v1:
                        sc.activation(dst, pmm[oc % 2][:, :], AF.Identity,
                                      bias=inbcol[:, oc:oc + 1])
                    else:
                        sc.activation(dst, pmm[oc % 2][:, :], AF.Copy)
                    sc.drain().then_inc(sem["uev2"], 1)
                sc.drain().then_inc(sem["ubr"], 1)

        @block.vector
        def _(v):
            for i in range(NT128):
                tt, q = divmod(i, 4)
                v.wait_ge(sem[f"ldx{i % 2}"], 16 * (i // 2 + 1))
                xc = xt[i % 2]
                if i >= 2:
                    v.wait_ge(sem["a2v"], A2V0 + APT * (i - 1))
                emit_ln_stats_dve(v, xc, st, j=i % 2)
                dve_fence(v).then_inc(sem["mvr"], 1)
                v.wait_ge(sem["a2v"], A2V0 + APT * i + 1)
                if v1:
                    if i >= 1:
                        v.wait_ge(sem["shid"], 16 * i)
                    v.tensor_scalar(hidt[:, :], xc[:, :],
                                    st["negm"][:, i % 2:i % 2 + 1],
                                    st["r"][:, i % 2:i % 2 + 1],
                                    ALU.add, ALU.mult)
                    dve_fence(v).then_inc(sem["unr"], 1)
                    emit_ln_stats_dve(v, hidt, st, j=2 + i % 2)
                    dve_fence(v).then_inc(sem["mvr"], 1)
                    v.wait_ge(sem["a2v"], A2V0 + 2 * i + 2)
                    v.tensor_scalar(hnt[:, :], hidt[:, :],
                                    st["negm"][:, 2 + i % 2:3 + i % 2],
                                    st["r"][:, 2 + i % 2:3 + i % 2],
                                    ALU.add, ALU.mult)
                    v.drain()
                    v.tensor_tensor(hnt[:, :], hnt[:, :], lnw_b[:, :], ALU.mult)
                    v.drain()
                    v.tensor_tensor(xnb[:, :], hnt[:, :], lnb_b[:, :], ALU.add)
                else:
                    v.tensor_scalar(xnb[:, :], xc[:, :],
                                    st["negm"][:, i % 2:i % 2 + 1],
                                    st["r"][:, i % 2:i % 2 + 1],
                                    ALU.add, ALU.mult)
                    dve_fence(v).then_inc(sem["unr"], 1)
                dve_fence(v).then_inc(sem["sqr"], 1)
                v.wait_ge(sem["tr1"], i + 1)
                if i >= 4:
                    v.wait_ge(sem["mm1"], 32 * (i // 4))
                v.tensor_copy(
                    bass.AP(xfmb, q * 128,
                            [[8 * T5, 128], [T5, 8], [1, 128]]), ptr3)
                dve_fence(v).then_inc(sem["rir"], 1)

        @block.tensor
        def _(te):
            te.wait_ge(sem["ldc"], NLDC)
            te.wait_ge(sem["ldw1"], 16 * 8)
            for tt in range(NC5):
                for q in range(4):
                    i = 4 * tt + q
                    te.wait_ge(sem["sqr"], i + 1)
                    if i > 0:
                        te.wait_ge(sem["rir"], i)
                    for c in range(8):
                        mm = te.transpose(ptr[:, c * 128:(c + 1) * 128],
                                          xnb[:, c * 128:(c + 1) * 128],
                                          identb[:, :])
                    mm.then_inc(sem["tr1"], 1)
                te.wait_ge(sem["rir"], 4 * (tt + 1))
                for oc in range(2 * NDC):
                    gidx = 32 * tt + oc
                    if gidx >= 2:
                        te.wait_ge(sem["uev2"], gidx - 1)
                    for dc in range(8):
                        mm = te.matmul(pmm[oc % 2][:, :],
                                       wproj_sb[dc][:, oc * 128:(oc + 1) * 128],
                                       xfmb[:, dc * T5:(dc + 1) * T5],
                                       start=(dc == 0), stop=(dc == 7))
                    mm.then_inc(sem["mm1"], 1)

        p1.close()
        pLN.close()
        nc.leave_named_scope(pfx + "p1", _sid, False)

        _sid = nc.enter_named_scope(pfx + "p2", False)[0]
        p2 = ExitStack()
        P1S = 16 * NDC * NC5
        UPW = 16 * NDC * (T5 + 4)
        if v1:
            up2 = [p2.enter_context(nc.sbuf_tensor(
                pfx + f"up2_{j}", [128, NDC * (T5 + 4)], F32)) for j in range(2)]
            cacc = [p2.enter_context(nc.sbuf_tensor(
                pfx + f"cacc{j}", [128, T5], F32)) for j in range(2)]
            tapa = p2.enter_context(nc.sbuf_tensor(pfx + "tapa", [128, T5], F32))
            tapb = p2.enter_context(nc.sbuf_tensor(pfx + "tapb", [128, T5], F32))
            ustg2 = [p2.enter_context(nc.sbuf_tensor(
                pfx + f"ustg2_{j}", [128, NDC * T5], F32)) for j in range(2)]

            @block.sync
            def _(s):
                for cc in range(NC5):
                    if cc == 0:
                        s.wait_ge(sem["spre"], P1S)
                        s.wait_ge(sem["sres"], P1S)
                        s.wait_ge(sem["shid"], 16 * NT128)
                    if cc >= 2:
                        s.wait_ge(sem["mmh"], NDC * (cc - 1))
                    buf = up2[cc % 2]
                    if cc == 0:
                        s.wait_ge(sem["zpd"], 16 * NDC)
                    fm_dma(s, upre_h, cc * T5, T5 + 4, buf, T5 + 4,
                           sem[f"upld{cc % 2}"], width=L + 4)

            @block.vector
            def _(v):
                v.wait_ge(sem["spre"], P1S)
                v.wait_ge(sem["sres"], P1S)
                v.wait_ge(sem["shid"], 16 * NT128)
                v.drain().then_inc(sem["mmsc"], 1)
                for cc in range(NC5):
                    buf = up2[cc % 2]
                    v.wait_ge(sem[f"upld{cc % 2}"], 16 * NDC * (cc // 2 + 1))
                    for dc in range(NDC):
                        idx = NDC * cc + dc
                        if idx >= 2:
                            v.wait_ge(sem["scf"], idx - 1)
                        base = dc * (T5 + 4)
                        ca = cacc[dc % 2]
                        v.tensor_scalar(tapa[:, :],
                                        buf[:, base + 1:base + 1 + T5],
                                        cw2col[:, dc * 4:dc * 4 + 1], None,
                                        ALU.mult)
                        v.tensor_scalar(tapb[:, :],
                                        buf[:, base + 2:base + 2 + T5],
                                        cw2col[:, dc * 4 + 1:dc * 4 + 2], None,
                                        ALU.mult)
                        v.drain()
                        v.scalar_tensor_tensor(
                            tapa[:, :], buf[:, base + 3:base + 3 + T5],
                            cw2col[:, dc * 4 + 2:dc * 4 + 3], tapa[:, :],
                            ALU.mult, ALU.add)
                        v.scalar_tensor_tensor(
                            tapb[:, :], buf[:, base + 4:base + 4 + T5],
                            cw2col[:, dc * 4 + 3:dc * 4 + 4], tapb[:, :],
                            ALU.mult, ALU.add)
                        v.drain()
                        v.tensor_tensor(ca[:, :], tapa[:, :], tapb[:, :],
                                        ALU.add)
                        v.drain().then_inc(sem["mmh"], 1)

            @block.scalar
            def _(sc):
                for cc in range(NC5):
                    if cc >= 2:
                        sc.wait_ge(sem[f"su{cc % 2}"], 16 * NDC * (cc // 2))
                    for dc in range(NDC):
                        idx = NDC * cc + dc
                        sc.wait_ge(sem["mmh"], idx + 1)
                        sc.activation(ustg2[cc % 2][:, dc * T5:(dc + 1) * T5],
                                      cacc[dc % 2][:, :], AF.Silu,
                                      bias=cbcol[:, dc:dc + 1])
                        sc.drain().then_inc(sem["scf"], 1)
                    sc.drain()
                    fm_dma(sc, u_h, cc * T5, T5, ustg2[cc % 2], T5,
                           sem[f"su{cc % 2}"], store=True)
            SU_T = 16 * NDC * (NC5 // 2)
        else:
            cw_sb = [p2.enter_context(nc.sbuf_tensor(
                pfx + f"cw{t_}_{ic}", [128, 8 * 128], BF16))
                for t_ in range(4) for ic in range(NDC)]
            up2 = [p2.enter_context(nc.sbuf_tensor(
                pfx + f"up2_{j}", [128, NDC * (T5 + 4)], BF16)) for j in range(2)]
            ustg2 = [p2.enter_context(nc.sbuf_tensor(
                pfx + f"ustg2_{j}", [128, 8 * T5], F32)) for j in range(2)]
            pcv = [p2.enter_context(nc.psum_tensor(pfx + f"pcv{j}", [128, T5], F32))
                   for j in range(2)]

            @block.gpsimd
            def _(g):
                for h2 in range(2):
                    if h2 == 0:
                        g.wait_ge(sem["spre"], P1S)
                        g.wait_ge(sem["sres"], P1S)
                    else:
                        g.wait_ge(sem["mmsc"], 1 + 64)
                    for t_ in range(4):
                        for ic in range(NDC):
                            g.dma_start(
                                out=cw_sb[t_ * NDC + ic][:, :],
                                in_=conv_w.ap()[t_, ic * 128:(ic + 1) * 128,
                                                h2 * 1024:(h2 + 1) * 1024]
                                ).then_inc(sem["cwld"], 16)

            @block.sync
            def _(s):
                for h2 in range(2):
                    for cc in range(NC5):
                        idx = h2 * NC5 + cc
                        if idx == 0:
                            s.wait_ge(sem["spre"], P1S)
                            s.wait_ge(sem["sres"], P1S)
                        if idx >= 2:
                            s.wait_ge(sem["mmsc"], 1 + 8 * (idx - 1))
                        buf = up2[idx % 2]
                        if idx == 0:
                            s.wait_ge(sem["zpd"], 16 * NDC)
                        fm_dma(s, upre_h, cc * T5, T5 + 4, buf[:, :],
                               T5 + 4, sem[f"upld{idx % 2}"], width=L + 4)

            @block.vector
            def _(v):
                v.wait_ge(sem["spre"], P1S)
                v.wait_ge(sem["sres"], P1S)
                v.drain().then_inc(sem["mmsc"], 1)

            @block.tensor
            def _(te):
                te.wait_ge(sem["mmsc"], 1)
                for h2 in range(2):
                    te.wait_ge(sem["cwld"], 16 * 64 * (h2 + 1))
                    for cc in range(NC5):
                        idx = h2 * NC5 + cc
                        te.wait_ge(sem[f"upld{idx % 2}"],
                                   16 * NDC * (idx // 2 + 1))
                        buf = up2[idx % 2]
                        for oc8 in range(8):
                            gidx = 8 * idx + oc8
                            if gidx >= 2:
                                te.wait_ge(sem["scf"], gidx - 1)
                            first = True
                            for t_ in range(4):
                                for ic in range(NDC):
                                    mm = te.matmul(
                                        pcv[oc8 % 2][:, :],
                                        cw_sb[t_ * NDC + ic][:,
                                                             oc8 * 128:
                                                             (oc8 + 1) * 128],
                                        buf[:, ic * (T5 + 4) + 1 + t_:
                                            ic * (T5 + 4) + 1 + t_ + T5],
                                        start=first,
                                        stop=(t_ == 3 and ic == NDC - 1))
                                    first = False
                            mm.then_inc(sem["mmsc"], 1)

            @block.scalar
            def _(sc):
                for h2 in range(2):
                    for cc in range(NC5):
                        idx = h2 * NC5 + cc
                        if idx >= 2:
                            sc.wait_ge(sem[f"su{idx % 2}"], 16 * 8 * (idx // 2))
                        for oc8 in range(8):
                            gidx = 8 * idx + oc8
                            sc.wait_ge(sem["mmsc"], 1 + gidx + 1)
                            oc = h2 * 8 + oc8
                            sc.activation(
                                ustg2[idx % 2][:, oc8 * T5:(oc8 + 1) * T5],
                                pcv[oc8 % 2][:, :], AF.Silu,
                                bias=cbcol[:, oc:oc + 1])
                            sc.drain().then_inc(sem["scf"], 1)
                        sc.drain()
                        fm_dma(sc, u_h, cc * T5, T5, ustg2[idx % 2], T5,
                               sem[f"su{idx % 2}"], ndc=8, row0=h2 * 1024,
                               store=True)
            SU_T = 16 * 8 * NC5

        p2.close()
        nc.leave_named_scope(pfx + "p2", _sid, False)

        _sid = nc.enter_named_scope(pfx + "p3", False)[0]
        p3 = ExitStack()
        ub3 = [p3.enter_context(nc.sbuf_tensor(pfx + f"ub3_{j}", [128, NDC * T5],
                                               BF16)) for j in range(2)]
        draw = p3.enter_context(nc.sbuf_tensor(pfx + "draw", [RDT, T5], BF16))
        bcstB = [p3.enter_context(nc.sbuf_tensor(pfx + f"bcstB{j}", [NST, T5], F32))
                 for j in range(2)]
        bcstC = [p3.enter_context(nc.sbuf_tensor(pfx + f"bcstC{j}", [NST, T5], F32))
                 for j in range(2)]
        dstg = [p3.enter_context(nc.sbuf_tensor(pfx + f"dstg{j}", [128, NDC * T5],
                                                F32)) for j in range(2)]
        sp1 = [p3.enter_context(nc.sbuf_tensor(pfx + f"sp1_{j}", [128, T5], F32))
               for j in range(2)]
        sp2 = [p3.enter_context(nc.sbuf_tensor(pfx + f"sp2_{j}", [128, T5], F32))
               for j in range(2)]
        pxd = p3.enter_context(nc.psum_tensor(pfx + "pxd", [64, T5], F32))
        pxb = p3.enter_context(nc.psum_tensor(pfx + "pxb", [16, T5], F32))
        pxc = p3.enter_context(nc.psum_tensor(pfx + "pxc", [16, T5], F32))
        pdt = [p3.enter_context(nc.psum_tensor(pfx + f"pdt{j}", [128, T5], F32))
               for j in range(2)]

        @block.gpsimd
        def _(g):
            for cc in range(NC5):
                if cc == 0:
                    g.wait_ge(sem["su0"], SU_T)
                    g.wait_ge(sem["su1"], SU_T)
                if cc >= 2:
                    g.wait_ge(sem["hev"], cc - 1)
                fm_dma(g, u_h, cc * T5, T5, ub3[cc % 2], T5,
                       sem[f"ubld{cc % 2}"])

        @block.tensor
        def _(te):
            for cc in range(NC5):
                te.wait_ge(sem[f"ubld{cc % 2}"], 16 * NDC * (cc // 2 + 1))
                if cc >= 1:
                    te.wait_ge(sem["spa3"], cc)
                ub = ub3[cc % 2]
                for dc in range(NDC):
                    te.matmul(pxd[:, :], wxd_sb[:, dc * RDT:(dc + 1) * RDT],
                              ub[:, dc * T5:(dc + 1) * T5],
                              start=(dc == 0), stop=(dc == NDC - 1))
                for dc in range(NDC):
                    te.matmul(pxb[:, :], wxB_sb[:, dc * NST:(dc + 1) * NST],
                              ub[:, dc * T5:(dc + 1) * T5],
                              start=(dc == 0), stop=(dc == NDC - 1))
                for dc in range(NDC):
                    mm = te.matmul(pxc[:, :],
                                   wxC_sb[:, dc * NST:(dc + 1) * NST],
                                   ub[:, dc * T5:(dc + 1) * T5],
                                   start=(dc == 0), stop=(dc == NDC - 1))
                mm.then_inc(sem["hev"], 1)
                te.wait_ge(sem["rcr"], cc + 1)
                for dc in range(NDC):
                    idx = NDC * cc + dc
                    if idx >= 2:
                        te.wait_ge(sem["spl"], idx - 1)
                    te.matmul(pdt[dc % 2][:, :],
                              wdt_sb[:, dc * 128:(dc + 1) * 128],
                              draw[:, :], start=True,
                              stop=True).then_inc(sem["dtmm"], 1)

        @block.scalar
        def _(sc):
            for cc in range(NC5):
                sc.wait_ge(sem["hev"], cc + 1)
                if cc >= 1:
                    sc.wait_ge(sem["dtmm"], NDC * cc)
                sc.activation(draw[:, :], pxd[:, :], AF.Copy)
                sc.drain().then_inc(sem["rcr"], 1)
                if cc >= 2:
                    sc.wait_ge(sem[f"sbc{cc % 2}"], 32 * (cc // 2))
                sc.activation(bcstB[cc % 2][:, :], pxb[:, :], AF.Copy)
                sc.activation(bcstC[cc % 2][:, :], pxc[:, :], AF.Copy)
                sc.drain().then_inc(sem["spa"], 1)
                sc.drain().then_inc(sem["spa3"], 1)
                for dc in range(NDC):
                    idx = NDC * cc + dc
                    sc.wait_ge(sem["dtmm"], idx + 1)
                    if idx >= 2:
                        sc.wait_ge(sem["dcons"], idx - 1)
                    pp = pdt[dc % 2]
                    sc.activation(sp1[dc % 2][:, :], pp[:, :], AF.Abs,
                                  bias=bdtcol[:, dc:dc + 1])
                    sc.activation(sp2[dc % 2][:, :], pp[:, :], AF.Relu,
                                  bias=bdtcol[:, dc:dc + 1])
                    sc.drain()
                    sc.activation(sp1[dc % 2][:, :], sp1[dc % 2][:, :],
                                  AF.Exp, scale=-1.0)
                    sc.drain()
                    sc.activation(sp1[dc % 2][:, :], sp1[dc % 2][:, :],
                                  AF.Ln, bias=1.0)
                    sc.drain().then_inc(sem["spl"], 1)

        @block.vector
        def _(v):
            for cc in range(NC5):
                if cc >= 2:
                    v.wait_ge(sem[f"sdel{cc % 2}"], 16 * NDC * (cc // 2))
                for dc in range(NDC):
                    idx = NDC * cc + dc
                    v.wait_ge(sem["spl"], idx + 1)
                    v.tensor_tensor(dstg[cc % 2][:, dc * T5:(dc + 1) * T5],
                                    sp1[dc % 2][:, :], sp2[dc % 2][:, :],
                                    ALU.add).then_inc(sem["dcons"], 1)
                v.drain().then_inc(sem["dstgr"], 1)

        @block.sync
        def _(s):
            for cc in range(NC5):
                s.wait_ge(sem["dstgr"], cc + 1)
                fm_dma(s, del_h, cc * T5, T5, dstg[cc % 2], T5,
                       sem[f"sdel{cc % 2}"], store=True)
                s.wait_ge(sem["spa"], cc + 1)
                s.dma_start(out=bass.AP(bc_h, cc * T5, [[L, NST], [1, T5]]),
                            in_=bcstB[cc % 2][:, :]).then_inc(
                                sem[f"sbc{cc % 2}"], 16)
                s.dma_start(out=bass.AP(bc_h, NST * L + cc * T5,
                                        [[L, NST], [1, T5]]),
                            in_=bcstC[cc % 2][:, :]).then_inc(
                                sem[f"sbc{cc % 2}"], 16)

        p3.close()
        nc.leave_named_scope(pfx + "p3", _sid, False)

        _sid = nc.enter_named_scope(pfx + "p4", False)[0]
        p4 = ExitStack()
        dd4 = [p4.enter_context(nc.sbuf_tensor(pfx + f"dd4_{j}",
                                                [128, NDC * TC], F32))
               for j in range(2)]
        du4 = [p4.enter_context(nc.sbuf_tensor(pfx + f"du4_{j}",
                                                [128, NDC * TC], F32))
               for j in range(2)]
        dB4 = [p4.enter_context(nc.sbuf_tensor(pfx + f"dB4_{j}",
                                                [128, NST * TC], F32))
               for j in range(2)]
        dC4 = p4.enter_context(nc.sbuf_tensor(pfx + "dC4", [128, NST * TC], F32))
        dlu4 = p4.enter_context(nc.sbuf_tensor(pfx + "dlu4", [128, TC], F32))
        tr1 = p4.enter_context(nc.sbuf_tensor(pfx + "tr1", [128, 8 * TC], F32))
        tr2 = p4.enter_context(nc.sbuf_tensor(pfx + "tr2", [128, 4 * TC], F32))
        tr3 = p4.enter_context(nc.sbuf_tensor(pfx + "tr3", [128, 2 * TC], F32))
        s4 = p4.enter_context(nc.sbuf_tensor(pfx + "s4", [128, NST * TC], F32))
        ybd = [p4.enter_context(nc.sbuf_tensor(pfx + f"ybd{j}", [128, TC],
                                               F32)) for j in range(4)]
        abb = [p4.enter_context(nc.sbuf_tensor(pfx + f"abb{j}", [128, NST * TC],
                                               F32)) for j in range(2)]
        ball = [p4.enter_context(nc.sbuf_tensor(pfx + "ball0",
                                                [128, NST * TC], F32))]
        yred = p4.enter_context(nc.sbuf_tensor(pfx + "yred", [128, TC], F32))

        LD4C = 16 * (2 * NDC + NST)
        LDCC = 16 * NST

        @block.sync
        def _(s):
            for ch in range(NCH):
                if ch == 0:
                    s.wait_ge(sem["sdel0"], 16 * NDC * (NC5 // 2))
                    s.wait_ge(sem["sdel1"], 16 * NDC * (NC5 // 2))
                    s.wait_ge(sem["sbc0"], 32 * (NC5 // 2))
                    s.wait_ge(sem["sbc1"], 32 * (NC5 // 2))
                if ch >= 2:
                    s.wait_ge(sem["vdone"], ch - 1)
                fm_dma(s, del_h, ch * TC, TC, dd4[ch % 2], TC,
                       sem[f"ld4p{ch % 2}"])
                fm_dma(s, u_h, ch * TC, TC, du4[ch % 2], TC,
                       sem[f"ld4p{ch % 2}"])
                for n in range(NST):
                    s.dma_start(out=dB4[ch % 2][:, n * TC:(n + 1) * TC],
                                in_=bass.AP(bc_h, n * L + ch * TC,
                                            [[0, 128], [1, TC]])).then_inc(
                                    sem[f"ld4p{ch % 2}"], 16)
                if ch >= 1:
                    s.wait_ge(sem["vdone"], ch)
                for n in range(NST):
                    s.dma_start(out=dC4[:, n * TC:(n + 1) * TC],
                                in_=bass.AP(bc_h, (NST + n) * L + ch * TC,
                                            [[0, 128], [1, TC]])).then_inc(
                                    sem["ld4"], 16)

        @block.scalar
        def _(sc):
            for ch in range(NCH):
                sc.wait_ge(sem[f"ld4p{ch % 2}"], LD4C * (ch // 2 + 1))
                for dc in range(NDC):
                    g_ = ch * NDC + dc
                    if g_ >= 2:
                        sc.wait_ge(sem["p4c"], g_ - 1)
                    for n in range(NST):
                        sc.activation(abb[g_ % 2][:, n * TC:(n + 1) * TC],
                                      dd4[ch % 2][:, dc * TC:(dc + 1) * TC],
                                      AF.Exp,
                                      scale=a_sb[:, dc * NST + n:
                                                 dc * NST + n + 1])
                    sc.drain().then_inc(sem["p4a"], 1)

        HTC = (NST // 2) * TC
        @block.gpsimd
        def _(g):
            for ch in range(NCH):
                for dc in range(NDC):
                    g_ = ch * NDC + dc
                    g.wait_ge(sem["p4y"], g_ + 1)
                    fm_dma(g, y_h, ch * TC, TC, ybd[g_ % 4], TC,
                           sem[f"sy{g_ % 4}"], ndc=1, row0=dc * 128,
                           store=True)

        @block.vector
        def _(v):
            for ch in range(NCH):
                v.wait_ge(sem[f"ld4p{ch % 2}"], LD4C * (ch // 2 + 1))
                for dc in range(NDC):
                    g_ = ch * NDC + dc
                    if g_ >= 4:
                        v.wait_ge(sem[f"sy{g_ % 4}"], 16 * (g_ // 4))
                    v.tensor_tensor(dlu4[:, :],
                                    dd4[ch % 2][:, dc * TC:(dc + 1) * TC],
                                    du4[ch % 2][:, dc * TC:(dc + 1) * TC],
                                    ALU.mult)
                    v.drain()
                    v.tensor_tensor(
                        ball[0][:, :],
                        bass.AP(dlu4, 0, [[TC, 128], [0, NST], [1, TC]]),
                        dB4[ch % 2][:, :], ALU.mult)
                    v.drain()
                    v.wait_ge(sem["p4a"], g_ + 1)
                    for n in range(NST):
                        v.tensor_tensor_scan(
                            s4[:, n * TC:(n + 1) * TC],
                            abb[g_ % 2][:, n * TC:(n + 1) * TC],
                            ball[0][:, n * TC:(n + 1) * TC],
                            carry[:, dc * NST + n:dc * NST + n + 1],
                            ALU.mult, ALU.add)
                    v.drain()
                    v.tensor_copy(
                        carry[:, dc * NST:(dc + 1) * NST],
                        bass.AP(s4, TC - 1, [[NST * TC, 128], [TC, NST],
                                             [1, 1]]))
                    if dc == 0:
                        v.wait_ge(sem["ld4"], LDCC * (ch + 1))
                    v.tensor_tensor(abb[g_ % 2][:, :], s4[:, :],
                                    dC4[:, :], ALU.mult)
                    v.drain()
                    v.tensor_tensor(tr1[:, :], abb[g_ % 2][:, 0:HTC],
                                    abb[g_ % 2][:, HTC:], ALU.add)
                    dve_fence(v).then_inc(sem["p4c"], 1)
                    v.tensor_tensor(tr2[:, :], tr1[:, 0:4 * TC],
                                    tr1[:, 4 * TC:], ALU.add)
                    v.drain()
                    v.tensor_tensor(tr3[:, :], tr2[:, 0:2 * TC],
                                    tr2[:, 2 * TC:], ALU.add)
                    v.drain()
                    v.tensor_tensor(yred[:, :], tr3[:, 0:TC], tr3[:, TC:],
                                    ALU.add)
                    v.drain()
                    if v1:
                        v.scalar_tensor_tensor(
                            ybd[g_ % 4][:, :],
                            du4[ch % 2][:, dc * TC:(dc + 1) * TC],
                            dcol[:, dc:dc + 1], yred[:, :],
                            ALU.mult, ALU.add)
                    else:
                        v.tensor_copy(ybd[g_ % 4][:, :], yred[:, :])
                    dve_fence(v).then_inc(sem["p4y"], 1)
                v.drain().then_inc(sem["vdone"], 1)
            if not v1:
                for dc in range(NDC):
                    v.reduce_sum(rft[:, :], carry[:, dc * NST:(dc + 1) * NST],
                                 axis=AX)
                    v.drain()
                    v.tensor_scalar(rf_sb[:, dc:dc + 1], rft[:, :],
                                    dcol[:, dc:dc + 1], None, ALU.mult)
                    v.drain()

        p4.close()
        nc.leave_named_scope(pfx + "p4", _sid, False)

        _sid = nc.enter_named_scope(pfx + "p5", False)[0]
        p5 = ExitStack()
        wout_sb = p5.enter_context(nc.sbuf_tensor(pfx + "wout_sb", [128, NDC * D],
                                                  BF16))
        y5 = p5.enter_context(nc.sbuf_tensor(pfx + "y5", [128, NDC * T5], F32))
        g5 = p5.enter_context(nc.sbuf_tensor(pfx + "g5", [128, NDC * T5], F32))
        ygb = p5.enter_context(nc.sbuf_tensor(pfx + "ygb", [128, NDC * T5], BF16))
        tmp5 = p5.enter_context(nc.sbuf_tensor(pfx + "tmp5", [128, T5], F32))
        tmp5b = p5.enter_context(nc.sbuf_tensor(pfx + "tmp5b", [128, T5], F32))
        X5 = [p5.enter_context(nc.sbuf_tensor(pfx + f"X5_{j}", [128, D], F32))
              for j in range(4)]
        hid5 = [p5.enter_context(nc.sbuf_tensor(pfx + f"hid5_{j}", [128, D], F32))
                for j in range(4)] if v1 else None
        xo5 = [p5.enter_context(nc.sbuf_tensor(pfx + f"xo5_{j}", [128, D], F32))
               for j in range(4)]
        pm5 = [p5.enter_context(nc.psum_tensor(pfx + f"pm5_{j}", [128, T5], F32))
               for j in range(2)]
        outb_b = (p5.enter_context(nc.sbuf_tensor(pfx + "outb_b", [128, D],
                                                  F32)) if v1 else None)

        LD5A = 16 * 2 * NDC
        LD5B = 16 * (8 if v1 else 4)
        NLDW5 = 16 * (NDC + (1 if v1 else 0))

        @block.gpsimd
        def _(g):
            for p_ in range(4):
                g.wait_ge(sem[f"sy{p_}"], 16 * (NCH * NDC // 4))
            if v1:
                g.dma_start(out=outb_b[:, :], in_=bcast_row(out_b, 0, D)
                            ).then_inc(sem["ldw5"], 16)
            for dc in range(NDC):
                g.dma_start(out=wout_sb[:, dc * D:(dc + 1) * D],
                            in_=w_out[dc * 128:(dc + 1) * 128, :]).then_inc(
                                sem["ldw5"], 16)

        @block.sync
        def _(s):
            for c5 in range(NC5):
                if c5 == 0:
                    for p_ in range(4):
                        s.wait_ge(sem[f"sy{p_}"], 16 * (NCH * NDC // 4))
                for dc in range(NDC):
                    if c5 >= 1:
                        s.wait_ge(sem["ygr"], NDC * (c5 - 1) + dc + 1)
                    fm_dma(s, y_h, c5 * T5, T5, y5[:, dc * T5:(dc + 1) * T5],
                           T5, sem["ld5a"], ndc=1, row0=dc * 128)
                    fm_dma(s, res_h, c5 * T5, T5,
                           g5[:, dc * T5:(dc + 1) * T5], T5, sem["ld5a"],
                           ndc=1, row0=dc * 128)
                for q in range(4):
                    i = 4 * c5 + q
                    if c5 >= 1:
                        s.wait_ge(sem["ep5"], 4 * (c5 - 1) + q + 1)
                    s.dma_start(out=X5[q][:, :],
                                in_=x_in[i * 128:(i + 1) * 128, :]).then_inc(
                                    sem["ld5b"], 16)
                    if v1:
                        s.dma_start(out=hid5[q][:, :],
                                    in_=hid_h[i * 128:(i + 1) * 128, :]
                                    ).then_inc(sem["ld5b"], 16)
                if c5 >= 1:
                    for q in range(4):
                        s.wait_ge(sem["ep5"], 4 * (c5 - 1) + q + 1)
                        i = 4 * (c5 - 1) + q
                        s.dma_start(out=x_out[i * 128:(i + 1) * 128, :],
                                    in_=xo5[q][:, :]).then_inc(sem["so5"], 16)
            for q in range(4):
                s.wait_ge(sem["ep5"], 4 * (NC5 - 1) + q + 1)
                i = 4 * (NC5 - 1) + q
                s.dma_start(out=x_out[i * 128:(i + 1) * 128, :],
                            in_=xo5[q][:, :]).then_inc(sem["so5"], 16)
            s.wait_ge(sem["so5"], 16 * NT128)

        @block.scalar
        def _(sc):
            for c5 in range(NC5):
                sc.wait_ge(sem["ld5a"], LD5A * (c5 + 1))
                for dc in range(NDC):
                    sc.activation(g5[:, dc * T5:(dc + 1) * T5],
                                  g5[:, dc * T5:(dc + 1) * T5], AF.Silu)
                    sc.drain().then_inc(sem["s5a"], 1)

        @block.vector
        def _(v):
            for c5 in range(NC5):
                for dc in range(NDC):
                    v.wait_ge(sem["s5a"], NDC * c5 + dc + 1)
                    if v1:
                        v.tensor_tensor(ygb[:, dc * T5:(dc + 1) * T5],
                                        y5[:, dc * T5:(dc + 1) * T5],
                                        g5[:, dc * T5:(dc + 1) * T5], ALU.mult)
                    else:
                        tb = tmp5 if dc % 2 == 0 else tmp5b
                        v.tensor_scalar(tb[:, :],
                                        y5[:, dc * T5:(dc + 1) * T5],
                                        rf_sb[:, dc:dc + 1], None, ALU.add)
                        v.drain()
                        v.tensor_tensor(ygb[:, dc * T5:(dc + 1) * T5],
                                        tb[:, :],
                                        g5[:, dc * T5:(dc + 1) * T5], ALU.mult)
                    v.drain().then_inc(sem["ygr"], 1)
                dve_fence(v)
                v.wait_ge(sem["ld5b"], LD5B * (c5 + 1))
                if c5 >= 1:
                    v.wait_ge(sem["so5"], 16 * 4 * c5)
                for q in range(4):
                    for oh in range(2):
                        e = 8 * c5 + 2 * q + oh
                        v.wait_ge(sem["ymm"], e + 1)
                        pm = pm5[(2 * q + oh) % 2]
                        if v1:
                            v.tensor_tensor(tmp5[:, :], pm[:, :],
                                            X5[q][:, oh * 512:(oh + 1) * 512],
                                            ALU.add)
                            v.tensor_tensor(tmp5b[:, :],
                                            hid5[q][:, oh * 512:(oh + 1) * 512],
                                            outb_b[:, oh * 512:(oh + 1) * 512],
                                            ALU.add)
                            v.drain()
                            v.tensor_tensor(
                                xo5[q][:, oh * 512:(oh + 1) * 512], tmp5[:, :],
                                tmp5b[:, :], ALU.add)
                        else:
                            v.tensor_tensor(xo5[q][:, oh * 512:(oh + 1) * 512],
                                            pm[:, :],
                                            X5[q][:, oh * 512:(oh + 1) * 512],
                                            ALU.add)
                        v.drain().then_inc(sem["p5e"], 1)
                    dve_fence(v).then_inc(sem["ep5"], 1)

        @block.tensor
        def _(te):
            te.wait_ge(sem["ldw5"], NLDW5)
            for c5 in range(NC5):
                te.wait_ge(sem["ygr"], NDC * (c5 + 1))
                for q in range(4):
                    for oh in range(2):
                        e = 8 * c5 + 2 * q + oh
                        if e >= 2:
                            te.wait_ge(sem["p5e"], e - 1)
                        for dc in range(NDC):
                            mm = te.matmul(
                                pm5[(2 * q + oh) % 2][:, :],
                                ygb[:, dc * T5 + q * 128:dc * T5 + (q + 1) * 128],
                                wout_sb[:, dc * D + oh * 512:
                                        dc * D + (oh + 1) * 512],
                                start=(dc == 0), stop=(dc == NDC - 1))
                        mm.then_inc(sem["ymm"], 1)

        p5.close()
        nc.leave_named_scope(pfx + "p5", _sid, False)
    es.close()
    pool.end()
    return [(sem["so5"], pool.base["so5"])]


def mamba_io_tensors(nc, pfx, v1, with_xio=True):
    DI = DIN
    io = {
        "w_proj": nc.dram_tensor(pfx + "w_proj", [D, 2 * DI], F32,
                                 kind="ExternalInput"),
        "conv_w": (nc.dram_tensor(pfx + "conv_w", [DI, 4], F32,
                                  kind="ExternalInput") if v1 else
                   nc.dram_tensor(pfx + "conv_w", [4, DI, DI], F32,
                                  kind="ExternalInput")),
        "conv_b": nc.dram_tensor(pfx + "conv_b", [DI], F32,
                                 kind="ExternalInput"),
        "wx_d": nc.dram_tensor(pfx + "wx_d", [DI, RDT], F32,
                               kind="ExternalInput"),
        "wx_B": nc.dram_tensor(pfx + "wx_B", [DI, NST], F32,
                               kind="ExternalInput"),
        "wx_C": nc.dram_tensor(pfx + "wx_C", [DI, NST], F32,
                               kind="ExternalInput"),
        "w_dt": nc.dram_tensor(pfx + "w_dt", [RDT, DI], F32,
                               kind="ExternalInput"),
        "b_dt": nc.dram_tensor(pfx + "b_dt", [DI], F32, kind="ExternalInput"),
        "a_log": nc.dram_tensor(pfx + "a_log", [DI, NST], F32,
                                kind="ExternalInput"),
        "d_vec": nc.dram_tensor(pfx + "d_vec", [DI], F32,
                                kind="ExternalInput"),
        "w_out": nc.dram_tensor(pfx + "w_out", [DI, D], F32,
                                kind="ExternalInput"),
    }
    if v1:
        io["ln_w"] = nc.dram_tensor(pfx + "ln_w", [D], F32,
                                    kind="ExternalInput")
        io["ln_b"] = nc.dram_tensor(pfx + "ln_b", [D], F32,
                                    kind="ExternalInput")
        io["in_b"] = nc.dram_tensor(pfx + "in_b", [2 * DI], F32,
                                    kind="ExternalInput")
        io["out_b"] = nc.dram_tensor(pfx + "out_b", [D], F32,
                                     kind="ExternalInput")
    return io


def build_mamba(v1: bool):
    nc = bass.Bass("TRN2", target_bir_lowering=False)
    io = mamba_io_tensors(nc, "", v1)
    io["x_in"] = nc.dram_tensor("x_in", [L, D], F32, kind="ExternalInput")
    io["ident"] = nc.dram_tensor("ident", [128, 128], F32,
                                 kind="ExternalInput")
    io["x_out"] = nc.dram_tensor("x_out", [L, D], F32, kind="ExternalOutput")
    pool = SemPool(nc)
    with nc.Block() as block:
        emit_mamba(nc, block, pool, "", [], io, v1)
    return nc



def emit_l1ffn(nc, block, pool, pfx, gates, io):
    es = ExitStack()
    sbt = lambda n, sh, dt: es.enter_context(nc.sbuf_tensor(pfx + n, sh, dt))
    pst = lambda n, sh, dt: es.enter_context(nc.psum_tensor(pfx + n, sh, dt))
    block = BlockWrap(block, pool, gates)
    pool.begin()
    NB = 512
    DFF = 4 * D
    x_in, x_out = io["x_in"], io["x_out"]
    w1t, w2t = io["w1t"], io["w2t"]
    lng, lnbd = io["lng"], io["lnbd"]
    rwt, rbv = io["rwt"], io["rbv"]
    rawu, rawv, gamma = io["rawu"], io["rawv"], io["gamma"]
    ident = io["ident"]

    w1_sb = [sbt(f"w1_{dc}", [128, DFF], BF16)
             for dc in range(8)]
    w2_sb = sbt("w2_sb", [128, 32 * D], BF16)
    rwt_sb = sbt("rwt_sb", [128, 8 * NB], BF16)
    ut_sb = sbt("ut_sb", [128, 8 * NB], BF16)
    vu_sb = sbt("vu_sb", [128, 4 * D], BF16)
    identb = sbt("identb", [128, 128], BF16)
    lnw_b = sbt("lnw_b", [128, D], F32)
    lnb_b = sbt("lnb_b", [128, D], F32)
    rb_b = sbt("rb_b", [128, NB], F32)
    nrm = {k: sbt("n_" + k, [128, 1], F32)
           for k in ["s", "ln", "sq", "mx", "ri"]}
    st = {"negm": sbt("s_negm", [128, 4], F32),
          "r": sbt("s_r", [128, 4], F32),
          "mv": sbt("s_mv", [128, 8], F32),
          "stat12": sbt("s_stat12", [128, 12], F32),
          "lnv": sbt("s_lnv", [128, 1], F32),
          "eps": sbt("s_eps", [128, 1], F32)}
    tk = {k: sbt("t_" + k, [128, 1], F32)
          for k in ["ssum", "ssc", "e2", "den", "rd", "rS", "th", "gs", "sp"]}
    mx8 = sbt("mx8", [128, 8], F32)

    xt = [sbt(f"xt{j}", [128, D], F32) for j in range(2)]
    xoh = [sbt(f"xoh{j}", [128, 512], F32) for j in range(2)]
    xnf = sbt("xnf", [128, D], F32)
    hnt = sbt("hnt", [128, D], F32)
    xnb = sbt("xnb", [128, D], BF16)
    ln2b = sbt("ln2b", [128, D], BF16)
    xfmb = sbt("xfmb", [128, 8 * 256], BF16)
    lfmb = sbt("lfmb", [128, 8 * 256], BF16)
    s1c = sbt("s1c", [128, 512], BF16)
    hwfm = sbt("hwfm", [128, 1024], BF16)
    hsb = sbt("hsb", [128, NB], F32)
    hwb = sbt("hwb", [128, NB], BF16)
    rc = sbt("rc", [128, NB], F32)
    spl = sbt("spl", [128, NB], F32)
    wt1 = sbt("wt1", [128, NB], F32)

    ptr = pst("ptr", [128, 1024], BF16)
    psc = pst("psc", [128, NB], F32)
    phh = pst("phh", [128, NB], F32)
    ps1 = pst("ps1", [128, 512], F32)
    psS = [pst(f"psS{j}", [128, 512], F32) for j in range(4)]
    ptr3 = bass.AP(ptr, 0, [[1024, 128], [128, 8], [1, 128]])

    sem = {n: pool.get(n) for n in [
        "ldi", "ldc", "ldw", "ldu", "unr", "sqr", "rir", "ubr", "utr",
        "uev", "vev", "ldx0", "ldx1", "mvr", "a2v", "xnr", "tr1", "xev",
        "mmsc", "scf", "mmh", "hev", "rcr", "spa", "ssr", "e2r", "hwr",
        "mm1", "gel", "s2u", "mmS", "epq", "gso0", "gso1", "lde0",
        "lde1"]}
    NGR = NT128 // 2

    if True:
        @block.gpsimd
        def _(g):
            g.dma_start(out=identb[:, :], in_=ident[:, :]).then_inc(
                sem["ldi"], 16)
            g.dma_start(out=lnw_b[:, :], in_=bcast_row(lng, 0, D)).then_inc(
                sem["ldc"], 16)
            g.dma_start(out=lnb_b[:, :], in_=bcast_row(lnbd, 0, D)).then_inc(
                sem["ldc"], 16)
            g.dma_start(out=rb_b[:, :], in_=bcast_row(rbv, 0, NB)).then_inc(
                sem["ldc"], 16)
            g.dma_start(out=hnt[:, :], in_=bcast_row(gamma, 0, D)).then_inc(
                sem["ldc"], 16)
            for dc in range(8):
                g.dma_start(out=w1_sb[dc][:, :],
                            in_=w1t[dc * 128:(dc + 1) * 128, :]).then_inc(
                                sem["ldw"], 16)
            for hc in range(32):
                g.dma_start(out=w2_sb[:, hc * D:(hc + 1) * D],
                            in_=w2t[hc * 128:(hc + 1) * 128, :]).then_inc(
                                sem["ldw"], 16)
            for dc in range(8):
                g.dma_start(out=rwt_sb[:, dc * NB:(dc + 1) * NB],
                            in_=rwt[dc * 128:(dc + 1) * 128, :]).then_inc(
                                sem["ldw"], 16)
            for k in range(8):
                if 1 <= k <= 4:
                    g.wait_ge(sem["ubr"], k)
                elif k >= 5:
                    g.wait_ge(sem["vev"], k - 4)
                src = rawu if k < 4 else rawv
                uc = k % 4
                g.dma_start(out=xt[0][:, :],
                            in_=src[uc * 128:(uc + 1) * 128, :]).then_inc(
                                sem["ldu"], 16)

        @block.vector
        def _(v):
            v.memset(st["eps"][:, :], 1e-5)
            v.drain()
            for k in range(8):
                v.wait_ge(sem["ldu"], 16 * (k + 1))
                v.tensor_tensor(xnf[:, :], xt[0][:, :], xt[0][:, :], ALU.mult)
                v.drain()
                v.reduce_sum(nrm["s"][:, :], xnf[:, :], axis=AX)
                dve_fence(v).then_inc(sem["unr"], 1)
                v.wait_ge(sem["sqr"], k + 1)
                v.tensor_scalar(nrm["mx"][:, :], nrm["sq"][:, :], 1e-6, None,
                                ALU.max)
                v.drain()
                v.reciprocal(nrm["ri"][:, :], nrm["mx"][:, :])
                dve_fence(v).then_inc(sem["rir"], 1)
                if k < 4:
                    v.wait_ge(sem["utr"], k + 1)
                    v.tensor_copy(
                        bass.AP(ut_sb, k * 128,
                                [[8 * NB, 128], [NB, 8], [1, 128]]), ptr3)
                    dve_fence(v).then_inc(sem["uev"], 1)
                else:
                    v.wait_ge(sem["ldc"], 64)
                    v.tensor_scalar(xnf[:, :], xt[0][:, :], nrm["ri"][:, :1],
                                    None, ALU.mult)
                    v.drain()
                    v.tensor_tensor(vu_sb[:, (k - 4) * D:(k - 3) * D],
                                    xnf[:, :], hnt[:, :], ALU.mult)
                    dve_fence(v).then_inc(sem["vev"], 1)

        @block.scalar
        def _(sc):
            for k in range(8):
                sc.wait_ge(sem["unr"], k + 1)
                sc.activation(nrm["ln"][:, :], nrm["s"][:, :], AF.Ln)
                sc.drain()
                sc.activation(nrm["sq"][:, :], nrm["ln"][:, :], AF.Exp,
                              scale=0.5)
                act_fence(sc).then_inc(sem["sqr"], 1)
                if k < 4:
                    sc.wait_ge(sem["rir"], k + 1)
                    if k >= 1:
                        sc.wait_ge(sem["utr"], k)
                    sc.activation(xnb[:, :], xt[0][:, :], AF.Identity,
                                  scale=nrm["ri"][:, :1])
                    act_fence(sc).then_inc(sem["ubr"], 1)

        @block.tensor
        def _(te):
            te.wait_ge(sem["ldi"], 16)
            for uc in range(4):
                te.wait_ge(sem["ubr"], uc + 1)
                if uc >= 1:
                    te.wait_ge(sem["uev"], uc)
                for c in range(8):
                    mm = te.transpose(ptr[:, c * 128:(c + 1) * 128],
                                      xnb[:, c * 128:(c + 1) * 128],
                                      identb[:, :])
                mm.then_inc(sem["utr"], 1)

        @block.sync
        def _(s):
            for g_ in range(NGR):
                for q in range(2):
                    i = 2 * g_ + q
                    if i == 0:
                        s.wait_ge(sem["vev"], 4)
                    if i >= 2:
                        s.wait_ge(sem["xnr"], i - 1)
                    s.dma_start(out=xt[q][:, :],
                                in_=x_in[i * 128:(i + 1) * 128, :]).then_inc(
                                    sem[f"ldx{q}"], 16)
                for j in range(4):
                    e = 4 * g_ + j
                    q2, oh = divmod(j, 2)
                    ii = 2 * g_ + q2
                    if e >= 2:
                        s.wait_ge(sem[f"gso{e % 2}"], 16 * (e // 2))
                    s.dma_start(out=xoh[e % 2][:, :],
                                in_=x_in[ii * 128:(ii + 1) * 128,
                                         oh * 512:(oh + 1) * 512]).then_inc(
                                    sem[f"lde{e % 2}"], 16)
            s.wait_ge(sem["gso0"], 16 * 32)
            s.wait_ge(sem["gso1"], 16 * 32)

        @block.gpsimd
        def _(g):
            for e in range(4 * NGR):
                g_, j = divmod(e, 4)
                q2, oh = divmod(j, 2)
                ii = 2 * g_ + q2
                g.wait_ge(sem["epq"], e + 1)
                g.dma_start(out=x_out[ii * 128:(ii + 1) * 128,
                                      oh * 512:(oh + 1) * 512],
                            in_=xoh[e % 2][:, :]).then_inc(
                                sem[f"gso{e % 2}"], 16)

        @block.vector
        def _(v):
            v.wait_ge(sem["utr"], 4)
            for i in range(NT128):
                g_, q = divmod(i, 2)
                v.wait_ge(sem[f"ldx{q}"], 16 * (g_ + 1))
                if i >= 2:
                    v.wait_ge(sem["a2v"], 2 * i - 2)
                emit_ln_stats_dve(v, xt[q], st, j=q)
                dve_fence(v).then_inc(sem["mvr"], 1)
                v.wait_ge(sem["a2v"], 2 * i + 1)
                v.tensor_scalar(xnf[:, :], xt[q][:, :],
                                st["negm"][:, q:q + 1], st["r"][:, q:q + 1],
                                ALU.add, ALU.mult)
                v.drain()
                v.tensor_copy(xnb[:, :], xnf[:, :])
                emit_ln_stats_dve(v, xnf, st, j=2 + q)
                dve_fence(v).then_inc(sem["mvr"], 1)
                v.wait_ge(sem["a2v"], 2 * i + 2)
                v.tensor_scalar(hnt[:, :], xnf[:, :],
                                st["negm"][:, 2 + q:3 + q],
                                st["r"][:, 2 + q:3 + q], ALU.add, ALU.mult)
                v.drain()
                v.tensor_tensor(xnf[:, :], hnt[:, :], lnw_b[:, :], ALU.mult)
                v.drain()
                v.tensor_tensor(ln2b[:, :], xnf[:, :], lnb_b[:, :], ALU.add)
                dve_fence(v).then_inc(sem["xnr"], 1)
                v.wait_ge(sem["tr1"], 3 * i + 1)
                v.tensor_copy(
                    bass.AP(xfmb, q * 128, [[2048, 128], [256, 8], [1, 128]]),
                    ptr3)
                dve_fence(v).then_inc(sem["xev"], 1)
                v.wait_ge(sem["tr1"], 3 * i + 2)
                v.tensor_copy(
                    bass.AP(lfmb, q * 128, [[2048, 128], [256, 8], [1, 128]]),
                    ptr3)
                dve_fence(v).then_inc(sem["xev"], 1)
                v.wait_ge(sem["mmsc"], i + 1)
                v.tensor_tensor(spl[:, :], psc[:, :], rb_b[:, :], ALU.add)
                dve_fence(v).then_inc(sem["scf"], 1)
                v.tensor_scalar(rc[:, :], spl[:, :], 10.0, -10.0,
                                ALU.min, ALU.max)
                dve_fence(v).then_inc(sem["rcr"], 1)
                v.wait_ge(sem["spa"], i + 1)
                v.max(mx8[:, :], spl[:, :])
                v.drain()
                v.reduce_sum(tk["ssum"][:, :], mx8[:, :], axis=AX)
                v.drain()
                v.tensor_scalar(tk["ssc"][:, :], tk["ssum"][:, :], 20.0, None,
                                ALU.min)
                dve_fence(v).then_inc(sem["ssr"], 1)
                v.wait_ge(sem["e2r"], i + 1)
                v.tensor_scalar(tk["den"][:, :], tk["e2"][:, :], 1.0, None,
                                ALU.add)
                v.tensor_scalar(tk["sp"][:, :], tk["ssum"][:, :], 1e-6, None,
                                ALU.add)
                v.drain()
                v.reciprocal(tk["rd"][:, :], tk["den"][:, :])
                v.reciprocal(tk["rS"][:, :], tk["sp"][:, :])
                v.drain()
                v.tensor_scalar(tk["th"][:, :], tk["rd"][:, :], -2.0, 1.0,
                                ALU.mult, ALU.add)
                v.drain()
                v.tensor_tensor(tk["gs"][:, :], tk["th"][:, :],
                                tk["rS"][:, :], ALU.mult)
                v.drain()
                v.tensor_scalar(rc[:, :], spl[:, :], mx8[:, 7:8], None,
                                ALU.is_ge)
                v.drain()
                v.tensor_tensor(wt1[:, :], rc[:, :], spl[:, :], ALU.mult)
                v.drain()
                v.tensor_scalar(rc[:, :], wt1[:, :], tk["gs"][:, :1], None,
                                ALU.mult)
                v.drain()
                v.wait_ge(sem["hev"], i + 1)
                v.tensor_tensor(hwb[:, :], hsb[:, :], rc[:, :], ALU.mult)
                dve_fence(v).then_inc(sem["hwr"], 1)
                v.wait_ge(sem["tr1"], 3 * i + 3)
                v.tensor_copy(hwfm[:, q * 512:(q + 1) * 512], ptr[:, 0:512])
                dve_fence(v).then_inc(sem["xev"], 1)
                if q == 1:
                    for j in range(4):
                        q2, oh = divmod(j, 2)
                        e = 4 * g_ + j
                        v.wait_ge(sem["mmS"], e + 1)
                        v.wait_ge(sem[f"lde{e % 2}"], 16 * (e // 2 + 1))
                        v.scalar_tensor_tensor(
                            xoh[e % 2][:, :], psS[j][:, :], 0.5,
                            xoh[e % 2][:, :], ALU.mult, ALU.add)
                        dve_fence(v).then_inc(sem["epq"], 1)

        @block.scalar
        def _(sc):
            for i in range(NT128):
                g_, q = divmod(i, 2)
                sc.wait_ge(sem["mvr"], 2 * i + 1)
                emit_ln_stats_act(sc, st, j=q)
                act_fence(sc).then_inc(sem["a2v"], 1)
                sc.wait_ge(sem["mvr"], 2 * i + 2)
                emit_ln_stats_act(sc, st, j=2 + q)
                act_fence(sc).then_inc(sem["a2v"], 1)
                sc.wait_ge(sem["rcr"], i + 1)
                sc.activation(spl[:, :], rc[:, :], AF.Exp)
                sc.drain()
                sc.activation(spl[:, :], spl[:, :], AF.Ln, bias=1.0)
                act_fence(sc).then_inc(sem["spa"], 1)
                sc.wait_ge(sem["mmh"], i + 1)
                if i >= 1:
                    sc.wait_ge(sem["hwr"], i)
                sc.activation(hsb[:, :], phh[:, :], AF.Copy)
                act_fence(sc).then_inc(sem["hev"], 1)
                sc.wait_ge(sem["ssr"], i + 1)
                sc.activation(tk["e2"][:, :], tk["ssc"][:, :], AF.Exp,
                              scale=2.0)
                act_fence(sc).then_inc(sem["e2r"], 1)
                if q == 1:
                    for p in range(16):
                        sc.wait_ge(sem["mm1"], 32 * g_ + 2 * p + 2)
                        if 16 * g_ + p >= 1:
                            sc.wait_ge(sem["s2u"], 32 * g_ + 2 * p)
                        sc.activation(s1c[:, :], ps1[:, :], AF.Gelu)
                        act_fence(sc).then_inc(sem["gel"], 1)

        @block.tensor
        def _(te):
            te.wait_ge(sem["ldw"], 16 * 48)
            te.wait_ge(sem["uev"], 4)
            te.wait_ge(sem["vev"], 4)
            for i in range(NT128):
                g_, q = divmod(i, 2)
                te.wait_ge(sem["xnr"], i + 1)
                if i > 0:
                    te.wait_ge(sem["xev"], 3 * i)
                for c in range(8):
                    mm = te.transpose(ptr[:, c * 128:(c + 1) * 128],
                                      xnb[:, c * 128:(c + 1) * 128],
                                      identb[:, :])
                mm.then_inc(sem["tr1"], 1)
                te.wait_ge(sem["xev"], 3 * i + 1)
                for c in range(8):
                    mm = te.transpose(ptr[:, c * 128:(c + 1) * 128],
                                      ln2b[:, c * 128:(c + 1) * 128],
                                      identb[:, :])
                mm.then_inc(sem["tr1"], 1)
                te.wait_ge(sem["xev"], 3 * i + 2)
                if i >= 1:
                    te.wait_ge(sem["scf"], i)
                for dc in range(8):
                    mm = te.matmul(psc[:, :],
                                   lfmb[:, dc * 256 + q * 128:
                                        dc * 256 + (q + 1) * 128],
                                   rwt_sb[:, dc * NB:(dc + 1) * NB],
                                   start=(dc == 0), stop=(dc == 7))
                mm.then_inc(sem["mmsc"], 1)
                te.wait_ge(sem["hev"], i)
                for dc in range(8):
                    mm = te.matmul(phh[:, :],
                                   xfmb[:, dc * 256 + q * 128:
                                        dc * 256 + (q + 1) * 128],
                                   ut_sb[:, dc * NB:(dc + 1) * NB],
                                   start=(dc == 0), stop=(dc == 7))
                mm.then_inc(sem["mmh"], 1)
                te.wait_ge(sem["hwr"], i + 1)
                for c in range(4):
                    mm = te.transpose(ptr[:, c * 128:(c + 1) * 128],
                                      hwb[:, c * 128:(c + 1) * 128],
                                      identb[:, :])
                mm.then_inc(sem["tr1"], 1)
                if q == 1:
                    te.wait_ge(sem["xev"], 3 * i + 3)
                    if g_ >= 1:
                        te.wait_ge(sem["epq"], 4 * g_)
                    for oc in range(33):
                        if oc < 32:
                            if oc >= 2:
                                te.wait_ge(sem["gel"], 16 * g_ + oc // 2)
                            for dc in range(8):
                                mm = te.matmul(
                                    ps1[:, (oc % 2) * 256:(oc % 2 + 1) * 256],
                                    w1_sb[dc][:, oc * 128:(oc + 1) * 128],
                                    xfmb[:, dc * 256:(dc + 1) * 256],
                                    start=(dc == 0), stop=(dc == 7))
                            mm.then_inc(sem["mm1"], 1)
                        if oc >= 1:
                            o2 = oc - 1
                            te.wait_ge(sem["gel"], 16 * g_ + o2 // 2 + 1)
                            for j in range(4):
                                q2, oh = divmod(j, 2)
                                mm = te.matmul(
                                    psS[j][:, :],
                                    s1c[:, (o2 % 2) * 256 + q2 * 128:
                                        (o2 % 2) * 256 + (q2 + 1) * 128],
                                    w2_sb[:, o2 * D + oh * 512:
                                          o2 * D + (oh + 1) * 512],
                                    start=(o2 == 0), stop=False)
                            mm.then_inc(sem["s2u"], 1)
                    for j in range(4):
                        q2, oh = divmod(j, 2)
                        for vc in range(4):
                            mm = te.matmul(
                                psS[j][:, :],
                                hwfm[:, q2 * 512 + vc * 128:
                                     q2 * 512 + (vc + 1) * 128],
                                vu_sb[:, vc * D + oh * 512:
                                      vc * D + (oh + 1) * 512],
                                start=False, stop=(vc == 3))
                        mm.then_inc(sem["mmS"], 1)

    es.close()
    pool.end()
    return [(sem["gso0"], pool.base["gso0"]),
            (sem["gso1"], pool.base["gso1"])]


def l1ffn_io_tensors(nc, pfx):
    NB = 512
    DFF = 4 * D
    return {
        "w1t": nc.dram_tensor(pfx + "w1t", [D, DFF], F32,
                              kind="ExternalInput"),
        "w2t": nc.dram_tensor(pfx + "w2t", [DFF, D], F32,
                              kind="ExternalInput"),
        "lng": nc.dram_tensor(pfx + "lng", [D], F32, kind="ExternalInput"),
        "lnbd": nc.dram_tensor(pfx + "lnbd", [D], F32, kind="ExternalInput"),
        "rwt": nc.dram_tensor(pfx + "rwt", [D, NB], F32,
                              kind="ExternalInput"),
        "rbv": nc.dram_tensor(pfx + "rbv", [NB], F32, kind="ExternalInput"),
        "rawu": nc.dram_tensor(pfx + "rawu", [NB, D], F32,
                               kind="ExternalInput"),
        "rawv": nc.dram_tensor(pfx + "rawv", [NB, D], F32,
                               kind="ExternalInput"),
        "gamma": nc.dram_tensor(pfx + "gamma", [D], F32,
                                kind="ExternalInput"),
    }


def build_l1ffn():
    nc = bass.Bass("TRN2", target_bir_lowering=False)
    io = l1ffn_io_tensors(nc, "")
    io["x_in"] = nc.dram_tensor("x_in", [L, D], F32, kind="ExternalInput")
    io["ident"] = nc.dram_tensor("ident", [128, 128], F32,
                                 kind="ExternalInput")
    io["x_out"] = nc.dram_tensor("x_out", [L, D], F32, kind="ExternalOutput")
    pool = SemPool(nc)
    with nc.Block() as block:
        emit_l1ffn(nc, block, pool, "", [], io)
    return nc



def build_full():
    nc = bass.Bass("TRN2", target_bir_lowering=False)
    x_in = nc.dram_tensor("x_in", [L, D], F32, kind="ExternalInput")
    ident = nc.dram_tensor("ident", [128, 128], F32, kind="ExternalInput")
    x_out = nc.dram_tensor("x_out", [L, D], F32, kind="ExternalOutput")
    w0 = nc.dram_tensor("f0_w", [D, D], F32, kind="ExternalInput")
    b0 = nc.dram_tensor("f0_b", [D], F32, kind="ExternalInput")
    m0 = mamba_io_tensors(nc, "m0_", False)
    f1 = l1ffn_io_tensors(nc, "f1_")
    m1 = mamba_io_tensors(nc, "m1_", True)
    xh = [nc.dram_tensor(f"xh{j}", [L, D], F32) for j in range(5)]
    pool = SemPool(nc)
    with nc.Block() as block:
        with nc.named_scope("A_l0ffn"):
            g = emit_l0ffn(nc, block, pool, "A", [],
                           {"x_in": x_in, "x_out": xh[0], "w": w0, "b": b0,
                            "ident": ident})
        with nc.named_scope("B_mamba0"):
            g = emit_mamba(nc, block, pool, "B", g,
                           {**m0, "x_in": xh[0], "x_out": xh[1],
                            "ident": ident}, v1=False)
        with nc.named_scope("C_l0ffn"):
            g = emit_l0ffn(nc, block, pool, "C", g,
                           {"x_in": xh[1], "x_out": xh[2], "w": w0, "b": b0,
                            "ident": ident})
        with nc.named_scope("D_l1ffn"):
            g = emit_l1ffn(nc, block, pool, "D", g,
                           {**f1, "x_in": xh[2], "x_out": xh[3],
                            "ident": ident})
        with nc.named_scope("E_mamba1"):
            g = emit_mamba(nc, block, pool, "E", g,
                           {**m1, "x_in": xh[3], "x_out": xh[4],
                            "ident": ident}, v1=True)
        with nc.named_scope("F_l1ffn"):
            g = emit_l1ffn(nc, block, pool, "F", g,
                           {**f1, "x_in": xh[4], "x_out": x_out,
                            "ident": ident})
    return nc


def full_in_map(x, inp):
    m = {"x_in": np.ascontiguousarray(x), "ident": IDENT_NP,
         "f0_w": np.ascontiguousarray(inp["L0_ffn_weight"]),
         "f0_b": np.ascontiguousarray(inp["L0_ffn_bias"])}
    for pfx, sub in (("m0_", mamba_in_map_l0(x, inp)),
                     ("m1_", mamba_in_map_l1(x, inp)),
                     ("f1_", l1ffn_in_map(x, inp))):
        for k, v in sub.items():
            if k not in ("x_in", "ident"):
                m[pfx + k] = v
    return m



_NC_CACHE = {}


def _get_nc(name, builder):
    if name not in _NC_CACHE:
        _NC_CACHE[name] = builder()
    return _NC_CACHE[name]


def run_l0ffn(x_cores, w, b):
    nc = _get_nc("l0ffn", build_l0ffn)
    in_maps = [{"x_in": np.ascontiguousarray(x), "w": w, "b": b,
                "ident": IDENT_NP} for x in x_cores]
    res = run_bass_kernel_spmd(nc, in_maps, core_ids=list(range(len(x_cores))))
    return [np.asarray(r["x_out"]) for r in res.results]


def mamba_in_map_l0(x, inp):
    return {
        "x_in": np.ascontiguousarray(x),
        "w_proj": inp["L0_attn_W_in"],
        "conv_w": inp["L0_attn_conv_w"],
        "conv_b": inp["L0_attn_conv_b"],
        "wx_d": np.ascontiguousarray(inp["L0_attn_W_x"][:, :RDT]),
        "wx_B": np.ascontiguousarray(inp["L0_attn_W_x"][:, RDT:RDT + NST]),
        "wx_C": np.ascontiguousarray(inp["L0_attn_W_x"][:, RDT + NST:]),
        "w_dt": inp["L0_attn_W_dt"],
        "b_dt": inp["L0_attn_b_dt"],
        "a_log": inp["L0_attn_A_log"],
        "d_vec": inp["L0_attn_D"],
        "w_out": inp["L0_attn_W_out"],
        "ident": IDENT_NP,
    }


def mamba_in_map_l1(x, inp):
    xp = np.ascontiguousarray(inp["L1_attn_xp_w"].T)
    return {
        "x_in": np.ascontiguousarray(x),
        "w_proj": np.ascontiguousarray(inp["L1_attn_in_w"].T),
        "conv_w": np.ascontiguousarray(inp["L1_attn_conv_w"].reshape(DIN, 4)),
        "conv_b": inp["L1_attn_conv_b"],
        "wx_d": np.ascontiguousarray(xp[:, :RDT]),
        "wx_B": np.ascontiguousarray(xp[:, RDT:RDT + NST]),
        "wx_C": np.ascontiguousarray(xp[:, RDT + NST:]),
        "w_dt": np.ascontiguousarray(inp["L1_attn_dtp_w"].T),
        "b_dt": inp["L1_attn_dtp_b"],
        "a_log": inp["L1_attn_A_log"],
        "d_vec": inp["L1_attn_D"],
        "w_out": np.ascontiguousarray(inp["L1_attn_out_w"].T),
        "ln_w": inp["L1_attn_ln_w"],
        "ln_b": inp["L1_attn_ln_b"],
        "in_b": inp["L1_attn_in_b"],
        "out_b": inp["L1_attn_out_b"],
        "ident": IDENT_NP,
    }


def l1ffn_in_map(x, inp):
    return {
        "x_in": np.ascontiguousarray(x),
        "w1t": np.ascontiguousarray(inp["L1_ffn_W1"].T),
        "w2t": np.ascontiguousarray(inp["L1_ffn_W2"].T),
        "lng": np.ascontiguousarray(inp["L1_ffn_ln_g"]).reshape(-1),
        "lnbd": np.ascontiguousarray(inp["L1_ffn_ln_b"]).reshape(-1),
        "rwt": np.ascontiguousarray(inp["L1_ffn_router_W"].T),
        "rbv": np.ascontiguousarray(inp["L1_ffn_router_b"]).reshape(-1),
        "rawu": np.ascontiguousarray(inp["L1_ffn_raw_U"]),
        "rawv": np.ascontiguousarray(inp["L1_ffn_raw_V"]),
        "gamma": np.ascontiguousarray(inp["L1_ffn_gamma"]).reshape(-1),
        "ident": IDENT_NP,
    }


def run_l1ffn(x_cores, inp):
    nc = _get_nc("l1ffn", build_l1ffn)
    in_maps = [l1ffn_in_map(x, inp) for x in x_cores]
    res = run_bass_kernel_spmd(nc, in_maps, core_ids=list(range(len(x_cores))))
    return [np.asarray(r["x_out"]) for r in res.results]


def run_mamba(x_cores, inp, v1):
    nc = _get_nc("mamba1" if v1 else "mamba0",
                 lambda: build_mamba(v1))
    mk = mamba_in_map_l1 if v1 else mamba_in_map_l0
    in_maps = [mk(x, inp) for x in x_cores]
    res = run_bass_kernel_spmd(nc, in_maps, core_ids=list(range(len(x_cores))))
    return [np.asarray(r["x_out"]) for r in res.results]


def kernel(**inputs):
    x = np.asarray(inputs["L0_attn_x"], dtype=np.float32)
    nc = _get_nc("full", build_full)
    in_maps = [full_in_map(np.ascontiguousarray(x[c]), inputs)
               for c in range(NCORE)]
    res = run_bass_kernel_spmd(nc, in_maps, core_ids=list(range(NCORE)))
    return np.stack([np.asarray(r["x_out"]) for r in res.results], axis=0)


def kernel_unfused(**inputs):
    x = np.asarray(inputs["L0_attn_x"], dtype=np.float32)
    x_cores = [np.ascontiguousarray(x[c]) for c in range(NCORE)]
    w, b = inputs["L0_ffn_weight"], inputs["L0_ffn_bias"]
    x_cores = run_l0ffn(x_cores, w, b)
    x_cores = run_mamba(x_cores, inputs, v1=False)
    x_cores = run_l0ffn(x_cores, w, b)
    x_cores = run_l1ffn(x_cores, inputs)
    x_cores = run_mamba(x_cores, inputs, v1=True)
    x_cores = run_l1ffn(x_cores, inputs)
    return np.stack(x_cores, axis=0)


# version 18
# speedup vs baseline: 1.1829x; 1.1829x over previous
import numpy as np
import concourse.bass as bass
import concourse.mybir as mybir
from concourse.bass_utils import run_bass_kernel_spmd

F32 = mybir.dt.float32
BF16 = mybir.dt.bfloat16
AF = mybir.ActivationFunctionType
ALU = mybir.AluOpType
AX = mybir.AxisListType.X

L = 4096
D = 1024
DIN = 2048
NST = 16
RDT = 64
MAGIC = 12582912.0
NT128 = L // 128
NCORE = 8

IDENT_NP = np.eye(128, dtype=np.float32)


def bcast_row(dram, row, n, part=128):
    return bass.AP(dram, row * n, [[0, part], [1, n]])



def emit_quant_dve(vector, src, dst_bf16, t, nblk=32, blk=32):
    v3 = lambda h: h[:, :].rearrange("p (a c) -> p a c", c=blk)
    sc_b = lambda h: bass.AP(h, 0, [[nblk, 128], [1, nblk], [0, blk]])
    V = vector
    V.reduce_max(t["sc0"][:, :], v3(src), axis=AX, apply_absolute_value=True)
    V.drain()
    V.tensor_scalar(t["sc"][:, :], t["sc0"][:, :], 1.0 / 7.5, 1e-30,
                    ALU.mult, ALU.max)
    V.drain()
    V.reciprocal(t["si"][:, :], t["sc"][:, :])
    V.drain()
    V.tensor_tensor(v3(t["v"]), v3(src), sc_b(t["si"]), ALU.mult)
    V.drain()
    V.tensor_tensor(t["q4"][:, :], t["v"][:, :], t["v"][:, :], ALU.mult)
    V.drain()
    V.tensor_scalar(t["g2"][:, :], t["q4"][:, :], 4.0, None, ALU.is_ge)
    V.tensor_scalar(t["g4"][:, :], t["q4"][:, :], 16.0, None, ALU.is_ge)
    V.drain()
    V.tensor_scalar(t["w1"][:, :], t["g2"][:, :], -4.0, 8.0, ALU.mult, ALU.add)
    V.tensor_scalar(t["sn"][:, :], t["g2"][:, :], -0.125, -0.125,
                    ALU.mult, ALU.add)
    V.drain()
    V.scalar_tensor_tensor(t["w1"][:, :], t["g4"][:, :], -2.0, t["w1"][:, :],
                           ALU.mult, ALU.add)
    V.scalar_tensor_tensor(t["sn"][:, :], t["g4"][:, :], -0.25, t["sn"][:, :],
                           ALU.mult, ALU.add)
    V.drain()
    V.tensor_tensor(t["tt"][:, :], t["v"][:, :], t["w1"][:, :], ALU.mult)
    V.drain()
    V.tensor_scalar(t["z"][:, :], t["tt"][:, :], -1.0, 0.5, ALU.mult, ALU.add)
    V.drain()
    V.tensor_scalar(t["rr"][:, :], t["z"][:, :], MAGIC, MAGIC,
                    ALU.add, ALU.subtract)
    V.drain()
    V.tensor_tensor(t["gt"][:, :], t["rr"][:, :], t["z"][:, :], ALU.is_gt)
    V.drain()
    V.tensor_tensor(t["ff"][:, :], t["rr"][:, :], t["gt"][:, :], ALU.subtract)
    V.drain()
    V.tensor_tensor(t["tt"][:, :], t["ff"][:, :], t["sn"][:, :], ALU.mult)
    V.drain()
    return V.tensor_tensor(v3(dst_bf16), v3(t["tt"]), sc_b(t["sc"]), ALU.mult)


def emit_ln_stats_dve(v, xt, st, j=0):
    F = xt.shape[1]
    h = F // 2
    v.bn_stats(st["stat12"][:, 0:6], xt[:, 0:h])
    v.bn_stats(st["stat12"][:, 6:12], xt[:, h:F])
    v.drain()
    return v.bn_aggr(st["mv"][:, 2 * j:2 * j + 2], st["stat12"][:, :])


def emit_ln_stats_act(sc, st, j=0):
    sc.activation(st["negm"][:, j:j + 1], st["mv"][:, 2 * j:2 * j + 1],
                  AF.Copy, scale=-1.0)
    sc.activation(st["lnv"][:, :], st["mv"][:, 2 * j + 1:2 * j + 2], AF.Ln,
                  bias=st["eps"][:, :1])
    sc.drain()
    return sc.activation(st["r"][:, j:j + 1], st["lnv"][:, :], AF.Exp,
                         scale=-0.5)


def alloc_quant_scratch(nc, pfx="", es=None):
    if es is None:
        mk = lambda n, sh: nc.alloc_sbuf_tensor(n, sh, F32)
    else:
        mk = lambda n, sh: es.enter_context(nc.sbuf_tensor(n, sh, F32))
    t = {k: mk(pfx + "q_" + k, [128, D])
         for k in ["v", "q4", "g2", "g4", "w1", "sn", "tt",
                   "z", "rr", "gt", "ff"]}
    for k in ["sc0", "sc", "si"]:
        t[k] = mk(pfx + "q_" + k, [128, 32])
    return t


def dve_fence(v, dummy=None):
    return v.drain()


def act_fence(sc, dummy=None):
    return sc.drain()



class SemPool:
    def __init__(self, nc):
        self.nc = nc
        self.sems = {}
        self.name_of = {}
        self.base = {}
        self.cur = {}

    def get(self, name):
        if name not in self.sems:
            s = self.nc.alloc_semaphore(name=name)
            self.sems[name] = s
            self.name_of[id(s)] = name
            self.base[name] = 0
        return self.sems[name]

    def begin(self):
        self.cur = {}

    def end(self):
        for n, v in self.cur.items():
            self.base[n] += v
        self.cur = {}

    def note_inc(self, sem, n):
        nm = self.name_of[id(sem)]
        self.cur[nm] = self.cur.get(nm, 0) + n

    def base_of(self, sem):
        return self.base[self.name_of[id(sem)]]

    def abs_total(self, name):
        return self.base.get(name, 0) + self.cur.get(name, 0)


class InstrWrap:
    def __init__(self, instr, pool):
        self._i, self._p = instr, pool

    def then_inc(self, sem, n):
        self._p.note_inc(sem, n)
        r = self._i.then_inc(sem, n)
        return InstrWrap(r, self._p) if hasattr(r, "then_inc") else r

    def __getattr__(self, k):
        return getattr(self._i, k)


class EngWrap:
    def __init__(self, eng, pool):
        self._e, self._p = eng, pool

    def wait_ge(self, sem, v):
        r = self._e.wait_ge(sem, v + self._p.base_of(sem))
        return InstrWrap(r, self._p) if hasattr(r, "then_inc") else r

    def wait_abs(self, sem, v):
        r = self._e.wait_ge(sem, v)
        return InstrWrap(r, self._p) if hasattr(r, "then_inc") else r

    def __getattr__(self, k):
        a = getattr(self._e, k)
        if not callable(a):
            return a

        def call(*args, **kw):
            r = a(*args, **kw)
            return InstrWrap(r, self._p) if hasattr(r, "then_inc") else r
        return call


class BlockWrap:
    ENGS = ("sync", "gpsimd", "vector", "scalar", "tensor")

    def __init__(self, block, pool, gates):
        self._b, self._p, self._g = block, pool, gates
        self._gated = set()

    def _mk(self, engname):
        dec = getattr(self._b, engname)
        pool, gates, gated = self._p, self._g, self._gated

        def deco(fn):
            def inner(eng):
                w = EngWrap(eng, pool)
                if engname not in gated:
                    gated.add(engname)
                    for s, v in gates:
                        w.wait_abs(s, v)
                return fn(w)
            return dec(inner)
        return deco

    @property
    def sync(self):
        return self._mk("sync")

    @property
    def gpsimd(self):
        return self._mk("gpsimd")

    @property
    def vector(self):
        return self._mk("vector")

    @property
    def scalar(self):
        return self._mk("scalar")

    @property
    def tensor(self):
        return self._mk("tensor")



DEBUG_L0FFN = False


def emit_l0ffn(nc, block, pool, pfx, gates, io):
    from contextlib import ExitStack
    es = ExitStack()
    sbt = lambda n, sh, dt: es.enter_context(nc.sbuf_tensor(pfx + n, sh, dt))
    pst = lambda n, sh, dt: es.enter_context(nc.psum_tensor(pfx + n, sh, dt))
    block = BlockWrap(block, pool, gates)
    pool.begin()
    dbg = {}
    x_in, x_out = io["x_in"], io["x_out"]
    w_in, b_in, ident = io["w"], io["b"], io["ident"]

    wr = [sbt(f"wr{j}", [128, D], F32) for j in range(2)]
    wq = sbt("wq", [128, D], BF16)
    wT = sbt("wT", [128, 8 * D], BF16)
    identb = sbt("identb", [128, 128], BF16)
    xt = [sbt(f"xt{j}", [128, D], F32) for j in range(2)]
    xn = sbt("xn", [128, D], F32)
    xq = sbt("xq", [128, D], BF16)
    xfm = sbt("xfm", [128, D], BF16)
    biasb = sbt("biasb", [128, D], F32)
    ty = sbt("ty", [128, 512], F32)
    xo = [sbt(f"xo{j}", [128, D], F32) for j in range(2)]
    t = alloc_quant_scratch(nc, pfx, es)
    st = {k: sbt("s_" + k, [128, 1], F32) for k in ["lnv", "eps"]}
    st["negm"] = sbt("s_negm", [128, 2], F32)
    st["r"] = sbt("s_r", [128, 2], F32)
    st["mv"] = sbt("s_mv", [128, 4], F32)
    st["stat12"] = sbt("s_stat12", [128, 12], F32)
    fv = sbt("fv", [128, 1], F32)
    fa = sbt("fa", [128, 1], F32)
    dbq = sbt("dbq", [128, D], F32)
    dbs = sbt("dbs", [128, 4], F32)

    ptr = pst("ptr", [128, 1024], BF16)
    py = [pst(f"py{j}", [128, 512], F32) for j in range(2)]

    sem = {n: pool.get(n) for n in
           ["ldw0", "ldw1", "ldi", "ldb", "wq_rdy", "wtr", "wev", "ldx0", "ldx1",
            "v2a", "a2v", "xq_rdy", "xtr", "xev", "mm", "ep", "so0", "so1",
            "cst",
            "dbg", "dbd", "mvr"]}

    ptr3 = bass.AP(ptr, 0, [[1024, 128], [128, 8], [1, 128]])

    if True:
        @block.gpsimd
        def _(g):
            g.dma_start(out=identb[:, :], in_=ident[:, :]).then_inc(sem["ldi"], 16)
            for k in range(8):
                if k >= 2:
                    g.wait_ge(sem["wq_rdy"], k - 1)
                g.dma_start(out=wr[k % 2][:, :],
                            in_=w_in[k * 128:(k + 1) * 128, :]).then_inc(
                                sem[f"ldw{k % 2}"], 16)

        @block.sync
        def _(s):
            s.dma_start(out=biasb[:, :], in_=bcast_row(b_in, 0, D)).then_inc(
                sem["ldb"], 16)
            for i in range(NT128):
                if i >= 2:
                    s.wait_ge(sem["ep"], i - 1)
                s.dma_start(out=xt[i % 2][:, :],
                            in_=x_in[i * 128:(i + 1) * 128, :]).then_inc(
                                sem[f"ldx{i % 2}"], 16)
                if i >= 1:
                    s.wait_ge(sem["ep"], i)
                    s.dma_start(out=x_out[(i - 1) * 128:i * 128, :],
                                in_=xo[(i - 1) % 2][:, :]).then_inc(
                                    sem[f"so{(i - 1) % 2}"], 16)
            s.wait_ge(sem["ep"], NT128)
            s.dma_start(out=x_out[(NT128 - 1) * 128:, :],
                        in_=xo[(NT128 - 1) % 2][:, :]).then_inc(
                            sem[f"so{(NT128 - 1) % 2}"], 16)
            s.wait_ge(sem["so0"], 16 * (NT128 // 2))
            s.wait_ge(sem["so1"], 16 * (NT128 // 2))

        @block.scalar
        def _(sc):
            sc.wait_ge(sem["cst"], 1)
            for i in range(NT128):
                sc.wait_ge(sem["mvr"], i + 1)
                emit_ln_stats_act(sc, st, j=i % 2)
                act_fence(sc).then_inc(sem["a2v"], 1)

        @block.vector
        def _(v):
            v.memset(st["eps"][:, :], 1e-5).then_inc(sem["cst"], 1)
            for k in range(8):
                v.wait_ge(sem[f"ldw{k % 2}"], 16 * (k // 2 + 1))
                emit_quant_dve(v, wr[k % 2], wq, t)
                dve_fence(v).then_inc(sem["wq_rdy"], 1)
                v.wait_ge(sem["wtr"], k + 1)
                v.tensor_copy(
                    bass.AP(wT, k * 128, [[8 * D, 128], [1024, 8], [1, 128]]),
                    ptr3)
                dve_fence(v).then_inc(sem["wev"], 1)
            for i in range(NT128):
                v.wait_ge(sem[f"ldx{i % 2}"], 16 * (i // 2 + 1))
                xc = xt[i % 2]
                if i >= 2:
                    v.wait_ge(sem["a2v"], i - 1)
                emit_ln_stats_dve(v, xc, st, j=i % 2)
                dve_fence(v).then_inc(sem["mvr"], 1)
                v.wait_ge(sem["a2v"], i + 1)
                if DEBUG_L0FFN and i >= 1:
                    v.wait_ge(sem["dbd"], 48 * i)
                v.tensor_scalar(xn[:, :], xc[:, :],
                                st["negm"][:, i % 2:i % 2 + 1],
                                st["r"][:, i % 2:i % 2 + 1],
                                ALU.add, ALU.mult)
                v.drain()
                emit_quant_dve(v, xn, xq, t)
                if DEBUG_L0FFN:
                    v.tensor_copy(dbq[:, :], xq[:, :])
                    v.tensor_copy(dbs[:, 0:1], st["negm"][:, i % 2:i % 2 + 1])
                    v.tensor_copy(dbs[:, 1:2], st["r"][:, i % 2:i % 2 + 1])
                    v.tensor_copy(dbs[:, 2:4], st["mv"][:, 2*(i%2):2*(i%2)+2])
                dve_fence(v).then_inc(sem["xq_rdy"], 1)
                if DEBUG_L0FFN:
                    v.tensor_copy(dbq[:, 0:1], dbq[:, 0:1]).then_inc(
                        sem["dbg"], 1)
                v.wait_ge(sem["xtr"], i + 1)
                v.tensor_copy(xfm[:, :], ptr[:, :])
                dve_fence(v).then_inc(sem["xev"], 1)
                if i == 0:
                    v.wait_ge(sem["ldb"], 16)
                if i >= 2:
                    v.wait_ge(sem[f"so{i % 2}"], 16 * (i // 2))
                for h in range(2):
                    v.wait_ge(sem["mm"], 2 * i + h + 1)
                    v.tensor_tensor(ty[:, :], py[h][:, :],
                                    biasb[:, h * 512:(h + 1) * 512], ALU.add)
                    v.drain()
                    v.scalar_tensor_tensor(
                        xo[i % 2][:, h * 512:(h + 1) * 512], ty[:, :], 0.5,
                        xc[:, h * 512:(h + 1) * 512], ALU.mult, ALU.add)
                    v.drain()
                dve_fence(v).then_inc(sem["ep"], 1)

        @block.tensor
        def _(te):
            te.wait_ge(sem["ldi"], 16)
            for k in range(8):
                te.wait_ge(sem["wq_rdy"], k + 1)
                if k >= 1:
                    te.wait_ge(sem["wev"], k)
                for c in range(8):
                    mm = te.transpose(ptr[:, c * 128:(c + 1) * 128],
                                      wq[:, c * 128:(c + 1) * 128],
                                      identb[:, :])
                mm.then_inc(sem["wtr"], 1)
            for i in range(NT128):
                te.wait_ge(sem["xq_rdy"], i + 1)
                te.wait_ge(sem["wev"], 8) if i == 0 else te.wait_ge(
                    sem["xev"], i)
                for c in range(8):
                    mm = te.transpose(ptr[:, c * 128:(c + 1) * 128],
                                      xq[:, c * 128:(c + 1) * 128],
                                      identb[:, :])
                mm.then_inc(sem["xtr"], 1)
                te.wait_ge(sem["xev"], i + 1)
                if i > 0:
                    te.wait_ge(sem["ep"], i)
                for h in range(2):
                    for c in range(8):
                        mm = te.matmul(py[h][:, :],
                                       xfm[:, c * 128:(c + 1) * 128],
                                       wT[:, c * 1024 + h * 512:
                                          c * 1024 + (h + 1) * 512],
                                       start=(c == 0), stop=(c == 7))
                    mm.then_inc(sem["mm"], 1)

        if DEBUG_L0FFN:
            @block.gpsimd
            def _(g):
                for i in range(NT128):
                    g.wait_ge(sem["dbg"], i + 1)
                    g.dma_start(out=dbg["xn"][i * 128:(i + 1) * 128, :],
                                in_=xn[:, :]).then_inc(sem["dbd"], 16)
                    g.dma_start(out=dbg["xq"][i * 128:(i + 1) * 128, :],
                                in_=dbq[:, :]).then_inc(sem["dbd"], 16)
                    g.dma_start(out=dbg["st"].ap()[i, :, :],
                                in_=dbs[:, :]).then_inc(sem["dbd"], 16)
                g.wait_ge(sem["wev"], 8)
                g.dma_start(out=dbg["wT"][:, :], in_=wT[:, :]).then_inc(
                    sem["dbd"], 16)
                g.wait_ge(sem["dbd"], 16 * (3 * NT128 + 1))

    es.close()
    pool.end()
    return [(sem["so0"], pool.base["so0"]), (sem["so1"], pool.base["so1"])]


def build_l0ffn():
    nc = bass.Bass("TRN2", target_bir_lowering=False)
    io = {
        "x_in": nc.dram_tensor("x_in", [L, D], F32, kind="ExternalInput"),
        "w": nc.dram_tensor("w", [D, D], F32, kind="ExternalInput"),
        "b": nc.dram_tensor("b", [D], F32, kind="ExternalInput"),
        "ident": nc.dram_tensor("ident", [128, 128], F32,
                                kind="ExternalInput"),
        "x_out": nc.dram_tensor("x_out", [L, D], F32, kind="ExternalOutput"),
    }
    pool = SemPool(nc)
    with nc.Block() as block:
        emit_l0ffn(nc, block, pool, "", [], io)
    return nc




from contextlib import ExitStack

TC = 256
NCH = L // TC
T5 = 512
NC5 = L // T5


def emit_mamba(nc, block, pool, pfx, gates, io, v1):
    es = ExitStack()
    sbt = lambda *a: es.enter_context(nc.sbuf_tensor(*a))
    block = BlockWrap(block, pool, gates)
    pool.begin()
    DI = DIN
    NDC = DI // 128
    x_in = io["x_in"]
    w_proj = io["w_proj"]
    conv_w = io["conv_w"]
    conv_b = io["conv_b"]
    wx_d, wx_B, wx_C = io["wx_d"], io["wx_B"], io["wx_C"]
    w_dt, b_dt = io["w_dt"], io["b_dt"]
    a_log, d_vec, w_out = io["a_log"], io["d_vec"], io["w_out"]
    ident = io["ident"]
    if v1:
        ln_w, ln_b = io["ln_w"], io["ln_b"]
        in_b, out_b = io["in_b"], io["out_b"]
    x_out = io["x_out"]

    upre_h = nc.dram_tensor(pfx + "upre_h", [DI, L + 4], F32 if v1 else BF16)
    res_h = nc.dram_tensor(pfx + "res_h", [DI, L], F32)
    u_h = nc.dram_tensor(pfx + "u_h", [DI, L], F32)
    del_h = nc.dram_tensor(pfx + "del_h", [DI, L], F32)
    bc_h = nc.dram_tensor(pfx + "bc_h", [2 * NST, L], F32)
    y_h = nc.dram_tensor(pfx + "y_h", [DI, L], F32)
    hid_h = nc.dram_tensor(pfx + "hid_h", [L, D], F32) if v1 else None

    def fmdc(dram, dc, col0, tlen, row0=0, width=L):
        return bass.AP(dram, (row0 + dc * 128) * width + col0,
                       [[width, 128], [1, tlen]])

    def fm_dma(eng, dram, col0, tlen, sbuf, seg, s_sem, ndc=NDC, row0=0,
               store=False, width=L):
        for dc in range(ndc):
            sb = sbuf[:, dc * seg:dc * seg + tlen]
            dr = fmdc(dram, dc, col0, tlen, row0, width)
            if store:
                eng.dma_start(out=dr, in_=sb).then_inc(s_sem, 16)
            else:
                eng.dma_start(out=sb, in_=dr).then_inc(s_sem, 16)

    identb = sbt(pfx + "identb", [128, 128], BF16)
    a_sb = sbt(pfx + "a_sb", [128, NDC * NST], F32)
    dcol = sbt(pfx + "dcol", [128, NDC], F32)
    bdtcol = sbt(pfx + "bdtcol", [128, NDC], F32)
    cbcol = sbt(pfx + "cbcol", [128, NDC], F32)
    wdt_sb = sbt(pfx + "wdt_sb", [RDT, DI], BF16)
    wxd_sb = sbt(pfx + "wxd_sb", [128, NDC * RDT], BF16)
    wxB_sb = sbt(pfx + "wxB_sb", [128, NDC * NST], BF16)
    wxC_sb = sbt(pfx + "wxC_sb", [128, NDC * NST], BF16)
    rf_sb = sbt(pfx + "rf_sb", [128, NDC], F32)
    rft = sbt(pfx + "rft", [128, 1], F32)
    carry = sbt(pfx + "carry", [128, NDC * NST], F32)
    st = {k: sbt(pfx + "s_" + k, [128, 1], F32)
          for k in ["lnv", "eps"]}
    st["negm"] = sbt(pfx + "s_negm", [128, 4], F32)
    st["r"] = sbt(pfx + "s_r", [128, 4], F32)
    st["mv"] = sbt(pfx + "s_mv", [128, 8], F32)
    st["stat12"] = sbt(pfx + "s_stat12", [128, 12], F32)
    zpad4 = sbt(pfx + "zpad4", [128, 4], F32 if v1 else BF16)
    if v1:
        inbcol = sbt(pfx + "inbcol", [128, 2 * NDC], F32)
        cw2col = sbt(pfx + "cw2col", [128, NDC * 4], F32)
    pLN = ExitStack()
    if v1:
        lnw_b = pLN.enter_context(nc.sbuf_tensor(pfx + "lnw_b", [128, D], F32))
        lnb_b = pLN.enter_context(nc.sbuf_tensor(pfx + "lnb_b", [128, D], F32))

    sem = {n: pool.get(n) for n in [
        "ldc", "ldw1",
        "ldx0", "ldx1", "unr", "mvr", "a2v", "sqr", "tr1", "rir", "mm1", "ubr",
        "spre", "sres", "shid",
        "cwld", "upld0", "upld1", "mmsc", "scf", "mmh", "su0", "su1",
        "uev2", "zp", "zpd",
        "ubld0", "ubld1", "hev", "rcr", "spa", "dtmm", "spl", "dcons",
        "dstgr", "sdel0", "sdel1", "sbc0", "sbc1", "spa3",
        "ld4", "ld4p0", "ld4p1", "p4a", "p4c", "p4y", "vdone",
        "sy0", "sy1", "sy2", "sy3",
        "ldw5", "ld5a", "ld5b", "s5a", "ygr", "ymm", "p5e", "ep5", "so5",
    ]}

    if True:
        @block.gpsimd
        def _(g):
            g.dma_start(out=identb[:, :], in_=ident[:, :]).then_inc(sem["ldc"], 16)
            g.dma_start(out=wdt_sb[:, :], in_=w_dt[:, :]).then_inc(sem["ldc"], 16)
            for dc in range(NDC):
                g.dma_start(out=wxd_sb[:, dc * RDT:(dc + 1) * RDT],
                            in_=wx_d[dc * 128:(dc + 1) * 128, :]).then_inc(
                                sem["ldc"], 16)
                g.dma_start(out=wxB_sb[:, dc * NST:(dc + 1) * NST],
                            in_=wx_B[dc * 128:(dc + 1) * 128, :]).then_inc(
                                sem["ldc"], 16)
                g.dma_start(out=wxC_sb[:, dc * NST:(dc + 1) * NST],
                            in_=wx_C[dc * 128:(dc + 1) * 128, :]).then_inc(
                                sem["ldc"], 16)
                g.dma_start(out=a_sb[:, dc * NST:(dc + 1) * NST],
                            in_=a_log[dc * 128:(dc + 1) * 128, :]).then_inc(
                                sem["ldc"], 16)
                g.dma_start(out=dcol[:, dc:dc + 1],
                            in_=bass.AP(d_vec, dc * 128, [[1, 128], [1, 1]])
                            ).then_inc(sem["ldc"], 16)
                g.dma_start(out=bdtcol[:, dc:dc + 1],
                            in_=bass.AP(b_dt, dc * 128, [[1, 128], [1, 1]])
                            ).then_inc(sem["ldc"], 16)
                g.dma_start(out=cbcol[:, dc:dc + 1],
                            in_=bass.AP(conv_b, dc * 128, [[1, 128], [1, 1]])
                            ).then_inc(sem["ldc"], 16)
                if v1:
                    g.dma_start(out=cw2col[:, dc * 4:(dc + 1) * 4],
                                in_=conv_w[dc * 128:(dc + 1) * 128, :]
                                ).then_inc(sem["ldc"], 16)
            if v1:
                g.dma_start(out=lnw_b[:, :], in_=bcast_row(ln_w, 0, D)
                            ).then_inc(sem["ldc"], 16)
                g.dma_start(out=lnb_b[:, :], in_=bcast_row(ln_b, 0, D)
                            ).then_inc(sem["ldc"], 16)
                for h in range(2 * NDC):
                    g.dma_start(out=inbcol[:, h:h + 1],
                                in_=bass.AP(in_b, h * 128, [[1, 128], [1, 1]])
                                ).then_inc(sem["ldc"], 16)

        NLDC = 16 * (2 + NDC * (8 if v1 else 7) + ((2 + 2 * NDC) if v1 else 0))

        @block.sync
        def _(s):
            s.wait_ge(sem["zp"], 1)
            for dc in range(NDC):
                s.dma_start(out=fmdc(upre_h, dc, 0, 4, width=L + 4),
                            in_=zpad4[:, :]).then_inc(sem["zpd"], 16)

        @block.vector
        def _(v):
            v.memset(st["eps"][:, :], 1e-5)
            v.memset(carry[:, :], 0.0)
            v.memset(zpad4[:, :], 0.0)
            v.drain().then_inc(sem["zp"], 1)
            v.wait_ge(sem["ldc"], NLDC)
            v.drain().then_inc(sem["unr"], 1)

        @block.scalar
        def _(sc):
            sc.wait_ge(sem["unr"], 1)
            sc.activation(a_sb[:, :], a_sb[:, :], AF.Exp, scale=1.0)
            sc.drain().then_inc(sem["a2v"], 1)

        @block.vector
        def _(v):
            v.wait_ge(sem["a2v"], 1)
            v.tensor_scalar(a_sb[:, :], a_sb[:, :], -1.0, None, ALU.mult)
            v.drain().then_inc(sem["unr"], 1)

        MVR0, A2V0, VP10 = 0, 1, 2
        APT = 2 if v1 else 1

        _sid = nc.enter_named_scope(pfx + "p1", False)[0]
        p1 = ExitStack()
        wproj_sb = [p1.enter_context(
            nc.sbuf_tensor(pfx + f"wp{dc}", [128, 2 * DI], BF16)) for dc in range(8)]
        xt = [p1.enter_context(nc.sbuf_tensor(pfx + f"xt{j}", [128, D], F32))
              for j in range(2)]
        hidt = p1.enter_context(nc.sbuf_tensor(pfx + "hidt", [128, D], F32))
        hnt = p1.enter_context(nc.sbuf_tensor(pfx + "hnt", [128, D], F32))
        xnb = p1.enter_context(nc.sbuf_tensor(pfx + "sqr", [128, D], BF16))
        xfmb = p1.enter_context(nc.sbuf_tensor(pfx + "xfmb", [128, 8 * T5], BF16))
        ustg = p1.enter_context(nc.sbuf_tensor(pfx + "ustg", [128, NDC * T5],
                                               F32 if v1 else BF16))
        rstg = p1.enter_context(nc.sbuf_tensor(pfx + "rstg", [128, NDC * T5], F32))
        ptr = p1.enter_context(nc.psum_tensor(pfx + "ptr", [128, 1024], BF16))
        pmm = [p1.enter_context(nc.psum_tensor(pfx + f"pmm{j}", [128, T5], F32))
               for j in range(2)]
        ptr3 = bass.AP(ptr, 0, [[1024, 128], [128, 8], [1, 128]])

        @block.gpsimd
        def _(g):
            for dc in range(8):
                g.dma_start(out=wproj_sb[dc][:, :],
                            in_=w_proj[dc * 128:(dc + 1) * 128, :]).then_inc(
                                sem["ldw1"], 16)
            for tt in range(NC5):
                if v1:
                    for q in range(4):
                        i = 4 * tt + q
                        g.wait_ge(sem["unr"], VP10 + i + 1)
                        g.dma_start(out=hid_h[i * 128:(i + 1) * 128, :],
                                    in_=hidt[:, :]).then_inc(sem["shid"], 16)
                g.wait_ge(sem["ubr"], tt + 1)
                fm_dma(g, upre_h, tt * T5 + 4, T5, ustg, T5, sem["spre"],
                       store=True, width=L + 4)
                fm_dma(g, res_h, tt * T5, T5, rstg, T5, sem["sres"],
                       store=True)

        @block.sync
        def _(s):
            for i in range(NT128):
                if i >= 2:
                    s.wait_ge(sem["unr"], VP10 + i - 1)
                s.dma_start(out=xt[i % 2][:, :],
                            in_=x_in[i * 128:(i + 1) * 128, :]).then_inc(
                                sem[f"ldx{i % 2}"], 16)

        @block.scalar
        def _(sc):
            for tt in range(NC5):
                for q in range(4):
                    i = 4 * tt + q
                    sc.wait_ge(sem["mvr"], MVR0 + APT * i + 1)
                    emit_ln_stats_act(sc, st, j=i % 2)
                    act_fence(sc).then_inc(sem["a2v"], 1)
                    if v1:
                        sc.wait_ge(sem["mvr"], MVR0 + 2 * i + 2)
                        emit_ln_stats_act(sc, st, j=2 + i % 2)
                        act_fence(sc).then_inc(sem["a2v"], 1)
                if tt >= 1:
                    sc.wait_ge(sem["spre"], 16 * NDC * tt)
                    sc.wait_ge(sem["sres"], 16 * NDC * tt)
                for oc in range(2 * NDC):
                    sc.wait_ge(sem["mm1"], 32 * tt + oc + 1)
                    dst = (ustg[:, oc * T5:(oc + 1) * T5] if oc < NDC else
                           rstg[:, (oc - NDC) * T5:(oc - NDC + 1) * T5])
                    if v1:
                        sc.activation(dst, pmm[oc % 2][:, :], AF.Identity,
                                      bias=inbcol[:, oc:oc + 1])
                    else:
                        sc.activation(dst, pmm[oc % 2][:, :], AF.Copy)
                    sc.drain().then_inc(sem["uev2"], 1)
                sc.drain().then_inc(sem["ubr"], 1)

        @block.vector
        def _(v):
            for i in range(NT128):
                tt, q = divmod(i, 4)
                v.wait_ge(sem[f"ldx{i % 2}"], 16 * (i // 2 + 1))
                xc = xt[i % 2]
                if i >= 2:
                    v.wait_ge(sem["a2v"], A2V0 + APT * (i - 1))
                emit_ln_stats_dve(v, xc, st, j=i % 2)
                dve_fence(v).then_inc(sem["mvr"], 1)
                v.wait_ge(sem["a2v"], A2V0 + APT * i + 1)
                if v1:
                    if i >= 1:
                        v.wait_ge(sem["shid"], 16 * i)
                    v.tensor_scalar(hidt[:, :], xc[:, :],
                                    st["negm"][:, i % 2:i % 2 + 1],
                                    st["r"][:, i % 2:i % 2 + 1],
                                    ALU.add, ALU.mult)
                    dve_fence(v).then_inc(sem["unr"], 1)
                    emit_ln_stats_dve(v, hidt, st, j=2 + i % 2)
                    dve_fence(v).then_inc(sem["mvr"], 1)
                    v.wait_ge(sem["a2v"], A2V0 + 2 * i + 2)
                    v.tensor_scalar(hnt[:, :], hidt[:, :],
                                    st["negm"][:, 2 + i % 2:3 + i % 2],
                                    st["r"][:, 2 + i % 2:3 + i % 2],
                                    ALU.add, ALU.mult)
                    v.drain()
                    v.tensor_tensor(hnt[:, :], hnt[:, :], lnw_b[:, :], ALU.mult)
                    v.drain()
                    v.tensor_tensor(xnb[:, :], hnt[:, :], lnb_b[:, :], ALU.add)
                else:
                    v.tensor_scalar(xnb[:, :], xc[:, :],
                                    st["negm"][:, i % 2:i % 2 + 1],
                                    st["r"][:, i % 2:i % 2 + 1],
                                    ALU.add, ALU.mult)
                    dve_fence(v).then_inc(sem["unr"], 1)
                dve_fence(v).then_inc(sem["sqr"], 1)
                v.wait_ge(sem["tr1"], i + 1)
                if i >= 4:
                    v.wait_ge(sem["mm1"], 32 * (i // 4))
                v.tensor_copy(
                    bass.AP(xfmb, q * 128,
                            [[8 * T5, 128], [T5, 8], [1, 128]]), ptr3)
                dve_fence(v).then_inc(sem["rir"], 1)

        @block.tensor
        def _(te):
            te.wait_ge(sem["ldc"], NLDC)
            te.wait_ge(sem["ldw1"], 16 * 8)
            for tt in range(NC5):
                for q in range(4):
                    i = 4 * tt + q
                    te.wait_ge(sem["sqr"], i + 1)
                    if i > 0:
                        te.wait_ge(sem["rir"], i)
                    for c in range(8):
                        mm = te.transpose(ptr[:, c * 128:(c + 1) * 128],
                                          xnb[:, c * 128:(c + 1) * 128],
                                          identb[:, :])
                    mm.then_inc(sem["tr1"], 1)
                te.wait_ge(sem["rir"], 4 * (tt + 1))
                for oc in range(2 * NDC):
                    gidx = 32 * tt + oc
                    if gidx >= 2:
                        te.wait_ge(sem["uev2"], gidx - 1)
                    for dc in range(8):
                        mm = te.matmul(pmm[oc % 2][:, :],
                                       wproj_sb[dc][:, oc * 128:(oc + 1) * 128],
                                       xfmb[:, dc * T5:(dc + 1) * T5],
                                       start=(dc == 0), stop=(dc == 7))
                    mm.then_inc(sem["mm1"], 1)

        p1.close()
        pLN.close()
        nc.leave_named_scope(pfx + "p1", _sid, False)

        _sid = nc.enter_named_scope(pfx + "p2", False)[0]
        p2 = ExitStack()
        P1S = 16 * NDC * NC5
        UPW = 16 * NDC * (T5 + 4)
        if v1:
            up2 = [p2.enter_context(nc.sbuf_tensor(
                pfx + f"up2_{j}", [128, NDC * (T5 + 4)], F32)) for j in range(2)]
            cacc = [p2.enter_context(nc.sbuf_tensor(
                pfx + f"cacc{j}", [128, T5], F32)) for j in range(2)]
            tapa = p2.enter_context(nc.sbuf_tensor(pfx + "tapa", [128, T5], F32))
            tapb = p2.enter_context(nc.sbuf_tensor(pfx + "tapb", [128, T5], F32))
            ustg2 = [p2.enter_context(nc.sbuf_tensor(
                pfx + f"ustg2_{j}", [128, NDC * T5], F32)) for j in range(2)]

            @block.sync
            def _(s):
                for cc in range(NC5):
                    if cc == 0:
                        s.wait_ge(sem["spre"], P1S)
                        s.wait_ge(sem["sres"], P1S)
                        s.wait_ge(sem["shid"], 16 * NT128)
                    if cc >= 2:
                        s.wait_ge(sem["mmh"], NDC * (cc - 1))
                    buf = up2[cc % 2]
                    if cc == 0:
                        s.wait_ge(sem["zpd"], 16 * NDC)
                    fm_dma(s, upre_h, cc * T5, T5 + 4, buf, T5 + 4,
                           sem[f"upld{cc % 2}"], width=L + 4)

            @block.vector
            def _(v):
                v.wait_ge(sem["spre"], P1S)
                v.wait_ge(sem["sres"], P1S)
                v.wait_ge(sem["shid"], 16 * NT128)
                v.drain().then_inc(sem["mmsc"], 1)
                for cc in range(NC5):
                    buf = up2[cc % 2]
                    v.wait_ge(sem[f"upld{cc % 2}"], 16 * NDC * (cc // 2 + 1))
                    for dc in range(NDC):
                        idx = NDC * cc + dc
                        if idx >= 2:
                            v.wait_ge(sem["scf"], idx - 1)
                        base = dc * (T5 + 4)
                        ca = cacc[dc % 2]
                        v.tensor_scalar(tapa[:, :],
                                        buf[:, base + 1:base + 1 + T5],
                                        cw2col[:, dc * 4:dc * 4 + 1], None,
                                        ALU.mult)
                        v.tensor_scalar(tapb[:, :],
                                        buf[:, base + 2:base + 2 + T5],
                                        cw2col[:, dc * 4 + 1:dc * 4 + 2], None,
                                        ALU.mult)
                        v.drain()
                        v.scalar_tensor_tensor(
                            tapa[:, :], buf[:, base + 3:base + 3 + T5],
                            cw2col[:, dc * 4 + 2:dc * 4 + 3], tapa[:, :],
                            ALU.mult, ALU.add)
                        v.scalar_tensor_tensor(
                            tapb[:, :], buf[:, base + 4:base + 4 + T5],
                            cw2col[:, dc * 4 + 3:dc * 4 + 4], tapb[:, :],
                            ALU.mult, ALU.add)
                        v.drain()
                        v.tensor_tensor(ca[:, :], tapa[:, :], tapb[:, :],
                                        ALU.add)
                        v.drain().then_inc(sem["mmh"], 1)

            @block.scalar
            def _(sc):
                for cc in range(NC5):
                    if cc >= 2:
                        sc.wait_ge(sem[f"su{cc % 2}"], 16 * NDC * (cc // 2))
                    for dc in range(NDC):
                        idx = NDC * cc + dc
                        sc.wait_ge(sem["mmh"], idx + 1)
                        sc.activation(ustg2[cc % 2][:, dc * T5:(dc + 1) * T5],
                                      cacc[dc % 2][:, :], AF.Silu,
                                      bias=cbcol[:, dc:dc + 1])
                        sc.drain().then_inc(sem["scf"], 1)
                    sc.drain()
                    fm_dma(sc, u_h, cc * T5, T5, ustg2[cc % 2], T5,
                           sem[f"su{cc % 2}"], store=True)
            SU_T = 16 * NDC * (NC5 // 2)
        else:
            cw_sb = [p2.enter_context(nc.sbuf_tensor(
                pfx + f"cw{t_}_{ic}", [128, 8 * 128], BF16))
                for t_ in range(4) for ic in range(NDC)]
            up2 = [p2.enter_context(nc.sbuf_tensor(
                pfx + f"up2_{j}", [128, NDC * (T5 + 4)], BF16)) for j in range(2)]
            ustg2 = [p2.enter_context(nc.sbuf_tensor(
                pfx + f"ustg2_{j}", [128, 8 * T5], F32)) for j in range(2)]
            pcv = [p2.enter_context(nc.psum_tensor(pfx + f"pcv{j}", [128, T5], F32))
                   for j in range(2)]

            @block.gpsimd
            def _(g):
                for h2 in range(2):
                    if h2 == 0:
                        g.wait_ge(sem["spre"], P1S)
                        g.wait_ge(sem["sres"], P1S)
                    else:
                        g.wait_ge(sem["mmsc"], 1 + 64)
                    for t_ in range(4):
                        for ic in range(NDC):
                            g.dma_start(
                                out=cw_sb[t_ * NDC + ic][:, :],
                                in_=conv_w.ap()[t_, ic * 128:(ic + 1) * 128,
                                                h2 * 1024:(h2 + 1) * 1024]
                                ).then_inc(sem["cwld"], 16)

            @block.sync
            def _(s):
                for h2 in range(2):
                    for cc in range(NC5):
                        idx = h2 * NC5 + cc
                        if idx == 0:
                            s.wait_ge(sem["spre"], P1S)
                            s.wait_ge(sem["sres"], P1S)
                        if idx >= 2:
                            s.wait_ge(sem["mmsc"], 1 + 8 * (idx - 1))
                        buf = up2[idx % 2]
                        if idx == 0:
                            s.wait_ge(sem["zpd"], 16 * NDC)
                        fm_dma(s, upre_h, cc * T5, T5 + 4, buf[:, :],
                               T5 + 4, sem[f"upld{idx % 2}"], width=L + 4)

            @block.vector
            def _(v):
                v.wait_ge(sem["spre"], P1S)
                v.wait_ge(sem["sres"], P1S)
                v.drain().then_inc(sem["mmsc"], 1)

            @block.tensor
            def _(te):
                te.wait_ge(sem["mmsc"], 1)
                for h2 in range(2):
                    te.wait_ge(sem["cwld"], 16 * 64 * (h2 + 1))
                    for cc in range(NC5):
                        idx = h2 * NC5 + cc
                        te.wait_ge(sem[f"upld{idx % 2}"],
                                   16 * NDC * (idx // 2 + 1))
                        buf = up2[idx % 2]
                        for oc8 in range(8):
                            gidx = 8 * idx + oc8
                            if gidx >= 2:
                                te.wait_ge(sem["scf"], gidx - 1)
                            first = True
                            for t_ in range(4):
                                for ic in range(NDC):
                                    mm = te.matmul(
                                        pcv[oc8 % 2][:, :],
                                        cw_sb[t_ * NDC + ic][:,
                                                             oc8 * 128:
                                                             (oc8 + 1) * 128],
                                        buf[:, ic * (T5 + 4) + 1 + t_:
                                            ic * (T5 + 4) + 1 + t_ + T5],
                                        start=first,
                                        stop=(t_ == 3 and ic == NDC - 1))
                                    first = False
                            mm.then_inc(sem["mmsc"], 1)

            @block.scalar
            def _(sc):
                for h2 in range(2):
                    for cc in range(NC5):
                        idx = h2 * NC5 + cc
                        if idx >= 2:
                            sc.wait_ge(sem[f"su{idx % 2}"], 16 * 8 * (idx // 2))
                        for oc8 in range(8):
                            gidx = 8 * idx + oc8
                            sc.wait_ge(sem["mmsc"], 1 + gidx + 1)
                            oc = h2 * 8 + oc8
                            sc.activation(
                                ustg2[idx % 2][:, oc8 * T5:(oc8 + 1) * T5],
                                pcv[oc8 % 2][:, :], AF.Silu,
                                bias=cbcol[:, oc:oc + 1])
                            sc.drain().then_inc(sem["scf"], 1)
                        sc.drain()
                        fm_dma(sc, u_h, cc * T5, T5, ustg2[idx % 2], T5,
                               sem[f"su{idx % 2}"], ndc=8, row0=h2 * 1024,
                               store=True)
            SU_T = 16 * 8 * NC5

        p2.close()
        nc.leave_named_scope(pfx + "p2", _sid, False)

        _sid = nc.enter_named_scope(pfx + "p3", False)[0]
        p3 = ExitStack()
        ub3 = [p3.enter_context(nc.sbuf_tensor(pfx + f"ub3_{j}", [128, NDC * T5],
                                               BF16)) for j in range(2)]
        draw = p3.enter_context(nc.sbuf_tensor(pfx + "draw", [RDT, T5], BF16))
        bcstB = [p3.enter_context(nc.sbuf_tensor(pfx + f"bcstB{j}", [NST, T5], F32))
                 for j in range(2)]
        bcstC = [p3.enter_context(nc.sbuf_tensor(pfx + f"bcstC{j}", [NST, T5], F32))
                 for j in range(2)]
        dstg = [p3.enter_context(nc.sbuf_tensor(pfx + f"dstg{j}", [128, NDC * T5],
                                                F32)) for j in range(2)]
        sp1 = [p3.enter_context(nc.sbuf_tensor(pfx + f"sp1_{j}", [128, T5], F32))
               for j in range(2)]
        sp2 = [p3.enter_context(nc.sbuf_tensor(pfx + f"sp2_{j}", [128, T5], F32))
               for j in range(2)]
        pxd = p3.enter_context(nc.psum_tensor(pfx + "pxd", [64, T5], F32))
        pxb = p3.enter_context(nc.psum_tensor(pfx + "pxb", [16, T5], F32))
        pxc = p3.enter_context(nc.psum_tensor(pfx + "pxc", [16, T5], F32))
        pdt = [p3.enter_context(nc.psum_tensor(pfx + f"pdt{j}", [128, T5], F32))
               for j in range(2)]

        @block.gpsimd
        def _(g):
            for cc in range(NC5):
                if cc == 0:
                    g.wait_ge(sem["su0"], SU_T)
                    g.wait_ge(sem["su1"], SU_T)
                if cc >= 2:
                    g.wait_ge(sem["hev"], cc - 1)
                fm_dma(g, u_h, cc * T5, T5, ub3[cc % 2], T5,
                       sem[f"ubld{cc % 2}"])

        @block.tensor
        def _(te):
            for cc in range(NC5):
                te.wait_ge(sem[f"ubld{cc % 2}"], 16 * NDC * (cc // 2 + 1))
                if cc >= 1:
                    te.wait_ge(sem["spa3"], cc)
                ub = ub3[cc % 2]
                for dc in range(NDC):
                    te.matmul(pxd[:, :], wxd_sb[:, dc * RDT:(dc + 1) * RDT],
                              ub[:, dc * T5:(dc + 1) * T5],
                              start=(dc == 0), stop=(dc == NDC - 1))
                for dc in range(NDC):
                    te.matmul(pxb[:, :], wxB_sb[:, dc * NST:(dc + 1) * NST],
                              ub[:, dc * T5:(dc + 1) * T5],
                              start=(dc == 0), stop=(dc == NDC - 1))
                for dc in range(NDC):
                    mm = te.matmul(pxc[:, :],
                                   wxC_sb[:, dc * NST:(dc + 1) * NST],
                                   ub[:, dc * T5:(dc + 1) * T5],
                                   start=(dc == 0), stop=(dc == NDC - 1))
                mm.then_inc(sem["hev"], 1)
                te.wait_ge(sem["rcr"], cc + 1)
                for dc in range(NDC):
                    idx = NDC * cc + dc
                    if idx >= 2:
                        te.wait_ge(sem["spl"], idx - 1)
                    te.matmul(pdt[dc % 2][:, :],
                              wdt_sb[:, dc * 128:(dc + 1) * 128],
                              draw[:, :], start=True,
                              stop=True).then_inc(sem["dtmm"], 1)

        @block.scalar
        def _(sc):
            for cc in range(NC5):
                sc.wait_ge(sem["hev"], cc + 1)
                if cc >= 1:
                    sc.wait_ge(sem["dtmm"], NDC * cc)
                sc.activation(draw[:, :], pxd[:, :], AF.Copy)
                sc.drain().then_inc(sem["rcr"], 1)
                if cc >= 2:
                    sc.wait_ge(sem[f"sbc{cc % 2}"], 32 * (cc // 2))
                sc.activation(bcstB[cc % 2][:, :], pxb[:, :], AF.Copy)
                sc.activation(bcstC[cc % 2][:, :], pxc[:, :], AF.Copy)
                sc.drain().then_inc(sem["spa"], 1)
                sc.drain().then_inc(sem["spa3"], 1)
                for dc in range(NDC):
                    idx = NDC * cc + dc
                    sc.wait_ge(sem["dtmm"], idx + 1)
                    if idx >= 2:
                        sc.wait_ge(sem["dcons"], idx - 1)
                    pp = pdt[dc % 2]
                    sc.activation(sp1[dc % 2][:, :], pp[:, :], AF.Abs,
                                  bias=bdtcol[:, dc:dc + 1])
                    sc.activation(sp2[dc % 2][:, :], pp[:, :], AF.Relu,
                                  bias=bdtcol[:, dc:dc + 1])
                    sc.drain()
                    sc.activation(sp1[dc % 2][:, :], sp1[dc % 2][:, :],
                                  AF.Exp, scale=-1.0)
                    sc.drain()
                    sc.activation(sp1[dc % 2][:, :], sp1[dc % 2][:, :],
                                  AF.Ln, bias=1.0)
                    sc.drain().then_inc(sem["spl"], 1)

        @block.vector
        def _(v):
            for cc in range(NC5):
                if cc >= 2:
                    v.wait_ge(sem[f"sdel{cc % 2}"], 16 * NDC * (cc // 2))
                for dc in range(NDC):
                    idx = NDC * cc + dc
                    v.wait_ge(sem["spl"], idx + 1)
                    v.tensor_tensor(dstg[cc % 2][:, dc * T5:(dc + 1) * T5],
                                    sp1[dc % 2][:, :], sp2[dc % 2][:, :],
                                    ALU.add).then_inc(sem["dcons"], 1)
                v.drain().then_inc(sem["dstgr"], 1)

        @block.sync
        def _(s):
            for cc in range(NC5):
                s.wait_ge(sem["dstgr"], cc + 1)
                fm_dma(s, del_h, cc * T5, T5, dstg[cc % 2], T5,
                       sem[f"sdel{cc % 2}"], store=True)
                s.wait_ge(sem["spa"], cc + 1)
                s.dma_start(out=bass.AP(bc_h, cc * T5, [[L, NST], [1, T5]]),
                            in_=bcstB[cc % 2][:, :]).then_inc(
                                sem[f"sbc{cc % 2}"], 16)
                s.dma_start(out=bass.AP(bc_h, NST * L + cc * T5,
                                        [[L, NST], [1, T5]]),
                            in_=bcstC[cc % 2][:, :]).then_inc(
                                sem[f"sbc{cc % 2}"], 16)

        p3.close()
        nc.leave_named_scope(pfx + "p3", _sid, False)

        _sid = nc.enter_named_scope(pfx + "p4", False)[0]
        p4 = ExitStack()
        dd4 = [p4.enter_context(nc.sbuf_tensor(pfx + f"dd4_{j}",
                                                [128, NDC * TC], F32))
               for j in range(2)]
        du4 = [p4.enter_context(nc.sbuf_tensor(pfx + f"du4_{j}",
                                                [128, NDC * TC], F32))
               for j in range(2)]
        dB4 = [p4.enter_context(nc.sbuf_tensor(pfx + f"dB4_{j}",
                                                [128, NST * TC], F32))
               for j in range(2)]
        dC4 = p4.enter_context(nc.sbuf_tensor(pfx + "dC4", [128, NST * TC], F32))
        dlu4 = p4.enter_context(nc.sbuf_tensor(pfx + "dlu4", [128, TC], F32))
        tr1 = p4.enter_context(nc.sbuf_tensor(pfx + "tr1", [128, 8 * TC], F32))
        tr2 = p4.enter_context(nc.sbuf_tensor(pfx + "tr2", [128, 4 * TC], F32))
        tr3 = p4.enter_context(nc.sbuf_tensor(pfx + "tr3", [128, 2 * TC], F32))
        s4 = p4.enter_context(nc.sbuf_tensor(pfx + "s4", [128, NST * TC], F32))
        ybd = [p4.enter_context(nc.sbuf_tensor(pfx + f"ybd{j}", [128, TC],
                                               F32)) for j in range(4)]
        abb = [p4.enter_context(nc.sbuf_tensor(pfx + f"abb{j}", [128, NST * TC],
                                               F32)) for j in range(2)]
        ball = [p4.enter_context(nc.sbuf_tensor(pfx + "ball0",
                                                [128, NST * TC], F32))]
        yred = p4.enter_context(nc.sbuf_tensor(pfx + "yred", [128, TC], F32))

        LD4C = 16 * (2 * NDC + NST)
        LDCC = 16 * NST

        @block.sync
        def _(s):
            for ch in range(NCH):
                if ch == 0:
                    s.wait_ge(sem["sdel0"], 16 * NDC * (NC5 // 2))
                    s.wait_ge(sem["sdel1"], 16 * NDC * (NC5 // 2))
                    s.wait_ge(sem["sbc0"], 32 * (NC5 // 2))
                    s.wait_ge(sem["sbc1"], 32 * (NC5 // 2))
                if ch >= 2:
                    s.wait_ge(sem["vdone"], ch - 1)
                fm_dma(s, del_h, ch * TC, TC, dd4[ch % 2], TC,
                       sem[f"ld4p{ch % 2}"])
                fm_dma(s, u_h, ch * TC, TC, du4[ch % 2], TC,
                       sem[f"ld4p{ch % 2}"])
                for n in range(NST):
                    s.dma_start(out=dB4[ch % 2][:, n * TC:(n + 1) * TC],
                                in_=bass.AP(bc_h, n * L + ch * TC,
                                            [[0, 128], [1, TC]])).then_inc(
                                    sem[f"ld4p{ch % 2}"], 16)
                if ch >= 1:
                    s.wait_ge(sem["vdone"], ch)
                for n in range(NST):
                    s.dma_start(out=dC4[:, n * TC:(n + 1) * TC],
                                in_=bass.AP(bc_h, (NST + n) * L + ch * TC,
                                            [[0, 128], [1, TC]])).then_inc(
                                    sem["ld4"], 16)

        @block.scalar
        def _(sc):
            for ch in range(NCH):
                sc.wait_ge(sem[f"ld4p{ch % 2}"], LD4C * (ch // 2 + 1))
                for dc in range(NDC):
                    g_ = ch * NDC + dc
                    if g_ >= 2:
                        sc.wait_ge(sem["p4c"], g_ - 1)
                    for n in range(NST):
                        sc.activation(abb[g_ % 2][:, n * TC:(n + 1) * TC],
                                      dd4[ch % 2][:, dc * TC:(dc + 1) * TC],
                                      AF.Exp,
                                      scale=a_sb[:, dc * NST + n:
                                                 dc * NST + n + 1])
                    sc.drain().then_inc(sem["p4a"], 1)

        HTC = (NST // 2) * TC
        @block.gpsimd
        def _(g):
            for ch in range(NCH):
                for dc in range(NDC):
                    g_ = ch * NDC + dc
                    g.wait_ge(sem["p4y"], g_ + 1)
                    fm_dma(g, y_h, ch * TC, TC, ybd[g_ % 4], TC,
                           sem[f"sy{g_ % 4}"], ndc=1, row0=dc * 128,
                           store=True)

        @block.vector
        def _(v):
            for ch in range(NCH):
                v.wait_ge(sem[f"ld4p{ch % 2}"], LD4C * (ch // 2 + 1))
                for dc in range(NDC):
                    g_ = ch * NDC + dc
                    if g_ >= 4:
                        v.wait_ge(sem[f"sy{g_ % 4}"], 16 * (g_ // 4))
                    v.tensor_tensor(dlu4[:, :],
                                    dd4[ch % 2][:, dc * TC:(dc + 1) * TC],
                                    du4[ch % 2][:, dc * TC:(dc + 1) * TC],
                                    ALU.mult)
                    v.drain()
                    v.tensor_tensor(
                        ball[0][:, :],
                        bass.AP(dlu4, 0, [[TC, 128], [0, NST], [1, TC]]),
                        dB4[ch % 2][:, :], ALU.mult)
                    v.drain()
                    v.wait_ge(sem["p4a"], g_ + 1)
                    for n in range(NST):
                        v.tensor_tensor_scan(
                            s4[:, n * TC:(n + 1) * TC],
                            abb[g_ % 2][:, n * TC:(n + 1) * TC],
                            ball[0][:, n * TC:(n + 1) * TC],
                            carry[:, dc * NST + n:dc * NST + n + 1],
                            ALU.mult, ALU.add)
                    v.drain()
                    v.tensor_copy(
                        carry[:, dc * NST:(dc + 1) * NST],
                        bass.AP(s4, TC - 1, [[NST * TC, 128], [TC, NST],
                                             [1, 1]]))
                    if dc == 0:
                        v.wait_ge(sem["ld4"], LDCC * (ch + 1))
                    v.tensor_tensor(abb[g_ % 2][:, :], s4[:, :],
                                    dC4[:, :], ALU.mult)
                    v.drain()
                    v.tensor_tensor(tr1[:, :], abb[g_ % 2][:, 0:HTC],
                                    abb[g_ % 2][:, HTC:], ALU.add)
                    dve_fence(v).then_inc(sem["p4c"], 1)
                    v.tensor_tensor(tr2[:, :], tr1[:, 0:4 * TC],
                                    tr1[:, 4 * TC:], ALU.add)
                    v.drain()
                    v.tensor_tensor(tr3[:, :], tr2[:, 0:2 * TC],
                                    tr2[:, 2 * TC:], ALU.add)
                    v.drain()
                    v.tensor_tensor(yred[:, :], tr3[:, 0:TC], tr3[:, TC:],
                                    ALU.add)
                    v.drain()
                    if v1:
                        v.scalar_tensor_tensor(
                            ybd[g_ % 4][:, :],
                            du4[ch % 2][:, dc * TC:(dc + 1) * TC],
                            dcol[:, dc:dc + 1], yred[:, :],
                            ALU.mult, ALU.add)
                    else:
                        v.tensor_copy(ybd[g_ % 4][:, :], yred[:, :])
                    dve_fence(v).then_inc(sem["p4y"], 1)
                v.drain().then_inc(sem["vdone"], 1)
            if not v1:
                for dc in range(NDC):
                    v.reduce_sum(rft[:, :], carry[:, dc * NST:(dc + 1) * NST],
                                 axis=AX)
                    v.drain()
                    v.tensor_scalar(rf_sb[:, dc:dc + 1], rft[:, :],
                                    dcol[:, dc:dc + 1], None, ALU.mult)
                    v.drain()

        p4.close()
        nc.leave_named_scope(pfx + "p4", _sid, False)

        _sid = nc.enter_named_scope(pfx + "p5", False)[0]
        p5 = ExitStack()
        wout_sb = p5.enter_context(nc.sbuf_tensor(pfx + "wout_sb", [128, NDC * D],
                                                  BF16))
        y5 = p5.enter_context(nc.sbuf_tensor(pfx + "y5", [128, NDC * T5], F32))
        g5 = p5.enter_context(nc.sbuf_tensor(pfx + "g5", [128, NDC * T5], F32))
        ygb = p5.enter_context(nc.sbuf_tensor(pfx + "ygb", [128, NDC * T5], BF16))
        tmp5 = p5.enter_context(nc.sbuf_tensor(pfx + "tmp5", [128, T5], F32))
        tmp5b = p5.enter_context(nc.sbuf_tensor(pfx + "tmp5b", [128, T5], F32))
        X5 = [p5.enter_context(nc.sbuf_tensor(pfx + f"X5_{j}", [128, D], F32))
              for j in range(4)]
        hid5 = [p5.enter_context(nc.sbuf_tensor(pfx + f"hid5_{j}", [128, D], F32))
                for j in range(4)] if v1 else None
        xo5 = [p5.enter_context(nc.sbuf_tensor(pfx + f"xo5_{j}", [128, D], F32))
               for j in range(4)]
        pm5 = [p5.enter_context(nc.psum_tensor(pfx + f"pm5_{j}", [128, T5], F32))
               for j in range(2)]
        outb_b = (p5.enter_context(nc.sbuf_tensor(pfx + "outb_b", [128, D],
                                                  F32)) if v1 else None)

        LD5A = 16 * 2 * NDC
        LD5B = 16 * (8 if v1 else 4)
        NLDW5 = 16 * (NDC + (1 if v1 else 0))

        @block.gpsimd
        def _(g):
            for p_ in range(4):
                g.wait_ge(sem[f"sy{p_}"], 16 * (NCH * NDC // 4))
            if v1:
                g.dma_start(out=outb_b[:, :], in_=bcast_row(out_b, 0, D)
                            ).then_inc(sem["ldw5"], 16)
            for dc in range(NDC):
                g.dma_start(out=wout_sb[:, dc * D:(dc + 1) * D],
                            in_=w_out[dc * 128:(dc + 1) * 128, :]).then_inc(
                                sem["ldw5"], 16)

        @block.sync
        def _(s):
            for c5 in range(NC5):
                if c5 == 0:
                    for p_ in range(4):
                        s.wait_ge(sem[f"sy{p_}"], 16 * (NCH * NDC // 4))
                for dc in range(NDC):
                    if c5 >= 1:
                        s.wait_ge(sem["ygr"], NDC * (c5 - 1) + dc + 1)
                    fm_dma(s, y_h, c5 * T5, T5, y5[:, dc * T5:(dc + 1) * T5],
                           T5, sem["ld5a"], ndc=1, row0=dc * 128)
                    fm_dma(s, res_h, c5 * T5, T5,
                           g5[:, dc * T5:(dc + 1) * T5], T5, sem["ld5a"],
                           ndc=1, row0=dc * 128)
                for q in range(4):
                    i = 4 * c5 + q
                    if c5 >= 1:
                        s.wait_ge(sem["ep5"], 4 * (c5 - 1) + q + 1)
                    s.dma_start(out=X5[q][:, :],
                                in_=x_in[i * 128:(i + 1) * 128, :]).then_inc(
                                    sem["ld5b"], 16)
                    if v1:
                        s.dma_start(out=hid5[q][:, :],
                                    in_=hid_h[i * 128:(i + 1) * 128, :]
                                    ).then_inc(sem["ld5b"], 16)
                if c5 >= 1:
                    for q in range(4):
                        s.wait_ge(sem["ep5"], 4 * (c5 - 1) + q + 1)
                        i = 4 * (c5 - 1) + q
                        s.dma_start(out=x_out[i * 128:(i + 1) * 128, :],
                                    in_=xo5[q][:, :]).then_inc(sem["so5"], 16)
            for q in range(4):
                s.wait_ge(sem["ep5"], 4 * (NC5 - 1) + q + 1)
                i = 4 * (NC5 - 1) + q
                s.dma_start(out=x_out[i * 128:(i + 1) * 128, :],
                            in_=xo5[q][:, :]).then_inc(sem["so5"], 16)
            s.wait_ge(sem["so5"], 16 * NT128)

        @block.scalar
        def _(sc):
            for c5 in range(NC5):
                sc.wait_ge(sem["ld5a"], LD5A * (c5 + 1))
                for dc in range(NDC):
                    sc.activation(g5[:, dc * T5:(dc + 1) * T5],
                                  g5[:, dc * T5:(dc + 1) * T5], AF.Silu)
                    sc.drain().then_inc(sem["s5a"], 1)

        @block.vector
        def _(v):
            for c5 in range(NC5):
                for dc in range(NDC):
                    v.wait_ge(sem["s5a"], NDC * c5 + dc + 1)
                    if v1:
                        v.tensor_tensor(ygb[:, dc * T5:(dc + 1) * T5],
                                        y5[:, dc * T5:(dc + 1) * T5],
                                        g5[:, dc * T5:(dc + 1) * T5], ALU.mult)
                    else:
                        tb = tmp5 if dc % 2 == 0 else tmp5b
                        v.tensor_scalar(tb[:, :],
                                        y5[:, dc * T5:(dc + 1) * T5],
                                        rf_sb[:, dc:dc + 1], None, ALU.add)
                        v.drain()
                        v.tensor_tensor(ygb[:, dc * T5:(dc + 1) * T5],
                                        tb[:, :],
                                        g5[:, dc * T5:(dc + 1) * T5], ALU.mult)
                    v.drain().then_inc(sem["ygr"], 1)
                dve_fence(v)
                v.wait_ge(sem["ld5b"], LD5B * (c5 + 1))
                if c5 >= 1:
                    v.wait_ge(sem["so5"], 16 * 4 * c5)
                for q in range(4):
                    for oh in range(2):
                        e = 8 * c5 + 2 * q + oh
                        v.wait_ge(sem["ymm"], e + 1)
                        pm = pm5[(2 * q + oh) % 2]
                        if v1:
                            v.tensor_tensor(tmp5[:, :], pm[:, :],
                                            X5[q][:, oh * 512:(oh + 1) * 512],
                                            ALU.add)
                            v.tensor_tensor(tmp5b[:, :],
                                            hid5[q][:, oh * 512:(oh + 1) * 512],
                                            outb_b[:, oh * 512:(oh + 1) * 512],
                                            ALU.add)
                            v.drain()
                            v.tensor_tensor(
                                xo5[q][:, oh * 512:(oh + 1) * 512], tmp5[:, :],
                                tmp5b[:, :], ALU.add)
                        else:
                            v.tensor_tensor(xo5[q][:, oh * 512:(oh + 1) * 512],
                                            pm[:, :],
                                            X5[q][:, oh * 512:(oh + 1) * 512],
                                            ALU.add)
                        v.drain().then_inc(sem["p5e"], 1)
                    dve_fence(v).then_inc(sem["ep5"], 1)

        @block.tensor
        def _(te):
            te.wait_ge(sem["ldw5"], NLDW5)
            for c5 in range(NC5):
                te.wait_ge(sem["ygr"], NDC * (c5 + 1))
                for q in range(4):
                    for oh in range(2):
                        e = 8 * c5 + 2 * q + oh
                        if e >= 2:
                            te.wait_ge(sem["p5e"], e - 1)
                        for dc in range(NDC):
                            mm = te.matmul(
                                pm5[(2 * q + oh) % 2][:, :],
                                ygb[:, dc * T5 + q * 128:dc * T5 + (q + 1) * 128],
                                wout_sb[:, dc * D + oh * 512:
                                        dc * D + (oh + 1) * 512],
                                start=(dc == 0), stop=(dc == NDC - 1))
                        mm.then_inc(sem["ymm"], 1)

        p5.close()
        nc.leave_named_scope(pfx + "p5", _sid, False)
    es.close()
    pool.end()
    return [(sem["so5"], pool.base["so5"])]


def mamba_io_tensors(nc, pfx, v1, with_xio=True):
    DI = DIN
    io = {
        "w_proj": nc.dram_tensor(pfx + "w_proj", [D, 2 * DI], F32,
                                 kind="ExternalInput"),
        "conv_w": (nc.dram_tensor(pfx + "conv_w", [DI, 4], F32,
                                  kind="ExternalInput") if v1 else
                   nc.dram_tensor(pfx + "conv_w", [4, DI, DI], F32,
                                  kind="ExternalInput")),
        "conv_b": nc.dram_tensor(pfx + "conv_b", [DI], F32,
                                 kind="ExternalInput"),
        "wx_d": nc.dram_tensor(pfx + "wx_d", [DI, RDT], F32,
                               kind="ExternalInput"),
        "wx_B": nc.dram_tensor(pfx + "wx_B", [DI, NST], F32,
                               kind="ExternalInput"),
        "wx_C": nc.dram_tensor(pfx + "wx_C", [DI, NST], F32,
                               kind="ExternalInput"),
        "w_dt": nc.dram_tensor(pfx + "w_dt", [RDT, DI], F32,
                               kind="ExternalInput"),
        "b_dt": nc.dram_tensor(pfx + "b_dt", [DI], F32, kind="ExternalInput"),
        "a_log": nc.dram_tensor(pfx + "a_log", [DI, NST], F32,
                                kind="ExternalInput"),
        "d_vec": nc.dram_tensor(pfx + "d_vec", [DI], F32,
                                kind="ExternalInput"),
        "w_out": nc.dram_tensor(pfx + "w_out", [DI, D], F32,
                                kind="ExternalInput"),
    }
    if v1:
        io["ln_w"] = nc.dram_tensor(pfx + "ln_w", [D], F32,
                                    kind="ExternalInput")
        io["ln_b"] = nc.dram_tensor(pfx + "ln_b", [D], F32,
                                    kind="ExternalInput")
        io["in_b"] = nc.dram_tensor(pfx + "in_b", [2 * DI], F32,
                                    kind="ExternalInput")
        io["out_b"] = nc.dram_tensor(pfx + "out_b", [D], F32,
                                     kind="ExternalInput")
    return io


def build_mamba(v1: bool):
    nc = bass.Bass("TRN2", target_bir_lowering=False)
    io = mamba_io_tensors(nc, "", v1)
    io["x_in"] = nc.dram_tensor("x_in", [L, D], F32, kind="ExternalInput")
    io["ident"] = nc.dram_tensor("ident", [128, 128], F32,
                                 kind="ExternalInput")
    io["x_out"] = nc.dram_tensor("x_out", [L, D], F32, kind="ExternalOutput")
    pool = SemPool(nc)
    with nc.Block() as block:
        emit_mamba(nc, block, pool, "", [], io, v1)
    return nc



def emit_l1ffn(nc, block, pool, pfx, gates, io):
    es = ExitStack()
    sbt = lambda n, sh, dt: es.enter_context(nc.sbuf_tensor(pfx + n, sh, dt))
    pst = lambda n, sh, dt: es.enter_context(nc.psum_tensor(pfx + n, sh, dt))
    block = BlockWrap(block, pool, gates)
    pool.begin()
    NB = 512
    DFF = 4 * D
    x_in, x_out = io["x_in"], io["x_out"]
    w1t, w2t = io["w1t"], io["w2t"]
    lng, lnbd = io["lng"], io["lnbd"]
    rwt, rbv = io["rwt"], io["rbv"]
    rawu, rawv, gamma = io["rawu"], io["rawv"], io["gamma"]
    ident = io["ident"]

    w1_sb = [sbt(f"w1_{dc}", [128, DFF], BF16)
             for dc in range(8)]
    w2_sb = sbt("w2_sb", [128, 32 * D], BF16)
    rwt_sb = sbt("rwt_sb", [128, 8 * NB], BF16)
    ut_sb = sbt("ut_sb", [128, 8 * NB], BF16)
    vu_sb = sbt("vu_sb", [128, 4 * D], BF16)
    identb = sbt("identb", [128, 128], BF16)
    lnw_b = sbt("lnw_b", [128, D], F32)
    lnb_b = sbt("lnb_b", [128, D], F32)
    rb_b = sbt("rb_b", [128, NB], F32)
    nrm = {k: sbt("n_" + k, [128, 1], F32)
           for k in ["s", "ln", "sq", "mx", "ri"]}
    st = {"negm": sbt("s_negm", [128, 4], F32),
          "r": sbt("s_r", [128, 4], F32),
          "mv": sbt("s_mv", [128, 8], F32),
          "stat12": sbt("s_stat12", [128, 12], F32),
          "lnv": sbt("s_lnv", [128, 1], F32),
          "eps": sbt("s_eps", [128, 1], F32)}
    tk = {k: sbt("t_" + k, [128, 1], F32)
          for k in ["ssum", "ssc", "e2", "den", "rd", "rS", "th", "gs", "sp"]}
    mx8 = sbt("mx8", [128, 8], F32)

    xt = [sbt(f"xt{j}", [128, D], F32) for j in range(2)]
    xoh = [sbt(f"xoh{j}", [128, 512], F32) for j in range(2)]
    xnf = sbt("xnf", [128, D], F32)
    hnt = sbt("hnt", [128, D], F32)
    xnb = sbt("xnb", [128, D], BF16)
    ln2b = sbt("ln2b", [128, D], BF16)
    xfmb = sbt("xfmb", [128, 8 * 256], BF16)
    lfmb = sbt("lfmb", [128, 8 * 256], BF16)
    s1c = sbt("s1c", [128, 512], BF16)
    hwfm = sbt("hwfm", [128, 1024], BF16)
    hsb = sbt("hsb", [128, NB], F32)
    hwb = [sbt(f"hwb{j}", [128, NB], BF16) for j in range(2)]
    rc = sbt("rc", [128, NB], F32)
    spl = sbt("spl", [128, NB], F32)

    ptr = pst("ptr", [128, 1024], BF16)
    psc = pst("psc", [128, NB], F32)
    phh = pst("phh", [128, NB], F32)
    ps1 = pst("ps1", [128, 512], F32)
    psS = [pst(f"psS{j}", [128, 512], F32) for j in range(4)]
    ptr3 = bass.AP(ptr, 0, [[1024, 128], [128, 8], [1, 128]])

    sem = {n: pool.get(n) for n in [
        "ldi", "ldc", "ldw", "ldu", "unr", "sqr", "rir", "ubr", "utr",
        "uev", "vev", "ldx0", "ldx1", "mvr", "a2v", "xnr", "tr1", "xev",
        "mmsc", "scf", "mmh", "hev", "rcr", "spa", "ssr", "e2r", "hwr",
        "mm1", "gel", "s2u", "mmS", "epq", "gso0", "gso1", "lde0",
        "lde1", "hwt"]}
    NGR = NT128 // 2

    if True:
        @block.gpsimd
        def _(g):
            g.dma_start(out=identb[:, :], in_=ident[:, :]).then_inc(
                sem["ldi"], 16)
            g.dma_start(out=lnw_b[:, :], in_=bcast_row(lng, 0, D)).then_inc(
                sem["ldc"], 16)
            g.dma_start(out=lnb_b[:, :], in_=bcast_row(lnbd, 0, D)).then_inc(
                sem["ldc"], 16)
            g.dma_start(out=rb_b[:, :], in_=bcast_row(rbv, 0, NB)).then_inc(
                sem["ldc"], 16)
            g.dma_start(out=hnt[:, :], in_=bcast_row(gamma, 0, D)).then_inc(
                sem["ldc"], 16)
            for dc in range(8):
                g.dma_start(out=w1_sb[dc][:, :],
                            in_=w1t[dc * 128:(dc + 1) * 128, :]).then_inc(
                                sem["ldw"], 16)
            for hc in range(32):
                g.dma_start(out=w2_sb[:, hc * D:(hc + 1) * D],
                            in_=w2t[hc * 128:(hc + 1) * 128, :]).then_inc(
                                sem["ldw"], 16)
            for dc in range(8):
                g.dma_start(out=rwt_sb[:, dc * NB:(dc + 1) * NB],
                            in_=rwt[dc * 128:(dc + 1) * 128, :]).then_inc(
                                sem["ldw"], 16)
            for k in range(8):
                if 1 <= k <= 4:
                    g.wait_ge(sem["ubr"], k)
                elif k >= 5:
                    g.wait_ge(sem["vev"], k - 4)
                src = rawu if k < 4 else rawv
                uc = k % 4
                g.dma_start(out=xt[0][:, :],
                            in_=src[uc * 128:(uc + 1) * 128, :]).then_inc(
                                sem["ldu"], 16)

        @block.vector
        def _(v):
            v.memset(st["eps"][:, :], 1e-5)
            v.drain()
            for k in range(8):
                v.wait_ge(sem["ldu"], 16 * (k + 1))
                v.tensor_tensor(xnf[:, :], xt[0][:, :], xt[0][:, :], ALU.mult)
                v.drain()
                v.reduce_sum(nrm["s"][:, :], xnf[:, :], axis=AX)
                dve_fence(v).then_inc(sem["unr"], 1)
                v.wait_ge(sem["sqr"], k + 1)
                v.tensor_scalar(nrm["mx"][:, :], nrm["sq"][:, :], 1e-6, None,
                                ALU.max)
                v.drain()
                v.reciprocal(nrm["ri"][:, :], nrm["mx"][:, :])
                dve_fence(v).then_inc(sem["rir"], 1)
                if k < 4:
                    v.wait_ge(sem["utr"], k + 1)
                    v.tensor_copy(
                        bass.AP(ut_sb, k * 128,
                                [[8 * NB, 128], [NB, 8], [1, 128]]), ptr3)
                    dve_fence(v).then_inc(sem["uev"], 1)
                else:
                    v.wait_ge(sem["ldc"], 64)
                    v.tensor_scalar(xnf[:, :], xt[0][:, :], nrm["ri"][:, :1],
                                    None, ALU.mult)
                    v.drain()
                    v.tensor_tensor(vu_sb[:, (k - 4) * D:(k - 3) * D],
                                    xnf[:, :], hnt[:, :], ALU.mult)
                    dve_fence(v).then_inc(sem["vev"], 1)

        @block.scalar
        def _(sc):
            for k in range(8):
                sc.wait_ge(sem["unr"], k + 1)
                sc.activation(nrm["ln"][:, :], nrm["s"][:, :], AF.Ln)
                sc.drain()
                sc.activation(nrm["sq"][:, :], nrm["ln"][:, :], AF.Exp,
                              scale=0.5)
                act_fence(sc).then_inc(sem["sqr"], 1)
                if k < 4:
                    sc.wait_ge(sem["rir"], k + 1)
                    if k >= 1:
                        sc.wait_ge(sem["utr"], k)
                    sc.activation(xnb[:, :], xt[0][:, :], AF.Identity,
                                  scale=nrm["ri"][:, :1])
                    act_fence(sc).then_inc(sem["ubr"], 1)

        @block.tensor
        def _(te):
            te.wait_ge(sem["ldi"], 16)
            for uc in range(4):
                te.wait_ge(sem["ubr"], uc + 1)
                if uc >= 1:
                    te.wait_ge(sem["uev"], uc)
                for c in range(8):
                    mm = te.transpose(ptr[:, c * 128:(c + 1) * 128],
                                      xnb[:, c * 128:(c + 1) * 128],
                                      identb[:, :])
                mm.then_inc(sem["utr"], 1)

        @block.sync
        def _(s):
            for g_ in range(NGR):
                for q in range(2):
                    i = 2 * g_ + q
                    if i == 0:
                        s.wait_ge(sem["vev"], 4)
                    if i >= 2:
                        s.wait_ge(sem["xnr"], i - 1)
                    s.dma_start(out=xt[q][:, :],
                                in_=x_in[i * 128:(i + 1) * 128, :]).then_inc(
                                    sem[f"ldx{q}"], 16)
                for j in range(4):
                    e = 4 * g_ + j
                    q2, oh = divmod(j, 2)
                    ii = 2 * g_ + q2
                    if e >= 2:
                        s.wait_ge(sem[f"gso{e % 2}"], 16 * (e // 2))
                    s.dma_start(out=xoh[e % 2][:, :],
                                in_=x_in[ii * 128:(ii + 1) * 128,
                                         oh * 512:(oh + 1) * 512]).then_inc(
                                    sem[f"lde{e % 2}"], 16)
            s.wait_ge(sem["gso0"], 16 * 32)
            s.wait_ge(sem["gso1"], 16 * 32)

        @block.gpsimd
        def _(g):
            for e in range(4 * NGR):
                g_, j = divmod(e, 4)
                q2, oh = divmod(j, 2)
                ii = 2 * g_ + q2
                g.wait_ge(sem["epq"], e + 1)
                g.dma_start(out=x_out[ii * 128:(ii + 1) * 128,
                                      oh * 512:(oh + 1) * 512],
                            in_=xoh[e % 2][:, :]).then_inc(
                                sem[f"gso{e % 2}"], 16)

        @block.vector
        def _(v):
            v.wait_ge(sem["utr"], 4)
            for i in range(NT128):
                g_, q = divmod(i, 2)
                v.wait_ge(sem[f"ldx{q}"], 16 * (g_ + 1))
                if i >= 2:
                    v.wait_ge(sem["a2v"], 2 * i - 2)
                emit_ln_stats_dve(v, xt[q], st, j=q)
                dve_fence(v).then_inc(sem["mvr"], 1)
                v.wait_ge(sem["a2v"], 2 * i + 1)
                v.tensor_scalar(xnf[:, :], xt[q][:, :],
                                st["negm"][:, q:q + 1], st["r"][:, q:q + 1],
                                ALU.add, ALU.mult)
                v.drain()
                v.tensor_copy(xnb[:, :], xnf[:, :])
                emit_ln_stats_dve(v, xnf, st, j=2 + q)
                dve_fence(v).then_inc(sem["mvr"], 1)
                v.wait_ge(sem["a2v"], 2 * i + 2)
                v.tensor_scalar(hnt[:, :], xnf[:, :],
                                st["negm"][:, 2 + q:3 + q],
                                st["r"][:, 2 + q:3 + q], ALU.add, ALU.mult)
                v.drain()
                v.tensor_tensor(xnf[:, :], hnt[:, :], lnw_b[:, :], ALU.mult)
                v.drain()
                v.tensor_tensor(ln2b[:, :], xnf[:, :], lnb_b[:, :], ALU.add)
                dve_fence(v).then_inc(sem["xnr"], 1)
                v.wait_ge(sem["tr1"], 2 * i + 1)
                v.tensor_copy(
                    bass.AP(xfmb, q * 128, [[2048, 128], [256, 8], [1, 128]]),
                    ptr3)
                dve_fence(v).then_inc(sem["xev"], 1)
                v.wait_ge(sem["tr1"], 2 * i + 2)
                v.tensor_copy(
                    bass.AP(lfmb, q * 128, [[2048, 128], [256, 8], [1, 128]]),
                    ptr3)
                dve_fence(v).then_inc(sem["xev"], 1)
                v.wait_ge(sem["mmsc"], i + 1)
                v.tensor_tensor(spl[:, :], psc[:, :], rb_b[:, :], ALU.add)
                dve_fence(v).then_inc(sem["scf"], 1)
                v.tensor_scalar(rc[:, :], spl[:, :], 10.0, -10.0,
                                ALU.min, ALU.max)
                dve_fence(v).then_inc(sem["rcr"], 1)
                v.wait_ge(sem["spa"], i + 1)
                v.max(mx8[:, :], spl[:, :])
                v.drain()
                v.reduce_sum(tk["ssum"][:, :], mx8[:, :], axis=AX)
                v.drain()
                v.tensor_scalar(tk["ssc"][:, :], tk["ssum"][:, :], 20.0, None,
                                ALU.min)
                dve_fence(v).then_inc(sem["ssr"], 1)
                v.wait_ge(sem["e2r"], i + 1)
                v.tensor_scalar(tk["den"][:, :], tk["e2"][:, :], 1.0, None,
                                ALU.add)
                v.tensor_scalar(tk["sp"][:, :], tk["ssum"][:, :], 1e-6, None,
                                ALU.add)
                v.drain()
                v.reciprocal(tk["rd"][:, :], tk["den"][:, :])
                v.reciprocal(tk["rS"][:, :], tk["sp"][:, :])
                v.drain()
                v.tensor_scalar(tk["th"][:, :], tk["rd"][:, :], -2.0, 1.0,
                                ALU.mult, ALU.add)
                v.drain()
                v.tensor_tensor(tk["gs"][:, :], tk["th"][:, :],
                                tk["rS"][:, :], ALU.mult)
                v.drain()
                v.tensor_scalar(rc[:, :], spl[:, :], mx8[:, 7:8], None,
                                ALU.is_ge)
                v.drain()
                v.tensor_tensor(rc[:, :], rc[:, :], spl[:, :], ALU.mult)
                v.drain()
                v.tensor_scalar(rc[:, :], rc[:, :], tk["gs"][:, :1], None,
                                ALU.mult)
                v.drain()
                v.wait_ge(sem["hev"], i + 1)
                v.tensor_tensor(hwb[q][:, :], hsb[:, :], rc[:, :], ALU.mult)
                dve_fence(v).then_inc(sem["hwr"], 1)
                if q == 1:
                    for qq in range(2):
                        v.wait_ge(sem["hwt"], 2 * g_ + qq + 1)
                        v.tensor_copy(hwfm[:, qq * 512:(qq + 1) * 512],
                                      ptr[:, 0:512])
                        dve_fence(v).then_inc(sem["xev"], 1)
                    for j in range(4):
                        q2, oh = divmod(j, 2)
                        e = 4 * g_ + j
                        v.wait_ge(sem["mmS"], e + 1)
                        v.wait_ge(sem[f"lde{e % 2}"], 16 * (e // 2 + 1))
                        v.scalar_tensor_tensor(
                            xoh[e % 2][:, :], psS[j][:, :], 0.5,
                            xoh[e % 2][:, :], ALU.mult, ALU.add)
                        dve_fence(v).then_inc(sem["epq"], 1)

        @block.scalar
        def _(sc):
            for i in range(NT128):
                g_, q = divmod(i, 2)
                sc.wait_ge(sem["mvr"], 2 * i + 1)
                emit_ln_stats_act(sc, st, j=q)
                act_fence(sc).then_inc(sem["a2v"], 1)
                sc.wait_ge(sem["mvr"], 2 * i + 2)
                emit_ln_stats_act(sc, st, j=2 + q)
                act_fence(sc).then_inc(sem["a2v"], 1)
                sc.wait_ge(sem["rcr"], i + 1)
                sc.activation(spl[:, :], rc[:, :], AF.Exp)
                sc.drain()
                sc.activation(spl[:, :], spl[:, :], AF.Ln, bias=1.0)
                act_fence(sc).then_inc(sem["spa"], 1)
                sc.wait_ge(sem["mmh"], i + 1)
                if i >= 1:
                    sc.wait_ge(sem["hwr"], i)
                sc.activation(hsb[:, :], phh[:, :], AF.Copy)
                act_fence(sc).then_inc(sem["hev"], 1)
                sc.wait_ge(sem["ssr"], i + 1)
                sc.activation(tk["e2"][:, :], tk["ssc"][:, :], AF.Exp,
                              scale=2.0)
                act_fence(sc).then_inc(sem["e2r"], 1)
                if q == 1:
                    for p in range(16):
                        sc.wait_ge(sem["mm1"], 32 * g_ + 2 * p + 2)
                        if 16 * g_ + p >= 1:
                            sc.wait_ge(sem["s2u"], 32 * g_ + 2 * p)
                        sc.activation(s1c[:, :], ps1[:, :], AF.Gelu)
                        act_fence(sc).then_inc(sem["gel"], 1)

        @block.tensor
        def _(te):
            te.wait_ge(sem["ldw"], 16 * 48)
            te.wait_ge(sem["uev"], 4)
            te.wait_ge(sem["vev"], 4)
            for i in range(NT128):
                g_, q = divmod(i, 2)
                te.wait_ge(sem["xnr"], i + 1)
                if i > 0:
                    te.wait_ge(sem["xev"], 6 * g_ + 2 * q)
                for c in range(8):
                    mm = te.transpose(ptr[:, c * 128:(c + 1) * 128],
                                      xnb[:, c * 128:(c + 1) * 128],
                                      identb[:, :])
                mm.then_inc(sem["tr1"], 1)
                te.wait_ge(sem["xev"], 6 * g_ + 2 * q + 1)
                for c in range(8):
                    mm = te.transpose(ptr[:, c * 128:(c + 1) * 128],
                                      ln2b[:, c * 128:(c + 1) * 128],
                                      identb[:, :])
                mm.then_inc(sem["tr1"], 1)
                te.wait_ge(sem["xev"], 6 * g_ + 2 * q + 2)
                if i >= 1:
                    te.wait_ge(sem["scf"], i)
                for dc in range(8):
                    mm = te.matmul(psc[:, :],
                                   lfmb[:, dc * 256 + q * 128:
                                        dc * 256 + (q + 1) * 128],
                                   rwt_sb[:, dc * NB:(dc + 1) * NB],
                                   start=(dc == 0), stop=(dc == 7))
                mm.then_inc(sem["mmsc"], 1)
                te.wait_ge(sem["hev"], i)
                for dc in range(8):
                    mm = te.matmul(phh[:, :],
                                   xfmb[:, dc * 256 + q * 128:
                                        dc * 256 + (q + 1) * 128],
                                   ut_sb[:, dc * NB:(dc + 1) * NB],
                                   start=(dc == 0), stop=(dc == 7))
                mm.then_inc(sem["mmh"], 1)
                if q == 1:
                    if g_ >= 1:
                        te.wait_ge(sem["epq"], 4 * g_)
                    for oc in range(33):
                        if oc < 32:
                            if oc >= 2:
                                te.wait_ge(sem["gel"], 16 * g_ + oc // 2)
                            for dc in range(8):
                                mm = te.matmul(
                                    ps1[:, (oc % 2) * 256:(oc % 2 + 1) * 256],
                                    w1_sb[dc][:, oc * 128:(oc + 1) * 128],
                                    xfmb[:, dc * 256:(dc + 1) * 256],
                                    start=(dc == 0), stop=(dc == 7))
                            mm.then_inc(sem["mm1"], 1)
                        if oc >= 1:
                            o2 = oc - 1
                            te.wait_ge(sem["gel"], 16 * g_ + o2 // 2 + 1)
                            for j in range(4):
                                q2, oh = divmod(j, 2)
                                mm = te.matmul(
                                    psS[j][:, :],
                                    s1c[:, (o2 % 2) * 256 + q2 * 128:
                                        (o2 % 2) * 256 + (q2 + 1) * 128],
                                    w2_sb[:, o2 * D + oh * 512:
                                          o2 * D + (oh + 1) * 512],
                                    start=(o2 == 0), stop=False)
                            mm.then_inc(sem["s2u"], 1)
                    for qq in range(2):
                        te.wait_ge(sem["hwr"], 2 * g_ + qq + 1)
                        te.wait_ge(sem["xev"], 6 * g_ + 4 + qq)
                        for c in range(4):
                            mm = te.transpose(ptr[:, c * 128:(c + 1) * 128],
                                              hwb[qq][:, c * 128:
                                                      (c + 1) * 128],
                                              identb[:, :])
                        mm.then_inc(sem["hwt"], 1)
                    te.wait_ge(sem["xev"], 6 * g_ + 6)
                    for j in range(4):
                        q2, oh = divmod(j, 2)
                        for vc in range(4):
                            mm = te.matmul(
                                psS[j][:, :],
                                hwfm[:, q2 * 512 + vc * 128:
                                     q2 * 512 + (vc + 1) * 128],
                                vu_sb[:, vc * D + oh * 512:
                                      vc * D + (oh + 1) * 512],
                                start=False, stop=(vc == 3))
                        mm.then_inc(sem["mmS"], 1)

    es.close()
    pool.end()
    return [(sem["gso0"], pool.base["gso0"]),
            (sem["gso1"], pool.base["gso1"])]


def l1ffn_io_tensors(nc, pfx):
    NB = 512
    DFF = 4 * D
    return {
        "w1t": nc.dram_tensor(pfx + "w1t", [D, DFF], F32,
                              kind="ExternalInput"),
        "w2t": nc.dram_tensor(pfx + "w2t", [DFF, D], F32,
                              kind="ExternalInput"),
        "lng": nc.dram_tensor(pfx + "lng", [D], F32, kind="ExternalInput"),
        "lnbd": nc.dram_tensor(pfx + "lnbd", [D], F32, kind="ExternalInput"),
        "rwt": nc.dram_tensor(pfx + "rwt", [D, NB], F32,
                              kind="ExternalInput"),
        "rbv": nc.dram_tensor(pfx + "rbv", [NB], F32, kind="ExternalInput"),
        "rawu": nc.dram_tensor(pfx + "rawu", [NB, D], F32,
                               kind="ExternalInput"),
        "rawv": nc.dram_tensor(pfx + "rawv", [NB, D], F32,
                               kind="ExternalInput"),
        "gamma": nc.dram_tensor(pfx + "gamma", [D], F32,
                                kind="ExternalInput"),
    }


def build_l1ffn():
    nc = bass.Bass("TRN2", target_bir_lowering=False)
    io = l1ffn_io_tensors(nc, "")
    io["x_in"] = nc.dram_tensor("x_in", [L, D], F32, kind="ExternalInput")
    io["ident"] = nc.dram_tensor("ident", [128, 128], F32,
                                 kind="ExternalInput")
    io["x_out"] = nc.dram_tensor("x_out", [L, D], F32, kind="ExternalOutput")
    pool = SemPool(nc)
    with nc.Block() as block:
        emit_l1ffn(nc, block, pool, "", [], io)
    return nc



def build_full():
    nc = bass.Bass("TRN2", target_bir_lowering=False)
    x_in = nc.dram_tensor("x_in", [L, D], F32, kind="ExternalInput")
    ident = nc.dram_tensor("ident", [128, 128], F32, kind="ExternalInput")
    x_out = nc.dram_tensor("x_out", [L, D], F32, kind="ExternalOutput")
    w0 = nc.dram_tensor("f0_w", [D, D], F32, kind="ExternalInput")
    b0 = nc.dram_tensor("f0_b", [D], F32, kind="ExternalInput")
    m0 = mamba_io_tensors(nc, "m0_", False)
    f1 = l1ffn_io_tensors(nc, "f1_")
    m1 = mamba_io_tensors(nc, "m1_", True)
    xh = [nc.dram_tensor(f"xh{j}", [L, D], F32) for j in range(5)]
    pool = SemPool(nc)
    with nc.Block() as block:
        with nc.named_scope("A_l0ffn"):
            g = emit_l0ffn(nc, block, pool, "A", [],
                           {"x_in": x_in, "x_out": xh[0], "w": w0, "b": b0,
                            "ident": ident})
        with nc.named_scope("B_mamba0"):
            g = emit_mamba(nc, block, pool, "B", g,
                           {**m0, "x_in": xh[0], "x_out": xh[1],
                            "ident": ident}, v1=False)
        with nc.named_scope("C_l0ffn"):
            g = emit_l0ffn(nc, block, pool, "C", g,
                           {"x_in": xh[1], "x_out": xh[2], "w": w0, "b": b0,
                            "ident": ident})
        with nc.named_scope("D_l1ffn"):
            g = emit_l1ffn(nc, block, pool, "D", g,
                           {**f1, "x_in": xh[2], "x_out": xh[3],
                            "ident": ident})
        with nc.named_scope("E_mamba1"):
            g = emit_mamba(nc, block, pool, "E", g,
                           {**m1, "x_in": xh[3], "x_out": xh[4],
                            "ident": ident}, v1=True)
        with nc.named_scope("F_l1ffn"):
            g = emit_l1ffn(nc, block, pool, "F", g,
                           {**f1, "x_in": xh[4], "x_out": x_out,
                            "ident": ident})
    return nc


def full_in_map(x, inp):
    m = {"x_in": np.ascontiguousarray(x), "ident": IDENT_NP,
         "f0_w": np.ascontiguousarray(inp["L0_ffn_weight"]),
         "f0_b": np.ascontiguousarray(inp["L0_ffn_bias"])}
    for pfx, sub in (("m0_", mamba_in_map_l0(x, inp)),
                     ("m1_", mamba_in_map_l1(x, inp)),
                     ("f1_", l1ffn_in_map(x, inp))):
        for k, v in sub.items():
            if k not in ("x_in", "ident"):
                m[pfx + k] = v
    return m



_NC_CACHE = {}


def _get_nc(name, builder):
    if name not in _NC_CACHE:
        _NC_CACHE[name] = builder()
    return _NC_CACHE[name]


def run_l0ffn(x_cores, w, b):
    nc = _get_nc("l0ffn", build_l0ffn)
    in_maps = [{"x_in": np.ascontiguousarray(x), "w": w, "b": b,
                "ident": IDENT_NP} for x in x_cores]
    res = run_bass_kernel_spmd(nc, in_maps, core_ids=list(range(len(x_cores))))
    return [np.asarray(r["x_out"]) for r in res.results]


def mamba_in_map_l0(x, inp):
    return {
        "x_in": np.ascontiguousarray(x),
        "w_proj": inp["L0_attn_W_in"],
        "conv_w": inp["L0_attn_conv_w"],
        "conv_b": inp["L0_attn_conv_b"],
        "wx_d": np.ascontiguousarray(inp["L0_attn_W_x"][:, :RDT]),
        "wx_B": np.ascontiguousarray(inp["L0_attn_W_x"][:, RDT:RDT + NST]),
        "wx_C": np.ascontiguousarray(inp["L0_attn_W_x"][:, RDT + NST:]),
        "w_dt": inp["L0_attn_W_dt"],
        "b_dt": inp["L0_attn_b_dt"],
        "a_log": inp["L0_attn_A_log"],
        "d_vec": inp["L0_attn_D"],
        "w_out": inp["L0_attn_W_out"],
        "ident": IDENT_NP,
    }


def mamba_in_map_l1(x, inp):
    xp = np.ascontiguousarray(inp["L1_attn_xp_w"].T)
    return {
        "x_in": np.ascontiguousarray(x),
        "w_proj": np.ascontiguousarray(inp["L1_attn_in_w"].T),
        "conv_w": np.ascontiguousarray(inp["L1_attn_conv_w"].reshape(DIN, 4)),
        "conv_b": inp["L1_attn_conv_b"],
        "wx_d": np.ascontiguousarray(xp[:, :RDT]),
        "wx_B": np.ascontiguousarray(xp[:, RDT:RDT + NST]),
        "wx_C": np.ascontiguousarray(xp[:, RDT + NST:]),
        "w_dt": np.ascontiguousarray(inp["L1_attn_dtp_w"].T),
        "b_dt": inp["L1_attn_dtp_b"],
        "a_log": inp["L1_attn_A_log"],
        "d_vec": inp["L1_attn_D"],
        "w_out": np.ascontiguousarray(inp["L1_attn_out_w"].T),
        "ln_w": inp["L1_attn_ln_w"],
        "ln_b": inp["L1_attn_ln_b"],
        "in_b": inp["L1_attn_in_b"],
        "out_b": inp["L1_attn_out_b"],
        "ident": IDENT_NP,
    }


def l1ffn_in_map(x, inp):
    return {
        "x_in": np.ascontiguousarray(x),
        "w1t": np.ascontiguousarray(inp["L1_ffn_W1"].T),
        "w2t": np.ascontiguousarray(inp["L1_ffn_W2"].T),
        "lng": np.ascontiguousarray(inp["L1_ffn_ln_g"]).reshape(-1),
        "lnbd": np.ascontiguousarray(inp["L1_ffn_ln_b"]).reshape(-1),
        "rwt": np.ascontiguousarray(inp["L1_ffn_router_W"].T),
        "rbv": np.ascontiguousarray(inp["L1_ffn_router_b"]).reshape(-1),
        "rawu": np.ascontiguousarray(inp["L1_ffn_raw_U"]),
        "rawv": np.ascontiguousarray(inp["L1_ffn_raw_V"]),
        "gamma": np.ascontiguousarray(inp["L1_ffn_gamma"]).reshape(-1),
        "ident": IDENT_NP,
    }


def run_l1ffn(x_cores, inp):
    nc = _get_nc("l1ffn", build_l1ffn)
    in_maps = [l1ffn_in_map(x, inp) for x in x_cores]
    res = run_bass_kernel_spmd(nc, in_maps, core_ids=list(range(len(x_cores))))
    return [np.asarray(r["x_out"]) for r in res.results]


def run_mamba(x_cores, inp, v1):
    nc = _get_nc("mamba1" if v1 else "mamba0",
                 lambda: build_mamba(v1))
    mk = mamba_in_map_l1 if v1 else mamba_in_map_l0
    in_maps = [mk(x, inp) for x in x_cores]
    res = run_bass_kernel_spmd(nc, in_maps, core_ids=list(range(len(x_cores))))
    return [np.asarray(r["x_out"]) for r in res.results]


def kernel(**inputs):
    x = np.asarray(inputs["L0_attn_x"], dtype=np.float32)
    nc = _get_nc("full", build_full)
    in_maps = [full_in_map(np.ascontiguousarray(x[c]), inputs)
               for c in range(NCORE)]
    res = run_bass_kernel_spmd(nc, in_maps, core_ids=list(range(NCORE)))
    return np.stack([np.asarray(r["x_out"]) for r in res.results], axis=0)


def kernel_unfused(**inputs):
    x = np.asarray(inputs["L0_attn_x"], dtype=np.float32)
    x_cores = [np.ascontiguousarray(x[c]) for c in range(NCORE)]
    w, b = inputs["L0_ffn_weight"], inputs["L0_ffn_bias"]
    x_cores = run_l0ffn(x_cores, w, b)
    x_cores = run_mamba(x_cores, inputs, v1=False)
    x_cores = run_l0ffn(x_cores, w, b)
    x_cores = run_l1ffn(x_cores, inputs)
    x_cores = run_mamba(x_cores, inputs, v1=True)
    x_cores = run_l1ffn(x_cores, inputs)
    return np.stack(x_cores, axis=0)


# version 19
# speedup vs baseline: 1.1899x; 1.0059x over previous
import numpy as np
import concourse.bass as bass
import concourse.mybir as mybir
from concourse.bass_utils import run_bass_kernel_spmd

F32 = mybir.dt.float32
BF16 = mybir.dt.bfloat16
AF = mybir.ActivationFunctionType
ALU = mybir.AluOpType
AX = mybir.AxisListType.X

L = 4096
D = 1024
DIN = 2048
NST = 16
RDT = 64
MAGIC = 12582912.0
NT128 = L // 128
NCORE = 8

IDENT_NP = np.eye(128, dtype=np.float32)


def bcast_row(dram, row, n, part=128):
    return bass.AP(dram, row * n, [[0, part], [1, n]])



def emit_quant_dve(vector, src, dst_bf16, t, nblk=32, blk=32):
    v3 = lambda h: h[:, :].rearrange("p (a c) -> p a c", c=blk)
    sc_b = lambda h: bass.AP(h, 0, [[nblk, 128], [1, nblk], [0, blk]])
    V = vector
    V.reduce_max(t["sc0"][:, :], v3(src), axis=AX, apply_absolute_value=True)
    V.drain()
    V.tensor_scalar(t["sc"][:, :], t["sc0"][:, :], 1.0 / 7.5, 1e-30,
                    ALU.mult, ALU.max)
    V.drain()
    V.reciprocal(t["si"][:, :], t["sc"][:, :])
    V.drain()
    V.tensor_tensor(v3(t["v"]), v3(src), sc_b(t["si"]), ALU.mult)
    V.drain()
    V.tensor_tensor(t["q4"][:, :], t["v"][:, :], t["v"][:, :], ALU.mult)
    V.drain()
    V.tensor_scalar(t["g2"][:, :], t["q4"][:, :], 4.0, None, ALU.is_ge)
    V.tensor_scalar(t["g4"][:, :], t["q4"][:, :], 16.0, None, ALU.is_ge)
    V.drain()
    V.tensor_scalar(t["w1"][:, :], t["g2"][:, :], -4.0, 8.0, ALU.mult, ALU.add)
    V.tensor_scalar(t["sn"][:, :], t["g2"][:, :], -0.125, -0.125,
                    ALU.mult, ALU.add)
    V.drain()
    V.scalar_tensor_tensor(t["w1"][:, :], t["g4"][:, :], -2.0, t["w1"][:, :],
                           ALU.mult, ALU.add)
    V.scalar_tensor_tensor(t["sn"][:, :], t["g4"][:, :], -0.25, t["sn"][:, :],
                           ALU.mult, ALU.add)
    V.drain()
    V.tensor_tensor(t["tt"][:, :], t["v"][:, :], t["w1"][:, :], ALU.mult)
    V.drain()
    V.tensor_scalar(t["z"][:, :], t["tt"][:, :], -1.0, 0.5, ALU.mult, ALU.add)
    V.drain()
    V.tensor_scalar(t["rr"][:, :], t["z"][:, :], MAGIC, MAGIC,
                    ALU.add, ALU.subtract)
    V.drain()
    V.tensor_tensor(t["gt"][:, :], t["rr"][:, :], t["z"][:, :], ALU.is_gt)
    V.drain()
    V.tensor_tensor(t["ff"][:, :], t["rr"][:, :], t["gt"][:, :], ALU.subtract)
    V.drain()
    V.tensor_tensor(t["tt"][:, :], t["ff"][:, :], t["sn"][:, :], ALU.mult)
    V.drain()
    return V.tensor_tensor(v3(dst_bf16), v3(t["tt"]), sc_b(t["sc"]), ALU.mult)


def emit_ln_stats_dve(v, xt, st, j=0):
    F = xt.shape[1]
    h = F // 2
    v.bn_stats(st["stat12"][:, 0:6], xt[:, 0:h])
    v.bn_stats(st["stat12"][:, 6:12], xt[:, h:F])
    v.drain()
    return v.bn_aggr(st["mv"][:, 2 * j:2 * j + 2], st["stat12"][:, :])


def emit_ln_stats_act(sc, st, j=0):
    sc.activation(st["negm"][:, j:j + 1], st["mv"][:, 2 * j:2 * j + 1],
                  AF.Copy, scale=-1.0)
    sc.activation(st["lnv"][:, :], st["mv"][:, 2 * j + 1:2 * j + 2], AF.Ln,
                  bias=st["eps"][:, :1])
    sc.drain()
    return sc.activation(st["r"][:, j:j + 1], st["lnv"][:, :], AF.Exp,
                         scale=-0.5)


def alloc_quant_scratch(nc, pfx="", es=None):
    if es is None:
        mk = lambda n, sh: nc.alloc_sbuf_tensor(n, sh, F32)
    else:
        mk = lambda n, sh: es.enter_context(nc.sbuf_tensor(n, sh, F32))
    t = {k: mk(pfx + "q_" + k, [128, D])
         for k in ["v", "q4", "g2", "g4", "w1", "sn", "tt",
                   "z", "rr", "gt", "ff"]}
    for k in ["sc0", "sc", "si"]:
        t[k] = mk(pfx + "q_" + k, [128, 32])
    return t


def dve_fence(v, dummy=None):
    return v.drain()


def act_fence(sc, dummy=None):
    return sc.drain()



class SemPool:
    def __init__(self, nc):
        self.nc = nc
        self.sems = {}
        self.name_of = {}
        self.base = {}
        self.cur = {}

    def get(self, name):
        if name not in self.sems:
            s = self.nc.alloc_semaphore(name=name)
            self.sems[name] = s
            self.name_of[id(s)] = name
            self.base[name] = 0
        return self.sems[name]

    def begin(self):
        self.cur = {}

    def end(self):
        for n, v in self.cur.items():
            self.base[n] += v
        self.cur = {}

    def note_inc(self, sem, n):
        nm = self.name_of[id(sem)]
        self.cur[nm] = self.cur.get(nm, 0) + n

    def base_of(self, sem):
        return self.base[self.name_of[id(sem)]]

    def abs_total(self, name):
        return self.base.get(name, 0) + self.cur.get(name, 0)


class InstrWrap:
    def __init__(self, instr, pool):
        self._i, self._p = instr, pool

    def then_inc(self, sem, n):
        self._p.note_inc(sem, n)
        r = self._i.then_inc(sem, n)
        return InstrWrap(r, self._p) if hasattr(r, "then_inc") else r

    def __getattr__(self, k):
        return getattr(self._i, k)


class EngWrap:
    def __init__(self, eng, pool):
        self._e, self._p = eng, pool

    def wait_ge(self, sem, v):
        r = self._e.wait_ge(sem, v + self._p.base_of(sem))
        return InstrWrap(r, self._p) if hasattr(r, "then_inc") else r

    def wait_abs(self, sem, v):
        r = self._e.wait_ge(sem, v)
        return InstrWrap(r, self._p) if hasattr(r, "then_inc") else r

    def __getattr__(self, k):
        a = getattr(self._e, k)
        if not callable(a):
            return a

        def call(*args, **kw):
            r = a(*args, **kw)
            return InstrWrap(r, self._p) if hasattr(r, "then_inc") else r
        return call


class BlockWrap:
    ENGS = ("sync", "gpsimd", "vector", "scalar", "tensor")

    def __init__(self, block, pool, gates):
        self._b, self._p, self._g = block, pool, gates
        self._gated = set()

    def _mk(self, engname):
        dec = getattr(self._b, engname)
        pool, gates, gated = self._p, self._g, self._gated

        def deco(fn):
            def inner(eng):
                w = EngWrap(eng, pool)
                if engname not in gated:
                    gated.add(engname)
                    for s, v in gates:
                        w.wait_abs(s, v)
                return fn(w)
            return dec(inner)
        return deco

    @property
    def sync(self):
        return self._mk("sync")

    @property
    def gpsimd(self):
        return self._mk("gpsimd")

    @property
    def vector(self):
        return self._mk("vector")

    @property
    def scalar(self):
        return self._mk("scalar")

    @property
    def tensor(self):
        return self._mk("tensor")



DEBUG_L0FFN = False


def emit_l0ffn(nc, block, pool, pfx, gates, io):
    from contextlib import ExitStack
    es = ExitStack()
    sbt = lambda n, sh, dt: es.enter_context(nc.sbuf_tensor(pfx + n, sh, dt))
    pst = lambda n, sh, dt: es.enter_context(nc.psum_tensor(pfx + n, sh, dt))
    block = BlockWrap(block, pool, gates)
    pool.begin()
    dbg = {}
    x_in, x_out = io["x_in"], io["x_out"]
    w_in, b_in, ident = io["w"], io["b"], io["ident"]

    wr = [sbt(f"wr{j}", [128, D], F32) for j in range(2)]
    wq = sbt("wq", [128, D], BF16)
    wT = sbt("wT", [128, 8 * D], BF16)
    identb = sbt("identb", [128, 128], BF16)
    xt = [sbt(f"xt{j}", [128, D], F32) for j in range(2)]
    xn = sbt("xn", [128, D], F32)
    xq = sbt("xq", [128, D], BF16)
    xfm = sbt("xfm", [128, D], BF16)
    biasb = sbt("biasb", [128, D], F32)
    ty = sbt("ty", [128, 512], F32)
    xo = [sbt(f"xo{j}", [128, D], F32) for j in range(2)]
    t = alloc_quant_scratch(nc, pfx, es)
    st = {k: sbt("s_" + k, [128, 1], F32) for k in ["lnv", "eps"]}
    st["negm"] = sbt("s_negm", [128, 2], F32)
    st["r"] = sbt("s_r", [128, 2], F32)
    st["mv"] = sbt("s_mv", [128, 4], F32)
    st["stat12"] = sbt("s_stat12", [128, 12], F32)
    fv = sbt("fv", [128, 1], F32)
    fa = sbt("fa", [128, 1], F32)
    dbq = sbt("dbq", [128, D], F32)
    dbs = sbt("dbs", [128, 4], F32)

    ptr = pst("ptr", [128, 1024], BF16)
    py = [pst(f"py{j}", [128, 512], F32) for j in range(2)]

    sem = {n: pool.get(n) for n in
           ["ldw0", "ldw1", "ldi", "ldb", "wq_rdy", "wtr", "wev", "ldx0", "ldx1",
            "v2a", "a2v", "xq_rdy", "xtr", "xev", "mm", "ep", "so0", "so1",
            "cst",
            "dbg", "dbd", "mvr"]}

    ptr3 = bass.AP(ptr, 0, [[1024, 128], [128, 8], [1, 128]])

    if True:
        @block.gpsimd
        def _(g):
            g.dma_start(out=identb[:, :], in_=ident[:, :]).then_inc(sem["ldi"], 16)
            for k in range(8):
                if k >= 2:
                    g.wait_ge(sem["wq_rdy"], k - 1)
                g.dma_start(out=wr[k % 2][:, :],
                            in_=w_in[k * 128:(k + 1) * 128, :]).then_inc(
                                sem[f"ldw{k % 2}"], 16)

        @block.sync
        def _(s):
            s.dma_start(out=biasb[:, :], in_=bcast_row(b_in, 0, D)).then_inc(
                sem["ldb"], 16)
            for i in range(NT128):
                if i >= 2:
                    s.wait_ge(sem["ep"], i - 1)
                s.dma_start(out=xt[i % 2][:, :],
                            in_=x_in[i * 128:(i + 1) * 128, :]).then_inc(
                                sem[f"ldx{i % 2}"], 16)
                if i >= 1:
                    s.wait_ge(sem["ep"], i)
                    s.dma_start(out=x_out[(i - 1) * 128:i * 128, :],
                                in_=xo[(i - 1) % 2][:, :]).then_inc(
                                    sem[f"so{(i - 1) % 2}"], 16)
            s.wait_ge(sem["ep"], NT128)
            s.dma_start(out=x_out[(NT128 - 1) * 128:, :],
                        in_=xo[(NT128 - 1) % 2][:, :]).then_inc(
                            sem[f"so{(NT128 - 1) % 2}"], 16)
            s.wait_ge(sem["so0"], 16 * (NT128 // 2))
            s.wait_ge(sem["so1"], 16 * (NT128 // 2))

        @block.scalar
        def _(sc):
            sc.wait_ge(sem["cst"], 1)
            for i in range(NT128):
                sc.wait_ge(sem["mvr"], i + 1)
                emit_ln_stats_act(sc, st, j=i % 2)
                act_fence(sc).then_inc(sem["a2v"], 1)

        def emit_ep(v, j):
            if j == 0:
                v.wait_ge(sem["ldb"], 16)
            if j >= 2:
                v.wait_ge(sem[f"so{j % 2}"], 16 * (j // 2))
            for h in range(2):
                v.wait_ge(sem["mm"], 2 * j + h + 1)
                v.tensor_tensor(ty[:, :], py[h][:, :],
                                biasb[:, h * 512:(h + 1) * 512], ALU.add)
                v.drain()
                v.scalar_tensor_tensor(
                    xo[j % 2][:, h * 512:(h + 1) * 512], ty[:, :], 0.5,
                    xt[j % 2][:, h * 512:(h + 1) * 512], ALU.mult, ALU.add)
                v.drain()
            dve_fence(v).then_inc(sem["ep"], 1)

        @block.vector
        def _(v):
            v.memset(st["eps"][:, :], 1e-5).then_inc(sem["cst"], 1)
            for k in range(8):
                v.wait_ge(sem[f"ldw{k % 2}"], 16 * (k // 2 + 1))
                emit_quant_dve(v, wr[k % 2], wq, t)
                dve_fence(v).then_inc(sem["wq_rdy"], 1)
                v.wait_ge(sem["wtr"], k + 1)
                v.tensor_copy(
                    bass.AP(wT, k * 128, [[8 * D, 128], [1024, 8], [1, 128]]),
                    ptr3)
                dve_fence(v).then_inc(sem["wev"], 1)
            for i in range(NT128):
                v.wait_ge(sem[f"ldx{i % 2}"], 16 * (i // 2 + 1))
                xc = xt[i % 2]
                if i >= 2:
                    v.wait_ge(sem["a2v"], i - 1)
                emit_ln_stats_dve(v, xc, st, j=i % 2)
                dve_fence(v).then_inc(sem["mvr"], 1)
                v.wait_ge(sem["a2v"], i + 1)
                if DEBUG_L0FFN and i >= 1:
                    v.wait_ge(sem["dbd"], 48 * i)
                v.tensor_scalar(xn[:, :], xc[:, :],
                                st["negm"][:, i % 2:i % 2 + 1],
                                st["r"][:, i % 2:i % 2 + 1],
                                ALU.add, ALU.mult)
                v.drain()
                emit_quant_dve(v, xn, xq, t)
                if DEBUG_L0FFN:
                    v.tensor_copy(dbq[:, :], xq[:, :])
                    v.tensor_copy(dbs[:, 0:1], st["negm"][:, i % 2:i % 2 + 1])
                    v.tensor_copy(dbs[:, 1:2], st["r"][:, i % 2:i % 2 + 1])
                    v.tensor_copy(dbs[:, 2:4], st["mv"][:, 2*(i%2):2*(i%2)+2])
                dve_fence(v).then_inc(sem["xq_rdy"], 1)
                if DEBUG_L0FFN:
                    v.tensor_copy(dbq[:, 0:1], dbq[:, 0:1]).then_inc(
                        sem["dbg"], 1)
                if i >= 1:
                    emit_ep(v, i - 1)
                v.wait_ge(sem["xtr"], i + 1)
                v.tensor_copy(xfm[:, :], ptr[:, :])
                dve_fence(v).then_inc(sem["xev"], 1)
            emit_ep(v, NT128 - 1)

        @block.tensor
        def _(te):
            te.wait_ge(sem["ldi"], 16)
            for k in range(8):
                te.wait_ge(sem["wq_rdy"], k + 1)
                if k >= 1:
                    te.wait_ge(sem["wev"], k)
                for c in range(8):
                    mm = te.transpose(ptr[:, c * 128:(c + 1) * 128],
                                      wq[:, c * 128:(c + 1) * 128],
                                      identb[:, :])
                mm.then_inc(sem["wtr"], 1)
            for i in range(NT128):
                te.wait_ge(sem["xq_rdy"], i + 1)
                te.wait_ge(sem["wev"], 8) if i == 0 else te.wait_ge(
                    sem["xev"], i)
                for c in range(8):
                    mm = te.transpose(ptr[:, c * 128:(c + 1) * 128],
                                      xq[:, c * 128:(c + 1) * 128],
                                      identb[:, :])
                mm.then_inc(sem["xtr"], 1)
                te.wait_ge(sem["xev"], i + 1)
                if i > 0:
                    te.wait_ge(sem["ep"], i)
                for h in range(2):
                    for c in range(8):
                        mm = te.matmul(py[h][:, :],
                                       xfm[:, c * 128:(c + 1) * 128],
                                       wT[:, c * 1024 + h * 512:
                                          c * 1024 + (h + 1) * 512],
                                       start=(c == 0), stop=(c == 7))
                    mm.then_inc(sem["mm"], 1)

        if DEBUG_L0FFN:
            @block.gpsimd
            def _(g):
                for i in range(NT128):
                    g.wait_ge(sem["dbg"], i + 1)
                    g.dma_start(out=dbg["xn"][i * 128:(i + 1) * 128, :],
                                in_=xn[:, :]).then_inc(sem["dbd"], 16)
                    g.dma_start(out=dbg["xq"][i * 128:(i + 1) * 128, :],
                                in_=dbq[:, :]).then_inc(sem["dbd"], 16)
                    g.dma_start(out=dbg["st"].ap()[i, :, :],
                                in_=dbs[:, :]).then_inc(sem["dbd"], 16)
                g.wait_ge(sem["wev"], 8)
                g.dma_start(out=dbg["wT"][:, :], in_=wT[:, :]).then_inc(
                    sem["dbd"], 16)
                g.wait_ge(sem["dbd"], 16 * (3 * NT128 + 1))

    es.close()
    pool.end()
    return [(sem["so0"], pool.base["so0"]), (sem["so1"], pool.base["so1"])]


def build_l0ffn():
    nc = bass.Bass("TRN2", target_bir_lowering=False)
    io = {
        "x_in": nc.dram_tensor("x_in", [L, D], F32, kind="ExternalInput"),
        "w": nc.dram_tensor("w", [D, D], F32, kind="ExternalInput"),
        "b": nc.dram_tensor("b", [D], F32, kind="ExternalInput"),
        "ident": nc.dram_tensor("ident", [128, 128], F32,
                                kind="ExternalInput"),
        "x_out": nc.dram_tensor("x_out", [L, D], F32, kind="ExternalOutput"),
    }
    pool = SemPool(nc)
    with nc.Block() as block:
        emit_l0ffn(nc, block, pool, "", [], io)
    return nc




from contextlib import ExitStack

TC = 256
NCH = L // TC
T5 = 512
NC5 = L // T5


def emit_mamba(nc, block, pool, pfx, gates, io, v1):
    es = ExitStack()
    sbt = lambda *a: es.enter_context(nc.sbuf_tensor(*a))
    block = BlockWrap(block, pool, gates)
    pool.begin()
    DI = DIN
    NDC = DI // 128
    x_in = io["x_in"]
    w_proj = io["w_proj"]
    conv_w = io["conv_w"]
    conv_b = io["conv_b"]
    wx_d, wx_B, wx_C = io["wx_d"], io["wx_B"], io["wx_C"]
    w_dt, b_dt = io["w_dt"], io["b_dt"]
    a_log, d_vec, w_out = io["a_log"], io["d_vec"], io["w_out"]
    ident = io["ident"]
    if v1:
        ln_w, ln_b = io["ln_w"], io["ln_b"]
        in_b, out_b = io["in_b"], io["out_b"]
    x_out = io["x_out"]

    upre_h = nc.dram_tensor(pfx + "upre_h", [DI, L + 4], F32 if v1 else BF16)
    res_h = nc.dram_tensor(pfx + "res_h", [DI, L], F32)
    u_h = nc.dram_tensor(pfx + "u_h", [DI, L], F32)
    del_h = nc.dram_tensor(pfx + "del_h", [DI, L], F32)
    bc_h = nc.dram_tensor(pfx + "bc_h", [2 * NST, L], F32)
    y_h = nc.dram_tensor(pfx + "y_h", [DI, L], F32)
    hid_h = nc.dram_tensor(pfx + "hid_h", [L, D], F32) if v1 else None

    def fmdc(dram, dc, col0, tlen, row0=0, width=L):
        return bass.AP(dram, (row0 + dc * 128) * width + col0,
                       [[width, 128], [1, tlen]])

    def fm_dma(eng, dram, col0, tlen, sbuf, seg, s_sem, ndc=NDC, row0=0,
               store=False, width=L):
        for dc in range(ndc):
            sb = sbuf[:, dc * seg:dc * seg + tlen]
            dr = fmdc(dram, dc, col0, tlen, row0, width)
            if store:
                eng.dma_start(out=dr, in_=sb).then_inc(s_sem, 16)
            else:
                eng.dma_start(out=sb, in_=dr).then_inc(s_sem, 16)

    identb = sbt(pfx + "identb", [128, 128], BF16)
    a_sb = sbt(pfx + "a_sb", [128, NDC * NST], F32)
    dcol = sbt(pfx + "dcol", [128, NDC], F32)
    bdtcol = sbt(pfx + "bdtcol", [128, NDC], F32)
    cbcol = sbt(pfx + "cbcol", [128, NDC], F32)
    wdt_sb = sbt(pfx + "wdt_sb", [RDT, DI], BF16)
    wxd_sb = sbt(pfx + "wxd_sb", [128, NDC * RDT], BF16)
    wxB_sb = sbt(pfx + "wxB_sb", [128, NDC * NST], BF16)
    wxC_sb = sbt(pfx + "wxC_sb", [128, NDC * NST], BF16)
    rf_sb = sbt(pfx + "rf_sb", [128, NDC], F32)
    rft = sbt(pfx + "rft", [128, 1], F32)
    carry = sbt(pfx + "carry", [128, NDC * NST], F32)
    st = {k: sbt(pfx + "s_" + k, [128, 1], F32)
          for k in ["lnv", "eps"]}
    st["negm"] = sbt(pfx + "s_negm", [128, 4], F32)
    st["r"] = sbt(pfx + "s_r", [128, 4], F32)
    st["mv"] = sbt(pfx + "s_mv", [128, 8], F32)
    st["stat12"] = sbt(pfx + "s_stat12", [128, 12], F32)
    zpad4 = sbt(pfx + "zpad4", [128, 4], F32 if v1 else BF16)
    if v1:
        inbcol = sbt(pfx + "inbcol", [128, 2 * NDC], F32)
        cw2col = sbt(pfx + "cw2col", [128, NDC * 4], F32)
    pLN = ExitStack()
    if v1:
        lnw_b = pLN.enter_context(nc.sbuf_tensor(pfx + "lnw_b", [128, D], F32))
        lnb_b = pLN.enter_context(nc.sbuf_tensor(pfx + "lnb_b", [128, D], F32))

    sem = {n: pool.get(n) for n in [
        "ldc", "ldw1",
        "ldx0", "ldx1", "unr", "mvr", "a2v", "sqr", "tr1", "rir", "mm1", "ubr",
        "spre", "sres", "shid",
        "cwld", "upld0", "upld1", "mmsc", "scf", "mmh", "su0", "su1",
        "uev2", "zp", "zpd",
        "ubld0", "ubld1", "hev", "rcr", "spa", "dtmm", "spl", "dcons",
        "dstgr", "sdel0", "sdel1", "sbc0", "sbc1", "spa3",
        "ld4", "ld4p0", "ld4p1", "p4a", "p4c", "p4y", "vdone",
        "sy0", "sy1", "sy2", "sy3",
        "ldw5", "ld5a", "ld5b", "s5a", "ygr", "ymm", "p5e", "ep5", "so5",
    ]}

    if True:
        @block.gpsimd
        def _(g):
            g.dma_start(out=identb[:, :], in_=ident[:, :]).then_inc(sem["ldc"], 16)
            g.dma_start(out=wdt_sb[:, :], in_=w_dt[:, :]).then_inc(sem["ldc"], 16)
            for dc in range(NDC):
                g.dma_start(out=wxd_sb[:, dc * RDT:(dc + 1) * RDT],
                            in_=wx_d[dc * 128:(dc + 1) * 128, :]).then_inc(
                                sem["ldc"], 16)
                g.dma_start(out=wxB_sb[:, dc * NST:(dc + 1) * NST],
                            in_=wx_B[dc * 128:(dc + 1) * 128, :]).then_inc(
                                sem["ldc"], 16)
                g.dma_start(out=wxC_sb[:, dc * NST:(dc + 1) * NST],
                            in_=wx_C[dc * 128:(dc + 1) * 128, :]).then_inc(
                                sem["ldc"], 16)
                g.dma_start(out=a_sb[:, dc * NST:(dc + 1) * NST],
                            in_=a_log[dc * 128:(dc + 1) * 128, :]).then_inc(
                                sem["ldc"], 16)
                g.dma_start(out=dcol[:, dc:dc + 1],
                            in_=bass.AP(d_vec, dc * 128, [[1, 128], [1, 1]])
                            ).then_inc(sem["ldc"], 16)
                g.dma_start(out=bdtcol[:, dc:dc + 1],
                            in_=bass.AP(b_dt, dc * 128, [[1, 128], [1, 1]])
                            ).then_inc(sem["ldc"], 16)
                g.dma_start(out=cbcol[:, dc:dc + 1],
                            in_=bass.AP(conv_b, dc * 128, [[1, 128], [1, 1]])
                            ).then_inc(sem["ldc"], 16)
                if v1:
                    g.dma_start(out=cw2col[:, dc * 4:(dc + 1) * 4],
                                in_=conv_w[dc * 128:(dc + 1) * 128, :]
                                ).then_inc(sem["ldc"], 16)
            if v1:
                g.dma_start(out=lnw_b[:, :], in_=bcast_row(ln_w, 0, D)
                            ).then_inc(sem["ldc"], 16)
                g.dma_start(out=lnb_b[:, :], in_=bcast_row(ln_b, 0, D)
                            ).then_inc(sem["ldc"], 16)
                for h in range(2 * NDC):
                    g.dma_start(out=inbcol[:, h:h + 1],
                                in_=bass.AP(in_b, h * 128, [[1, 128], [1, 1]])
                                ).then_inc(sem["ldc"], 16)

        NLDC = 16 * (2 + NDC * (8 if v1 else 7) + ((2 + 2 * NDC) if v1 else 0))

        @block.sync
        def _(s):
            s.wait_ge(sem["zp"], 1)
            for dc in range(NDC):
                s.dma_start(out=fmdc(upre_h, dc, 0, 4, width=L + 4),
                            in_=zpad4[:, :]).then_inc(sem["zpd"], 16)

        @block.vector
        def _(v):
            v.memset(st["eps"][:, :], 1e-5)
            v.memset(carry[:, :], 0.0)
            v.memset(zpad4[:, :], 0.0)
            v.drain().then_inc(sem["zp"], 1)
            v.wait_ge(sem["ldc"], NLDC)
            v.drain().then_inc(sem["unr"], 1)

        @block.scalar
        def _(sc):
            sc.wait_ge(sem["unr"], 1)
            sc.activation(a_sb[:, :], a_sb[:, :], AF.Exp, scale=1.0)
            sc.drain().then_inc(sem["a2v"], 1)

        @block.vector
        def _(v):
            v.wait_ge(sem["a2v"], 1)
            v.tensor_scalar(a_sb[:, :], a_sb[:, :], -1.0, None, ALU.mult)
            v.drain().then_inc(sem["unr"], 1)

        MVR0, A2V0, VP10 = 0, 1, 2
        APT = 2 if v1 else 1

        _sid = nc.enter_named_scope(pfx + "p1", False)[0]
        p1 = ExitStack()
        wproj_sb = [p1.enter_context(
            nc.sbuf_tensor(pfx + f"wp{dc}", [128, 2 * DI], BF16)) for dc in range(8)]
        xt = [p1.enter_context(nc.sbuf_tensor(pfx + f"xt{j}", [128, D], F32))
              for j in range(2)]
        hidt = p1.enter_context(nc.sbuf_tensor(pfx + "hidt", [128, D], F32))
        hnt = p1.enter_context(nc.sbuf_tensor(pfx + "hnt", [128, D], F32))
        xnb = p1.enter_context(nc.sbuf_tensor(pfx + "sqr", [128, D], BF16))
        xfmb = p1.enter_context(nc.sbuf_tensor(pfx + "xfmb", [128, 8 * T5], BF16))
        ustg = p1.enter_context(nc.sbuf_tensor(pfx + "ustg", [128, NDC * T5],
                                               F32 if v1 else BF16))
        rstg = p1.enter_context(nc.sbuf_tensor(pfx + "rstg", [128, NDC * T5], F32))
        ptr = p1.enter_context(nc.psum_tensor(pfx + "ptr", [128, 1024], BF16))
        pmm = [p1.enter_context(nc.psum_tensor(pfx + f"pmm{j}", [128, T5], F32))
               for j in range(2)]
        ptr3 = bass.AP(ptr, 0, [[1024, 128], [128, 8], [1, 128]])

        @block.gpsimd
        def _(g):
            for dc in range(8):
                g.dma_start(out=wproj_sb[dc][:, :],
                            in_=w_proj[dc * 128:(dc + 1) * 128, :]).then_inc(
                                sem["ldw1"], 16)
            for tt in range(NC5):
                if v1:
                    for q in range(4):
                        i = 4 * tt + q
                        g.wait_ge(sem["unr"], VP10 + i + 1)
                        g.dma_start(out=hid_h[i * 128:(i + 1) * 128, :],
                                    in_=hidt[:, :]).then_inc(sem["shid"], 16)
                g.wait_ge(sem["ubr"], tt + 1)
                fm_dma(g, upre_h, tt * T5 + 4, T5, ustg, T5, sem["spre"],
                       store=True, width=L + 4)
                fm_dma(g, res_h, tt * T5, T5, rstg, T5, sem["sres"],
                       store=True)

        @block.sync
        def _(s):
            for i in range(NT128):
                if i >= 2:
                    s.wait_ge(sem["unr"], VP10 + i - 1)
                s.dma_start(out=xt[i % 2][:, :],
                            in_=x_in[i * 128:(i + 1) * 128, :]).then_inc(
                                sem[f"ldx{i % 2}"], 16)

        @block.scalar
        def _(sc):
            for tt in range(NC5):
                for q in range(4):
                    i = 4 * tt + q
                    sc.wait_ge(sem["mvr"], MVR0 + APT * i + 1)
                    emit_ln_stats_act(sc, st, j=i % 2)
                    act_fence(sc).then_inc(sem["a2v"], 1)
                    if v1:
                        sc.wait_ge(sem["mvr"], MVR0 + 2 * i + 2)
                        emit_ln_stats_act(sc, st, j=2 + i % 2)
                        act_fence(sc).then_inc(sem["a2v"], 1)
                if tt >= 1:
                    sc.wait_ge(sem["spre"], 16 * NDC * tt)
                    sc.wait_ge(sem["sres"], 16 * NDC * tt)
                for oc in range(2 * NDC):
                    sc.wait_ge(sem["mm1"], 32 * tt + oc + 1)
                    dst = (ustg[:, oc * T5:(oc + 1) * T5] if oc < NDC else
                           rstg[:, (oc - NDC) * T5:(oc - NDC + 1) * T5])
                    if v1:
                        sc.activation(dst, pmm[oc % 2][:, :], AF.Identity,
                                      bias=inbcol[:, oc:oc + 1])
                    else:
                        sc.activation(dst, pmm[oc % 2][:, :], AF.Copy)
                    sc.drain().then_inc(sem["uev2"], 1)
                sc.drain().then_inc(sem["ubr"], 1)

        @block.vector
        def _(v):
            for i in range(NT128):
                tt, q = divmod(i, 4)
                v.wait_ge(sem[f"ldx{i % 2}"], 16 * (i // 2 + 1))
                xc = xt[i % 2]
                if i >= 2:
                    v.wait_ge(sem["a2v"], A2V0 + APT * (i - 1))
                emit_ln_stats_dve(v, xc, st, j=i % 2)
                dve_fence(v).then_inc(sem["mvr"], 1)
                v.wait_ge(sem["a2v"], A2V0 + APT * i + 1)
                if v1:
                    if i >= 1:
                        v.wait_ge(sem["shid"], 16 * i)
                    v.tensor_scalar(hidt[:, :], xc[:, :],
                                    st["negm"][:, i % 2:i % 2 + 1],
                                    st["r"][:, i % 2:i % 2 + 1],
                                    ALU.add, ALU.mult)
                    dve_fence(v).then_inc(sem["unr"], 1)
                    emit_ln_stats_dve(v, hidt, st, j=2 + i % 2)
                    dve_fence(v).then_inc(sem["mvr"], 1)
                    v.wait_ge(sem["a2v"], A2V0 + 2 * i + 2)
                    v.tensor_scalar(hnt[:, :], hidt[:, :],
                                    st["negm"][:, 2 + i % 2:3 + i % 2],
                                    st["r"][:, 2 + i % 2:3 + i % 2],
                                    ALU.add, ALU.mult)
                    v.drain()
                    v.tensor_tensor(hnt[:, :], hnt[:, :], lnw_b[:, :], ALU.mult)
                    v.drain()
                    v.tensor_tensor(xnb[:, :], hnt[:, :], lnb_b[:, :], ALU.add)
                else:
                    v.tensor_scalar(xnb[:, :], xc[:, :],
                                    st["negm"][:, i % 2:i % 2 + 1],
                                    st["r"][:, i % 2:i % 2 + 1],
                                    ALU.add, ALU.mult)
                    dve_fence(v).then_inc(sem["unr"], 1)
                dve_fence(v).then_inc(sem["sqr"], 1)
                v.wait_ge(sem["tr1"], i + 1)
                if i >= 4:
                    v.wait_ge(sem["mm1"], 32 * (i // 4))
                v.tensor_copy(
                    bass.AP(xfmb, q * 128,
                            [[8 * T5, 128], [T5, 8], [1, 128]]), ptr3)
                dve_fence(v).then_inc(sem["rir"], 1)

        @block.tensor
        def _(te):
            te.wait_ge(sem["ldc"], NLDC)
            te.wait_ge(sem["ldw1"], 16 * 8)
            for tt in range(NC5):
                for q in range(4):
                    i = 4 * tt + q
                    te.wait_ge(sem["sqr"], i + 1)
                    if i > 0:
                        te.wait_ge(sem["rir"], i)
                    for c in range(8):
                        mm = te.transpose(ptr[:, c * 128:(c + 1) * 128],
                                          xnb[:, c * 128:(c + 1) * 128],
                                          identb[:, :])
                    mm.then_inc(sem["tr1"], 1)
                te.wait_ge(sem["rir"], 4 * (tt + 1))
                for oc in range(2 * NDC):
                    gidx = 32 * tt + oc
                    if gidx >= 2:
                        te.wait_ge(sem["uev2"], gidx - 1)
                    for dc in range(8):
                        mm = te.matmul(pmm[oc % 2][:, :],
                                       wproj_sb[dc][:, oc * 128:(oc + 1) * 128],
                                       xfmb[:, dc * T5:(dc + 1) * T5],
                                       start=(dc == 0), stop=(dc == 7))
                    mm.then_inc(sem["mm1"], 1)

        p1.close()
        pLN.close()
        nc.leave_named_scope(pfx + "p1", _sid, False)

        _sid = nc.enter_named_scope(pfx + "p2", False)[0]
        p2 = ExitStack()
        P1S = 16 * NDC * NC5
        UPW = 16 * NDC * (T5 + 4)
        if v1:
            up2 = [p2.enter_context(nc.sbuf_tensor(
                pfx + f"up2_{j}", [128, NDC * (T5 + 4)], F32)) for j in range(2)]
            cacc = [p2.enter_context(nc.sbuf_tensor(
                pfx + f"cacc{j}", [128, T5], F32)) for j in range(2)]
            tapa = p2.enter_context(nc.sbuf_tensor(pfx + "tapa", [128, T5], F32))
            tapb = p2.enter_context(nc.sbuf_tensor(pfx + "tapb", [128, T5], F32))
            ustg2 = [p2.enter_context(nc.sbuf_tensor(
                pfx + f"ustg2_{j}", [128, NDC * T5], F32)) for j in range(2)]

            @block.sync
            def _(s):
                for cc in range(NC5):
                    if cc == 0:
                        s.wait_ge(sem["spre"], P1S)
                        s.wait_ge(sem["sres"], P1S)
                        s.wait_ge(sem["shid"], 16 * NT128)
                    if cc >= 2:
                        s.wait_ge(sem["mmh"], NDC * (cc - 1))
                    buf = up2[cc % 2]
                    if cc == 0:
                        s.wait_ge(sem["zpd"], 16 * NDC)
                    fm_dma(s, upre_h, cc * T5, T5 + 4, buf, T5 + 4,
                           sem[f"upld{cc % 2}"], width=L + 4)

            @block.vector
            def _(v):
                v.wait_ge(sem["spre"], P1S)
                v.wait_ge(sem["sres"], P1S)
                v.wait_ge(sem["shid"], 16 * NT128)
                v.drain().then_inc(sem["mmsc"], 1)
                for cc in range(NC5):
                    buf = up2[cc % 2]
                    v.wait_ge(sem[f"upld{cc % 2}"], 16 * NDC * (cc // 2 + 1))
                    for dc in range(NDC):
                        idx = NDC * cc + dc
                        if idx >= 2:
                            v.wait_ge(sem["scf"], idx - 1)
                        base = dc * (T5 + 4)
                        ca = cacc[dc % 2]
                        v.tensor_scalar(tapa[:, :],
                                        buf[:, base + 1:base + 1 + T5],
                                        cw2col[:, dc * 4:dc * 4 + 1], None,
                                        ALU.mult)
                        v.tensor_scalar(tapb[:, :],
                                        buf[:, base + 2:base + 2 + T5],
                                        cw2col[:, dc * 4 + 1:dc * 4 + 2], None,
                                        ALU.mult)
                        v.drain()
                        v.scalar_tensor_tensor(
                            tapa[:, :], buf[:, base + 3:base + 3 + T5],
                            cw2col[:, dc * 4 + 2:dc * 4 + 3], tapa[:, :],
                            ALU.mult, ALU.add)
                        v.scalar_tensor_tensor(
                            tapb[:, :], buf[:, base + 4:base + 4 + T5],
                            cw2col[:, dc * 4 + 3:dc * 4 + 4], tapb[:, :],
                            ALU.mult, ALU.add)
                        v.drain()
                        v.tensor_tensor(ca[:, :], tapa[:, :], tapb[:, :],
                                        ALU.add)
                        v.drain().then_inc(sem["mmh"], 1)

            @block.scalar
            def _(sc):
                for cc in range(NC5):
                    if cc >= 2:
                        sc.wait_ge(sem[f"su{cc % 2}"], 16 * NDC * (cc // 2))
                    for dc in range(NDC):
                        idx = NDC * cc + dc
                        sc.wait_ge(sem["mmh"], idx + 1)
                        sc.activation(ustg2[cc % 2][:, dc * T5:(dc + 1) * T5],
                                      cacc[dc % 2][:, :], AF.Silu,
                                      bias=cbcol[:, dc:dc + 1])
                        sc.drain().then_inc(sem["scf"], 1)
                    sc.drain()
                    fm_dma(sc, u_h, cc * T5, T5, ustg2[cc % 2], T5,
                           sem[f"su{cc % 2}"], store=True)
            SU_T = 16 * NDC * (NC5 // 2)
        else:
            cw_sb = [p2.enter_context(nc.sbuf_tensor(
                pfx + f"cw{t_}_{ic}", [128, 8 * 128], BF16))
                for t_ in range(4) for ic in range(NDC)]
            up2 = [p2.enter_context(nc.sbuf_tensor(
                pfx + f"up2_{j}", [128, NDC * (T5 + 4)], BF16)) for j in range(2)]
            ustg2 = [p2.enter_context(nc.sbuf_tensor(
                pfx + f"ustg2_{j}", [128, 8 * T5], F32)) for j in range(2)]
            pcv = [p2.enter_context(nc.psum_tensor(pfx + f"pcv{j}", [128, T5], F32))
                   for j in range(2)]

            @block.gpsimd
            def _(g):
                for h2 in range(2):
                    if h2 == 0:
                        g.wait_ge(sem["spre"], P1S)
                        g.wait_ge(sem["sres"], P1S)
                    else:
                        g.wait_ge(sem["mmsc"], 1 + 64)
                    for t_ in range(4):
                        for ic in range(NDC):
                            g.dma_start(
                                out=cw_sb[t_ * NDC + ic][:, :],
                                in_=conv_w.ap()[t_, ic * 128:(ic + 1) * 128,
                                                h2 * 1024:(h2 + 1) * 1024]
                                ).then_inc(sem["cwld"], 16)

            @block.sync
            def _(s):
                for h2 in range(2):
                    for cc in range(NC5):
                        idx = h2 * NC5 + cc
                        if idx == 0:
                            s.wait_ge(sem["spre"], P1S)
                            s.wait_ge(sem["sres"], P1S)
                        if idx >= 2:
                            s.wait_ge(sem["mmsc"], 1 + 8 * (idx - 1))
                        buf = up2[idx % 2]
                        if idx == 0:
                            s.wait_ge(sem["zpd"], 16 * NDC)
                        fm_dma(s, upre_h, cc * T5, T5 + 4, buf[:, :],
                               T5 + 4, sem[f"upld{idx % 2}"], width=L + 4)

            @block.vector
            def _(v):
                v.wait_ge(sem["spre"], P1S)
                v.wait_ge(sem["sres"], P1S)
                v.drain().then_inc(sem["mmsc"], 1)

            @block.tensor
            def _(te):
                te.wait_ge(sem["mmsc"], 1)
                for h2 in range(2):
                    te.wait_ge(sem["cwld"], 16 * 64 * (h2 + 1))
                    for cc in range(NC5):
                        idx = h2 * NC5 + cc
                        te.wait_ge(sem[f"upld{idx % 2}"],
                                   16 * NDC * (idx // 2 + 1))
                        buf = up2[idx % 2]
                        for oc8 in range(8):
                            gidx = 8 * idx + oc8
                            if gidx >= 2:
                                te.wait_ge(sem["scf"], gidx - 1)
                            first = True
                            for t_ in range(4):
                                for ic in range(NDC):
                                    mm = te.matmul(
                                        pcv[oc8 % 2][:, :],
                                        cw_sb[t_ * NDC + ic][:,
                                                             oc8 * 128:
                                                             (oc8 + 1) * 128],
                                        buf[:, ic * (T5 + 4) + 1 + t_:
                                            ic * (T5 + 4) + 1 + t_ + T5],
                                        start=first,
                                        stop=(t_ == 3 and ic == NDC - 1))
                                    first = False
                            mm.then_inc(sem["mmsc"], 1)

            @block.scalar
            def _(sc):
                for h2 in range(2):
                    for cc in range(NC5):
                        idx = h2 * NC5 + cc
                        if idx >= 2:
                            sc.wait_ge(sem[f"su{idx % 2}"], 16 * 8 * (idx // 2))
                        for oc8 in range(8):
                            gidx = 8 * idx + oc8
                            sc.wait_ge(sem["mmsc"], 1 + gidx + 1)
                            oc = h2 * 8 + oc8
                            sc.activation(
                                ustg2[idx % 2][:, oc8 * T5:(oc8 + 1) * T5],
                                pcv[oc8 % 2][:, :], AF.Silu,
                                bias=cbcol[:, oc:oc + 1])
                            sc.drain().then_inc(sem["scf"], 1)
                        sc.drain()
                        fm_dma(sc, u_h, cc * T5, T5, ustg2[idx % 2], T5,
                               sem[f"su{idx % 2}"], ndc=8, row0=h2 * 1024,
                               store=True)
            SU_T = 16 * 8 * NC5

        p2.close()
        nc.leave_named_scope(pfx + "p2", _sid, False)

        _sid = nc.enter_named_scope(pfx + "p3", False)[0]
        p3 = ExitStack()
        ub3 = [p3.enter_context(nc.sbuf_tensor(pfx + f"ub3_{j}", [128, NDC * T5],
                                               BF16)) for j in range(2)]
        draw = p3.enter_context(nc.sbuf_tensor(pfx + "draw", [RDT, T5], BF16))
        bcstB = [p3.enter_context(nc.sbuf_tensor(pfx + f"bcstB{j}", [NST, T5], F32))
                 for j in range(2)]
        bcstC = [p3.enter_context(nc.sbuf_tensor(pfx + f"bcstC{j}", [NST, T5], F32))
                 for j in range(2)]
        dstg = [p3.enter_context(nc.sbuf_tensor(pfx + f"dstg{j}", [128, NDC * T5],
                                                F32)) for j in range(2)]
        sp1 = [p3.enter_context(nc.sbuf_tensor(pfx + f"sp1_{j}", [128, T5], F32))
               for j in range(2)]
        sp2 = [p3.enter_context(nc.sbuf_tensor(pfx + f"sp2_{j}", [128, T5], F32))
               for j in range(2)]
        pxd = p3.enter_context(nc.psum_tensor(pfx + "pxd", [64, T5], F32))
        pxb = p3.enter_context(nc.psum_tensor(pfx + "pxb", [16, T5], F32))
        pxc = p3.enter_context(nc.psum_tensor(pfx + "pxc", [16, T5], F32))
        pdt = [p3.enter_context(nc.psum_tensor(pfx + f"pdt{j}", [128, T5], F32))
               for j in range(2)]

        @block.gpsimd
        def _(g):
            for cc in range(NC5):
                if cc == 0:
                    g.wait_ge(sem["su0"], SU_T)
                    g.wait_ge(sem["su1"], SU_T)
                if cc >= 2:
                    g.wait_ge(sem["hev"], cc - 1)
                fm_dma(g, u_h, cc * T5, T5, ub3[cc % 2], T5,
                       sem[f"ubld{cc % 2}"])

        @block.tensor
        def _(te):
            for cc in range(NC5):
                te.wait_ge(sem[f"ubld{cc % 2}"], 16 * NDC * (cc // 2 + 1))
                if cc >= 1:
                    te.wait_ge(sem["spa3"], cc)
                ub = ub3[cc % 2]
                for dc in range(NDC):
                    te.matmul(pxd[:, :], wxd_sb[:, dc * RDT:(dc + 1) * RDT],
                              ub[:, dc * T5:(dc + 1) * T5],
                              start=(dc == 0), stop=(dc == NDC - 1))
                for dc in range(NDC):
                    te.matmul(pxb[:, :], wxB_sb[:, dc * NST:(dc + 1) * NST],
                              ub[:, dc * T5:(dc + 1) * T5],
                              start=(dc == 0), stop=(dc == NDC - 1))
                for dc in range(NDC):
                    mm = te.matmul(pxc[:, :],
                                   wxC_sb[:, dc * NST:(dc + 1) * NST],
                                   ub[:, dc * T5:(dc + 1) * T5],
                                   start=(dc == 0), stop=(dc == NDC - 1))
                mm.then_inc(sem["hev"], 1)
                te.wait_ge(sem["rcr"], cc + 1)
                for dc in range(NDC):
                    idx = NDC * cc + dc
                    if idx >= 2:
                        te.wait_ge(sem["spl"], idx - 1)
                    te.matmul(pdt[dc % 2][:, :],
                              wdt_sb[:, dc * 128:(dc + 1) * 128],
                              draw[:, :], start=True,
                              stop=True).then_inc(sem["dtmm"], 1)

        @block.scalar
        def _(sc):
            for cc in range(NC5):
                sc.wait_ge(sem["hev"], cc + 1)
                if cc >= 1:
                    sc.wait_ge(sem["dtmm"], NDC * cc)
                sc.activation(draw[:, :], pxd[:, :], AF.Copy)
                sc.drain().then_inc(sem["rcr"], 1)
                if cc >= 2:
                    sc.wait_ge(sem[f"sbc{cc % 2}"], 32 * (cc // 2))
                sc.activation(bcstB[cc % 2][:, :], pxb[:, :], AF.Copy)
                sc.activation(bcstC[cc % 2][:, :], pxc[:, :], AF.Copy)
                sc.drain().then_inc(sem["spa"], 1)
                sc.drain().then_inc(sem["spa3"], 1)
                for dc in range(NDC):
                    idx = NDC * cc + dc
                    sc.wait_ge(sem["dtmm"], idx + 1)
                    if idx >= 2:
                        sc.wait_ge(sem["dcons"], idx - 1)
                    pp = pdt[dc % 2]
                    sc.activation(sp1[dc % 2][:, :], pp[:, :], AF.Abs,
                                  bias=bdtcol[:, dc:dc + 1])
                    sc.activation(sp2[dc % 2][:, :], pp[:, :], AF.Relu,
                                  bias=bdtcol[:, dc:dc + 1])
                    sc.drain()
                    sc.activation(sp1[dc % 2][:, :], sp1[dc % 2][:, :],
                                  AF.Exp, scale=-1.0)
                    sc.drain()
                    sc.activation(sp1[dc % 2][:, :], sp1[dc % 2][:, :],
                                  AF.Ln, bias=1.0)
                    sc.drain().then_inc(sem["spl"], 1)

        @block.vector
        def _(v):
            for cc in range(NC5):
                if cc >= 2:
                    v.wait_ge(sem[f"sdel{cc % 2}"], 16 * NDC * (cc // 2))
                for dc in range(NDC):
                    idx = NDC * cc + dc
                    v.wait_ge(sem["spl"], idx + 1)
                    v.tensor_tensor(dstg[cc % 2][:, dc * T5:(dc + 1) * T5],
                                    sp1[dc % 2][:, :], sp2[dc % 2][:, :],
                                    ALU.add).then_inc(sem["dcons"], 1)
                v.drain().then_inc(sem["dstgr"], 1)

        @block.sync
        def _(s):
            for cc in range(NC5):
                s.wait_ge(sem["dstgr"], cc + 1)
                fm_dma(s, del_h, cc * T5, T5, dstg[cc % 2], T5,
                       sem[f"sdel{cc % 2}"], store=True)
                s.wait_ge(sem["spa"], cc + 1)
                s.dma_start(out=bass.AP(bc_h, cc * T5, [[L, NST], [1, T5]]),
                            in_=bcstB[cc % 2][:, :]).then_inc(
                                sem[f"sbc{cc % 2}"], 16)
                s.dma_start(out=bass.AP(bc_h, NST * L + cc * T5,
                                        [[L, NST], [1, T5]]),
                            in_=bcstC[cc % 2][:, :]).then_inc(
                                sem[f"sbc{cc % 2}"], 16)

        p3.close()
        nc.leave_named_scope(pfx + "p3", _sid, False)

        _sid = nc.enter_named_scope(pfx + "p4", False)[0]
        p4 = ExitStack()
        dd4 = [p4.enter_context(nc.sbuf_tensor(pfx + f"dd4_{j}",
                                                [128, NDC * TC], F32))
               for j in range(2)]
        du4 = [p4.enter_context(nc.sbuf_tensor(pfx + f"du4_{j}",
                                                [128, NDC * TC], F32))
               for j in range(2)]
        dB4 = [p4.enter_context(nc.sbuf_tensor(pfx + f"dB4_{j}",
                                                [128, NST * TC], F32))
               for j in range(2)]
        dC4 = p4.enter_context(nc.sbuf_tensor(pfx + "dC4", [128, NST * TC], F32))
        dlu4 = p4.enter_context(nc.sbuf_tensor(pfx + "dlu4", [128, TC], F32))
        tr1 = p4.enter_context(nc.sbuf_tensor(pfx + "tr1", [128, 8 * TC], F32))
        tr2 = p4.enter_context(nc.sbuf_tensor(pfx + "tr2", [128, 4 * TC], F32))
        tr3 = p4.enter_context(nc.sbuf_tensor(pfx + "tr3", [128, 2 * TC], F32))
        s4 = p4.enter_context(nc.sbuf_tensor(pfx + "s4", [128, NST * TC], F32))
        ybd = [p4.enter_context(nc.sbuf_tensor(pfx + f"ybd{j}", [128, TC],
                                               F32)) for j in range(4)]
        abb = [p4.enter_context(nc.sbuf_tensor(pfx + f"abb{j}", [128, NST * TC],
                                               F32)) for j in range(2)]
        ball = [p4.enter_context(nc.sbuf_tensor(pfx + "ball0",
                                                [128, NST * TC], F32))]
        yred = p4.enter_context(nc.sbuf_tensor(pfx + "yred", [128, TC], F32))

        LD4C = 16 * (2 * NDC + NST)
        LDCC = 16 * NST

        @block.sync
        def _(s):
            for ch in range(NCH):
                if ch == 0:
                    s.wait_ge(sem["sdel0"], 16 * NDC * (NC5 // 2))
                    s.wait_ge(sem["sdel1"], 16 * NDC * (NC5 // 2))
                    s.wait_ge(sem["sbc0"], 32 * (NC5 // 2))
                    s.wait_ge(sem["sbc1"], 32 * (NC5 // 2))
                if ch >= 2:
                    s.wait_ge(sem["vdone"], ch - 1)
                fm_dma(s, del_h, ch * TC, TC, dd4[ch % 2], TC,
                       sem[f"ld4p{ch % 2}"])
                fm_dma(s, u_h, ch * TC, TC, du4[ch % 2], TC,
                       sem[f"ld4p{ch % 2}"])
                for n in range(NST):
                    s.dma_start(out=dB4[ch % 2][:, n * TC:(n + 1) * TC],
                                in_=bass.AP(bc_h, n * L + ch * TC,
                                            [[0, 128], [1, TC]])).then_inc(
                                    sem[f"ld4p{ch % 2}"], 16)
                if ch >= 1:
                    s.wait_ge(sem["vdone"], ch)
                for n in range(NST):
                    s.dma_start(out=dC4[:, n * TC:(n + 1) * TC],
                                in_=bass.AP(bc_h, (NST + n) * L + ch * TC,
                                            [[0, 128], [1, TC]])).then_inc(
                                    sem["ld4"], 16)

        @block.scalar
        def _(sc):
            for ch in range(NCH):
                sc.wait_ge(sem[f"ld4p{ch % 2}"], LD4C * (ch // 2 + 1))
                for dc in range(NDC):
                    g_ = ch * NDC + dc
                    if g_ >= 2:
                        sc.wait_ge(sem["p4c"], g_ - 1)
                    for n in range(NST):
                        sc.activation(abb[g_ % 2][:, n * TC:(n + 1) * TC],
                                      dd4[ch % 2][:, dc * TC:(dc + 1) * TC],
                                      AF.Exp,
                                      scale=a_sb[:, dc * NST + n:
                                                 dc * NST + n + 1])
                    sc.drain().then_inc(sem["p4a"], 1)

        HTC = (NST // 2) * TC
        @block.gpsimd
        def _(g):
            for ch in range(NCH):
                for dc in range(NDC):
                    g_ = ch * NDC + dc
                    g.wait_ge(sem["p4y"], g_ + 1)
                    fm_dma(g, y_h, ch * TC, TC, ybd[g_ % 4], TC,
                           sem[f"sy{g_ % 4}"], ndc=1, row0=dc * 128,
                           store=True)

        @block.vector
        def _(v):
            for ch in range(NCH):
                v.wait_ge(sem[f"ld4p{ch % 2}"], LD4C * (ch // 2 + 1))
                for dc in range(NDC):
                    g_ = ch * NDC + dc
                    if g_ >= 4:
                        v.wait_ge(sem[f"sy{g_ % 4}"], 16 * (g_ // 4))
                    v.tensor_tensor(dlu4[:, :],
                                    dd4[ch % 2][:, dc * TC:(dc + 1) * TC],
                                    du4[ch % 2][:, dc * TC:(dc + 1) * TC],
                                    ALU.mult)
                    v.drain()
                    v.tensor_tensor(
                        ball[0][:, :],
                        bass.AP(dlu4, 0, [[TC, 128], [0, NST], [1, TC]]),
                        dB4[ch % 2][:, :], ALU.mult)
                    v.drain()
                    v.wait_ge(sem["p4a"], g_ + 1)
                    for n in range(NST):
                        v.tensor_tensor_scan(
                            s4[:, n * TC:(n + 1) * TC],
                            abb[g_ % 2][:, n * TC:(n + 1) * TC],
                            ball[0][:, n * TC:(n + 1) * TC],
                            carry[:, dc * NST + n:dc * NST + n + 1],
                            ALU.mult, ALU.add)
                    v.drain()
                    v.tensor_copy(
                        carry[:, dc * NST:(dc + 1) * NST],
                        bass.AP(s4, TC - 1, [[NST * TC, 128], [TC, NST],
                                             [1, 1]]))
                    if dc == 0:
                        v.wait_ge(sem["ld4"], LDCC * (ch + 1))
                    v.tensor_tensor(abb[g_ % 2][:, :], s4[:, :],
                                    dC4[:, :], ALU.mult)
                    v.drain()
                    v.tensor_tensor(tr1[:, :], abb[g_ % 2][:, 0:HTC],
                                    abb[g_ % 2][:, HTC:], ALU.add)
                    dve_fence(v).then_inc(sem["p4c"], 1)
                    v.tensor_tensor(tr2[:, :], tr1[:, 0:4 * TC],
                                    tr1[:, 4 * TC:], ALU.add)
                    v.drain()
                    v.tensor_tensor(tr3[:, :], tr2[:, 0:2 * TC],
                                    tr2[:, 2 * TC:], ALU.add)
                    v.drain()
                    v.tensor_tensor(yred[:, :], tr3[:, 0:TC], tr3[:, TC:],
                                    ALU.add)
                    v.drain()
                    if v1:
                        v.scalar_tensor_tensor(
                            ybd[g_ % 4][:, :],
                            du4[ch % 2][:, dc * TC:(dc + 1) * TC],
                            dcol[:, dc:dc + 1], yred[:, :],
                            ALU.mult, ALU.add)
                    else:
                        v.tensor_copy(ybd[g_ % 4][:, :], yred[:, :])
                    dve_fence(v).then_inc(sem["p4y"], 1)
                v.drain().then_inc(sem["vdone"], 1)
            if not v1:
                for dc in range(NDC):
                    v.reduce_sum(rft[:, :], carry[:, dc * NST:(dc + 1) * NST],
                                 axis=AX)
                    v.drain()
                    v.tensor_scalar(rf_sb[:, dc:dc + 1], rft[:, :],
                                    dcol[:, dc:dc + 1], None, ALU.mult)
                    v.drain()

        p4.close()
        nc.leave_named_scope(pfx + "p4", _sid, False)

        _sid = nc.enter_named_scope(pfx + "p5", False)[0]
        p5 = ExitStack()
        wout_sb = p5.enter_context(nc.sbuf_tensor(pfx + "wout_sb", [128, NDC * D],
                                                  BF16))
        y5 = p5.enter_context(nc.sbuf_tensor(pfx + "y5", [128, NDC * T5], F32))
        g5 = p5.enter_context(nc.sbuf_tensor(pfx + "g5", [128, NDC * T5], F32))
        ygb = p5.enter_context(nc.sbuf_tensor(pfx + "ygb", [128, NDC * T5], BF16))
        tmp5 = p5.enter_context(nc.sbuf_tensor(pfx + "tmp5", [128, T5], F32))
        tmp5b = p5.enter_context(nc.sbuf_tensor(pfx + "tmp5b", [128, T5], F32))
        X5 = [p5.enter_context(nc.sbuf_tensor(pfx + f"X5_{j}", [128, D], F32))
              for j in range(4)]
        hid5 = [p5.enter_context(nc.sbuf_tensor(pfx + f"hid5_{j}", [128, D], F32))
                for j in range(4)] if v1 else None
        xo5 = [p5.enter_context(nc.sbuf_tensor(pfx + f"xo5_{j}", [128, D], F32))
               for j in range(4)]
        pm5 = [p5.enter_context(nc.psum_tensor(pfx + f"pm5_{j}", [128, T5], F32))
               for j in range(2)]
        outb_b = (p5.enter_context(nc.sbuf_tensor(pfx + "outb_b", [128, D],
                                                  F32)) if v1 else None)

        LD5A = 16 * 2 * NDC
        LD5B = 16 * (8 if v1 else 4)
        NLDW5 = 16 * (NDC + (1 if v1 else 0))

        @block.gpsimd
        def _(g):
            for p_ in range(4):
                g.wait_ge(sem[f"sy{p_}"], 16 * (NCH * NDC // 4))
            if v1:
                g.dma_start(out=outb_b[:, :], in_=bcast_row(out_b, 0, D)
                            ).then_inc(sem["ldw5"], 16)
            for dc in range(NDC):
                g.dma_start(out=wout_sb[:, dc * D:(dc + 1) * D],
                            in_=w_out[dc * 128:(dc + 1) * 128, :]).then_inc(
                                sem["ldw5"], 16)

        @block.sync
        def _(s):
            for c5 in range(NC5):
                if c5 == 0:
                    for p_ in range(4):
                        s.wait_ge(sem[f"sy{p_}"], 16 * (NCH * NDC // 4))
                for dc in range(NDC):
                    if c5 >= 1:
                        s.wait_ge(sem["ygr"], NDC * (c5 - 1) + dc + 1)
                    fm_dma(s, y_h, c5 * T5, T5, y5[:, dc * T5:(dc + 1) * T5],
                           T5, sem["ld5a"], ndc=1, row0=dc * 128)
                    fm_dma(s, res_h, c5 * T5, T5,
                           g5[:, dc * T5:(dc + 1) * T5], T5, sem["ld5a"],
                           ndc=1, row0=dc * 128)
                for q in range(4):
                    i = 4 * c5 + q
                    if c5 >= 1:
                        s.wait_ge(sem["ep5"], 4 * (c5 - 1) + q + 1)
                    s.dma_start(out=X5[q][:, :],
                                in_=x_in[i * 128:(i + 1) * 128, :]).then_inc(
                                    sem["ld5b"], 16)
                    if v1:
                        s.dma_start(out=hid5[q][:, :],
                                    in_=hid_h[i * 128:(i + 1) * 128, :]
                                    ).then_inc(sem["ld5b"], 16)
                if c5 >= 1:
                    for q in range(4):
                        s.wait_ge(sem["ep5"], 4 * (c5 - 1) + q + 1)
                        i = 4 * (c5 - 1) + q
                        s.dma_start(out=x_out[i * 128:(i + 1) * 128, :],
                                    in_=xo5[q][:, :]).then_inc(sem["so5"], 16)
            for q in range(4):
                s.wait_ge(sem["ep5"], 4 * (NC5 - 1) + q + 1)
                i = 4 * (NC5 - 1) + q
                s.dma_start(out=x_out[i * 128:(i + 1) * 128, :],
                            in_=xo5[q][:, :]).then_inc(sem["so5"], 16)
            s.wait_ge(sem["so5"], 16 * NT128)

        @block.scalar
        def _(sc):
            for c5 in range(NC5):
                sc.wait_ge(sem["ld5a"], LD5A * (c5 + 1))
                for dc in range(NDC):
                    sc.activation(g5[:, dc * T5:(dc + 1) * T5],
                                  g5[:, dc * T5:(dc + 1) * T5], AF.Silu)
                    sc.drain().then_inc(sem["s5a"], 1)

        @block.vector
        def _(v):
            for c5 in range(NC5):
                for dc in range(NDC):
                    v.wait_ge(sem["s5a"], NDC * c5 + dc + 1)
                    if v1:
                        v.tensor_tensor(ygb[:, dc * T5:(dc + 1) * T5],
                                        y5[:, dc * T5:(dc + 1) * T5],
                                        g5[:, dc * T5:(dc + 1) * T5], ALU.mult)
                    else:
                        tb = tmp5 if dc % 2 == 0 else tmp5b
                        v.tensor_scalar(tb[:, :],
                                        y5[:, dc * T5:(dc + 1) * T5],
                                        rf_sb[:, dc:dc + 1], None, ALU.add)
                        v.drain()
                        v.tensor_tensor(ygb[:, dc * T5:(dc + 1) * T5],
                                        tb[:, :],
                                        g5[:, dc * T5:(dc + 1) * T5], ALU.mult)
                    v.drain().then_inc(sem["ygr"], 1)
                dve_fence(v)
                v.wait_ge(sem["ld5b"], LD5B * (c5 + 1))
                if c5 >= 1:
                    v.wait_ge(sem["so5"], 16 * 4 * c5)
                for q in range(4):
                    for oh in range(2):
                        e = 8 * c5 + 2 * q + oh
                        v.wait_ge(sem["ymm"], e + 1)
                        pm = pm5[(2 * q + oh) % 2]
                        if v1:
                            v.tensor_tensor(tmp5[:, :], pm[:, :],
                                            X5[q][:, oh * 512:(oh + 1) * 512],
                                            ALU.add)
                            v.tensor_tensor(tmp5b[:, :],
                                            hid5[q][:, oh * 512:(oh + 1) * 512],
                                            outb_b[:, oh * 512:(oh + 1) * 512],
                                            ALU.add)
                            v.drain()
                            v.tensor_tensor(
                                xo5[q][:, oh * 512:(oh + 1) * 512], tmp5[:, :],
                                tmp5b[:, :], ALU.add)
                        else:
                            v.tensor_tensor(xo5[q][:, oh * 512:(oh + 1) * 512],
                                            pm[:, :],
                                            X5[q][:, oh * 512:(oh + 1) * 512],
                                            ALU.add)
                        v.drain().then_inc(sem["p5e"], 1)
                    dve_fence(v).then_inc(sem["ep5"], 1)

        @block.tensor
        def _(te):
            te.wait_ge(sem["ldw5"], NLDW5)
            for c5 in range(NC5):
                te.wait_ge(sem["ygr"], NDC * (c5 + 1))
                for q in range(4):
                    for oh in range(2):
                        e = 8 * c5 + 2 * q + oh
                        if e >= 2:
                            te.wait_ge(sem["p5e"], e - 1)
                        for dc in range(NDC):
                            mm = te.matmul(
                                pm5[(2 * q + oh) % 2][:, :],
                                ygb[:, dc * T5 + q * 128:dc * T5 + (q + 1) * 128],
                                wout_sb[:, dc * D + oh * 512:
                                        dc * D + (oh + 1) * 512],
                                start=(dc == 0), stop=(dc == NDC - 1))
                        mm.then_inc(sem["ymm"], 1)

        p5.close()
        nc.leave_named_scope(pfx + "p5", _sid, False)
    es.close()
    pool.end()
    return [(sem["so5"], pool.base["so5"])]


def mamba_io_tensors(nc, pfx, v1, with_xio=True):
    DI = DIN
    io = {
        "w_proj": nc.dram_tensor(pfx + "w_proj", [D, 2 * DI], F32,
                                 kind="ExternalInput"),
        "conv_w": (nc.dram_tensor(pfx + "conv_w", [DI, 4], F32,
                                  kind="ExternalInput") if v1 else
                   nc.dram_tensor(pfx + "conv_w", [4, DI, DI], F32,
                                  kind="ExternalInput")),
        "conv_b": nc.dram_tensor(pfx + "conv_b", [DI], F32,
                                 kind="ExternalInput"),
        "wx_d": nc.dram_tensor(pfx + "wx_d", [DI, RDT], F32,
                               kind="ExternalInput"),
        "wx_B": nc.dram_tensor(pfx + "wx_B", [DI, NST], F32,
                               kind="ExternalInput"),
        "wx_C": nc.dram_tensor(pfx + "wx_C", [DI, NST], F32,
                               kind="ExternalInput"),
        "w_dt": nc.dram_tensor(pfx + "w_dt", [RDT, DI], F32,
                               kind="ExternalInput"),
        "b_dt": nc.dram_tensor(pfx + "b_dt", [DI], F32, kind="ExternalInput"),
        "a_log": nc.dram_tensor(pfx + "a_log", [DI, NST], F32,
                                kind="ExternalInput"),
        "d_vec": nc.dram_tensor(pfx + "d_vec", [DI], F32,
                                kind="ExternalInput"),
        "w_out": nc.dram_tensor(pfx + "w_out", [DI, D], F32,
                                kind="ExternalInput"),
    }
    if v1:
        io["ln_w"] = nc.dram_tensor(pfx + "ln_w", [D], F32,
                                    kind="ExternalInput")
        io["ln_b"] = nc.dram_tensor(pfx + "ln_b", [D], F32,
                                    kind="ExternalInput")
        io["in_b"] = nc.dram_tensor(pfx + "in_b", [2 * DI], F32,
                                    kind="ExternalInput")
        io["out_b"] = nc.dram_tensor(pfx + "out_b", [D], F32,
                                     kind="ExternalInput")
    return io


def build_mamba(v1: bool):
    nc = bass.Bass("TRN2", target_bir_lowering=False)
    io = mamba_io_tensors(nc, "", v1)
    io["x_in"] = nc.dram_tensor("x_in", [L, D], F32, kind="ExternalInput")
    io["ident"] = nc.dram_tensor("ident", [128, 128], F32,
                                 kind="ExternalInput")
    io["x_out"] = nc.dram_tensor("x_out", [L, D], F32, kind="ExternalOutput")
    pool = SemPool(nc)
    with nc.Block() as block:
        emit_mamba(nc, block, pool, "", [], io, v1)
    return nc



def emit_l1ffn(nc, block, pool, pfx, gates, io):
    es = ExitStack()
    sbt = lambda n, sh, dt: es.enter_context(nc.sbuf_tensor(pfx + n, sh, dt))
    pst = lambda n, sh, dt: es.enter_context(nc.psum_tensor(pfx + n, sh, dt))
    block = BlockWrap(block, pool, gates)
    pool.begin()
    NB = 512
    DFF = 4 * D
    x_in, x_out = io["x_in"], io["x_out"]
    w1t, w2t = io["w1t"], io["w2t"]
    lng, lnbd = io["lng"], io["lnbd"]
    rwt, rbv = io["rwt"], io["rbv"]
    rawu, rawv, gamma = io["rawu"], io["rawv"], io["gamma"]
    ident = io["ident"]

    w1_sb = [sbt(f"w1_{dc}", [128, DFF], BF16)
             for dc in range(8)]
    w2_sb = sbt("w2_sb", [128, 32 * D], BF16)
    rwt_sb = sbt("rwt_sb", [128, 8 * NB], BF16)
    ut_sb = sbt("ut_sb", [128, 8 * NB], BF16)
    vu_sb = sbt("vu_sb", [128, 4 * D], BF16)
    identb = sbt("identb", [128, 128], BF16)
    lnw_b = sbt("lnw_b", [128, D], F32)
    lnb_b = sbt("lnb_b", [128, D], F32)
    rb_b = sbt("rb_b", [128, NB], F32)
    nrm = {k: sbt("n_" + k, [128, 1], F32)
           for k in ["s", "ln", "sq", "mx", "ri"]}
    st = {"negm": sbt("s_negm", [128, 4], F32),
          "r": sbt("s_r", [128, 4], F32),
          "mv": sbt("s_mv", [128, 8], F32),
          "stat12": sbt("s_stat12", [128, 12], F32),
          "lnv": sbt("s_lnv", [128, 1], F32),
          "eps": sbt("s_eps", [128, 1], F32)}
    tk = {k: sbt("t_" + k, [128, 1], F32)
          for k in ["ssum", "ssc", "e2", "den", "rd", "rS", "th", "gs", "sp"]}
    mx8 = sbt("mx8", [128, 8], F32)

    xt = [sbt(f"xt{j}", [128, D], F32) for j in range(2)]
    xoh = [sbt(f"xoh{j}", [128, 512], F32) for j in range(2)]
    xnf = sbt("xnf", [128, D], F32)
    hnt = sbt("hnt", [128, D], F32)
    xnb = sbt("xnb", [128, D], BF16)
    ln2b = sbt("ln2b", [128, D], BF16)
    xfmb = sbt("xfmb", [128, 8 * 256], BF16)
    lfmb = sbt("lfmb", [128, 8 * 256], BF16)
    s1c = sbt("s1c", [128, 512], BF16)
    hwfm = sbt("hwfm", [128, 1024], BF16)
    hsb = sbt("hsb", [128, NB], F32)
    hwb = [sbt(f"hwb{j}", [128, NB], BF16) for j in range(2)]
    rc = sbt("rc", [128, NB], F32)
    spl = sbt("spl", [128, NB], F32)

    ptr = pst("ptr", [128, 1024], BF16)
    psc = pst("psc", [128, NB], F32)
    phh = pst("phh", [128, NB], F32)
    ps1 = pst("ps1", [128, 512], F32)
    psS = [pst(f"psS{j}", [128, 512], F32) for j in range(4)]
    ptr3 = bass.AP(ptr, 0, [[1024, 128], [128, 8], [1, 128]])

    sem = {n: pool.get(n) for n in [
        "ldi", "ldc", "ldw", "ldu", "unr", "sqr", "rir", "ubr", "utr",
        "uev", "vev", "ldx0", "ldx1", "mvr", "a2v", "xnr", "tr1", "xev",
        "mmsc", "scf", "mmh", "hev", "rcr", "spa", "ssr", "e2r", "hwr",
        "mm1", "gel", "s2u", "mmS", "epq", "gso0", "gso1", "lde0",
        "lde1", "hwt"]}
    NGR = NT128 // 2

    if True:
        @block.gpsimd
        def _(g):
            g.dma_start(out=identb[:, :], in_=ident[:, :]).then_inc(
                sem["ldi"], 16)
            g.dma_start(out=lnw_b[:, :], in_=bcast_row(lng, 0, D)).then_inc(
                sem["ldc"], 16)
            g.dma_start(out=lnb_b[:, :], in_=bcast_row(lnbd, 0, D)).then_inc(
                sem["ldc"], 16)
            g.dma_start(out=rb_b[:, :], in_=bcast_row(rbv, 0, NB)).then_inc(
                sem["ldc"], 16)
            g.dma_start(out=hnt[:, :], in_=bcast_row(gamma, 0, D)).then_inc(
                sem["ldc"], 16)
            for dc in range(8):
                g.dma_start(out=w1_sb[dc][:, :],
                            in_=w1t[dc * 128:(dc + 1) * 128, :]).then_inc(
                                sem["ldw"], 16)
            for hc in range(32):
                g.dma_start(out=w2_sb[:, hc * D:(hc + 1) * D],
                            in_=w2t[hc * 128:(hc + 1) * 128, :]).then_inc(
                                sem["ldw"], 16)
            for dc in range(8):
                g.dma_start(out=rwt_sb[:, dc * NB:(dc + 1) * NB],
                            in_=rwt[dc * 128:(dc + 1) * 128, :]).then_inc(
                                sem["ldw"], 16)
            for k in range(8):
                if 1 <= k <= 4:
                    g.wait_ge(sem["ubr"], k)
                elif k >= 5:
                    g.wait_ge(sem["vev"], k - 4)
                src = rawu if k < 4 else rawv
                uc = k % 4
                g.dma_start(out=xt[0][:, :],
                            in_=src[uc * 128:(uc + 1) * 128, :]).then_inc(
                                sem["ldu"], 16)

        @block.vector
        def _(v):
            v.memset(st["eps"][:, :], 1e-5)
            v.drain()
            for k in range(8):
                v.wait_ge(sem["ldu"], 16 * (k + 1))
                v.tensor_tensor(xnf[:, :], xt[0][:, :], xt[0][:, :], ALU.mult)
                v.drain()
                v.reduce_sum(nrm["s"][:, :], xnf[:, :], axis=AX)
                dve_fence(v).then_inc(sem["unr"], 1)
                v.wait_ge(sem["sqr"], k + 1)
                v.tensor_scalar(nrm["mx"][:, :], nrm["sq"][:, :], 1e-6, None,
                                ALU.max)
                v.drain()
                v.reciprocal(nrm["ri"][:, :], nrm["mx"][:, :])
                dve_fence(v).then_inc(sem["rir"], 1)
                if k < 4:
                    v.wait_ge(sem["utr"], k + 1)
                    v.tensor_copy(
                        bass.AP(ut_sb, k * 128,
                                [[8 * NB, 128], [NB, 8], [1, 128]]), ptr3)
                    dve_fence(v).then_inc(sem["uev"], 1)
                else:
                    v.wait_ge(sem["ldc"], 64)
                    v.tensor_scalar(xnf[:, :], xt[0][:, :], nrm["ri"][:, :1],
                                    None, ALU.mult)
                    v.drain()
                    v.tensor_tensor(vu_sb[:, (k - 4) * D:(k - 3) * D],
                                    xnf[:, :], hnt[:, :], ALU.mult)
                    dve_fence(v).then_inc(sem["vev"], 1)

        @block.scalar
        def _(sc):
            for k in range(8):
                sc.wait_ge(sem["unr"], k + 1)
                sc.activation(nrm["ln"][:, :], nrm["s"][:, :], AF.Ln)
                sc.drain()
                sc.activation(nrm["sq"][:, :], nrm["ln"][:, :], AF.Exp,
                              scale=0.5)
                act_fence(sc).then_inc(sem["sqr"], 1)
                if k < 4:
                    sc.wait_ge(sem["rir"], k + 1)
                    if k >= 1:
                        sc.wait_ge(sem["utr"], k)
                    sc.activation(xnb[:, :], xt[0][:, :], AF.Identity,
                                  scale=nrm["ri"][:, :1])
                    act_fence(sc).then_inc(sem["ubr"], 1)

        @block.tensor
        def _(te):
            te.wait_ge(sem["ldi"], 16)
            for uc in range(4):
                te.wait_ge(sem["ubr"], uc + 1)
                if uc >= 1:
                    te.wait_ge(sem["uev"], uc)
                for c in range(8):
                    mm = te.transpose(ptr[:, c * 128:(c + 1) * 128],
                                      xnb[:, c * 128:(c + 1) * 128],
                                      identb[:, :])
                mm.then_inc(sem["utr"], 1)

        @block.sync
        def _(s):
            for g_ in range(NGR):
                for q in range(2):
                    i = 2 * g_ + q
                    if i == 0:
                        s.wait_ge(sem["vev"], 4)
                    if i >= 2:
                        s.wait_ge(sem["xnr"], i - 1)
                    s.dma_start(out=xt[q][:, :],
                                in_=x_in[i * 128:(i + 1) * 128, :]).then_inc(
                                    sem[f"ldx{q}"], 16)
                for j in range(4):
                    e = 4 * g_ + j
                    q2, oh = divmod(j, 2)
                    ii = 2 * g_ + q2
                    if e >= 2:
                        s.wait_ge(sem[f"gso{e % 2}"], 16 * (e // 2))
                    s.dma_start(out=xoh[e % 2][:, :],
                                in_=x_in[ii * 128:(ii + 1) * 128,
                                         oh * 512:(oh + 1) * 512]).then_inc(
                                    sem[f"lde{e % 2}"], 16)
            s.wait_ge(sem["gso0"], 16 * 32)
            s.wait_ge(sem["gso1"], 16 * 32)

        @block.gpsimd
        def _(g):
            for e in range(4 * NGR):
                g_, j = divmod(e, 4)
                q2, oh = divmod(j, 2)
                ii = 2 * g_ + q2
                g.wait_ge(sem["epq"], e + 1)
                g.dma_start(out=x_out[ii * 128:(ii + 1) * 128,
                                      oh * 512:(oh + 1) * 512],
                            in_=xoh[e % 2][:, :]).then_inc(
                                sem[f"gso{e % 2}"], 16)

        @block.vector
        def _(v):
            v.wait_ge(sem["utr"], 4)
            for i in range(NT128):
                g_, q = divmod(i, 2)
                v.wait_ge(sem[f"ldx{q}"], 16 * (g_ + 1))
                if i >= 2:
                    v.wait_ge(sem["a2v"], 2 * i - 2)
                emit_ln_stats_dve(v, xt[q], st, j=q)
                dve_fence(v).then_inc(sem["mvr"], 1)
                v.wait_ge(sem["a2v"], 2 * i + 1)
                v.tensor_scalar(xnf[:, :], xt[q][:, :],
                                st["negm"][:, q:q + 1], st["r"][:, q:q + 1],
                                ALU.add, ALU.mult)
                v.drain()
                v.tensor_copy(xnb[:, :], xnf[:, :])
                emit_ln_stats_dve(v, xnf, st, j=2 + q)
                dve_fence(v).then_inc(sem["mvr"], 1)
                v.wait_ge(sem["a2v"], 2 * i + 2)
                v.tensor_scalar(hnt[:, :], xnf[:, :],
                                st["negm"][:, 2 + q:3 + q],
                                st["r"][:, 2 + q:3 + q], ALU.add, ALU.mult)
                v.drain()
                v.tensor_tensor(xnf[:, :], hnt[:, :], lnw_b[:, :], ALU.mult)
                v.drain()
                v.tensor_tensor(ln2b[:, :], xnf[:, :], lnb_b[:, :], ALU.add)
                dve_fence(v).then_inc(sem["xnr"], 1)
                v.wait_ge(sem["tr1"], 2 * i + 1)
                v.tensor_copy(
                    bass.AP(xfmb, q * 128, [[2048, 128], [256, 8], [1, 128]]),
                    ptr3)
                dve_fence(v).then_inc(sem["xev"], 1)
                v.wait_ge(sem["tr1"], 2 * i + 2)
                v.tensor_copy(
                    bass.AP(lfmb, q * 128, [[2048, 128], [256, 8], [1, 128]]),
                    ptr3)
                dve_fence(v).then_inc(sem["xev"], 1)
                v.wait_ge(sem["mmsc"], i + 1)
                v.tensor_tensor(spl[:, :], psc[:, :], rb_b[:, :], ALU.add)
                dve_fence(v).then_inc(sem["scf"], 1)
                v.tensor_scalar(rc[:, :], spl[:, :], 10.0, -10.0,
                                ALU.min, ALU.max)
                dve_fence(v).then_inc(sem["rcr"], 1)
                v.wait_ge(sem["spa"], i + 1)
                v.max(mx8[:, :], spl[:, :])
                v.drain()
                v.reduce_sum(tk["ssum"][:, :], mx8[:, :], axis=AX)
                v.drain()
                v.tensor_scalar(tk["ssc"][:, :], tk["ssum"][:, :], 20.0, None,
                                ALU.min)
                dve_fence(v).then_inc(sem["ssr"], 1)
                v.wait_ge(sem["e2r"], i + 1)
                v.tensor_scalar(tk["den"][:, :], tk["e2"][:, :], 1.0, None,
                                ALU.add)
                v.tensor_scalar(tk["sp"][:, :], tk["ssum"][:, :], 1e-6, None,
                                ALU.add)
                v.drain()
                v.reciprocal(tk["rd"][:, :], tk["den"][:, :])
                v.reciprocal(tk["rS"][:, :], tk["sp"][:, :])
                v.drain()
                v.tensor_scalar(tk["th"][:, :], tk["rd"][:, :], -2.0, 1.0,
                                ALU.mult, ALU.add)
                v.drain()
                v.tensor_tensor(tk["gs"][:, :], tk["th"][:, :],
                                tk["rS"][:, :], ALU.mult)
                v.drain()
                v.tensor_scalar(rc[:, :], spl[:, :], mx8[:, 7:8], None,
                                ALU.is_ge)
                v.drain()
                v.tensor_tensor(rc[:, :], rc[:, :], spl[:, :], ALU.mult)
                v.drain()
                v.tensor_scalar(rc[:, :], rc[:, :], tk["gs"][:, :1], None,
                                ALU.mult)
                v.drain()
                v.wait_ge(sem["hev"], i + 1)
                v.tensor_tensor(hwb[q][:, :], hsb[:, :], rc[:, :], ALU.mult)
                dve_fence(v).then_inc(sem["hwr"], 1)
                if q == 1:
                    for qq in range(2):
                        v.wait_ge(sem["hwt"], 2 * g_ + qq + 1)
                        v.tensor_copy(hwfm[:, qq * 512:(qq + 1) * 512],
                                      ptr[:, 0:512])
                        dve_fence(v).then_inc(sem["xev"], 1)
                    for j in range(4):
                        q2, oh = divmod(j, 2)
                        e = 4 * g_ + j
                        v.wait_ge(sem["mmS"], e + 1)
                        v.wait_ge(sem[f"lde{e % 2}"], 16 * (e // 2 + 1))
                        v.scalar_tensor_tensor(
                            xoh[e % 2][:, :], psS[j][:, :], 0.5,
                            xoh[e % 2][:, :], ALU.mult, ALU.add)
                        dve_fence(v).then_inc(sem["epq"], 1)

        @block.scalar
        def _(sc):
            for i in range(NT128):
                g_, q = divmod(i, 2)
                sc.wait_ge(sem["mvr"], 2 * i + 1)
                emit_ln_stats_act(sc, st, j=q)
                act_fence(sc).then_inc(sem["a2v"], 1)
                sc.wait_ge(sem["mvr"], 2 * i + 2)
                emit_ln_stats_act(sc, st, j=2 + q)
                act_fence(sc).then_inc(sem["a2v"], 1)
                sc.wait_ge(sem["rcr"], i + 1)
                sc.activation(spl[:, :], rc[:, :], AF.Exp)
                sc.drain()
                sc.activation(spl[:, :], spl[:, :], AF.Ln, bias=1.0)
                act_fence(sc).then_inc(sem["spa"], 1)
                sc.wait_ge(sem["mmh"], i + 1)
                if i >= 1:
                    sc.wait_ge(sem["hwr"], i)
                sc.activation(hsb[:, :], phh[:, :], AF.Copy)
                act_fence(sc).then_inc(sem["hev"], 1)
                sc.wait_ge(sem["ssr"], i + 1)
                sc.activation(tk["e2"][:, :], tk["ssc"][:, :], AF.Exp,
                              scale=2.0)
                act_fence(sc).then_inc(sem["e2r"], 1)
                if q == 1:
                    for p in range(16):
                        sc.wait_ge(sem["mm1"], 32 * g_ + 2 * p + 2)
                        if 16 * g_ + p >= 1:
                            sc.wait_ge(sem["s2u"], 32 * g_ + 2 * p)
                        sc.activation(s1c[:, :], ps1[:, :], AF.Gelu)
                        act_fence(sc).then_inc(sem["gel"], 1)

        @block.tensor
        def _(te):
            te.wait_ge(sem["ldw"], 16 * 48)
            te.wait_ge(sem["uev"], 4)
            te.wait_ge(sem["vev"], 4)
            for i in range(NT128):
                g_, q = divmod(i, 2)
                te.wait_ge(sem["xnr"], i + 1)
                if i > 0:
                    te.wait_ge(sem["xev"], 6 * g_ + 2 * q)
                for c in range(8):
                    mm = te.transpose(ptr[:, c * 128:(c + 1) * 128],
                                      xnb[:, c * 128:(c + 1) * 128],
                                      identb[:, :])
                mm.then_inc(sem["tr1"], 1)
                te.wait_ge(sem["xev"], 6 * g_ + 2 * q + 1)
                for c in range(8):
                    mm = te.transpose(ptr[:, c * 128:(c + 1) * 128],
                                      ln2b[:, c * 128:(c + 1) * 128],
                                      identb[:, :])
                mm.then_inc(sem["tr1"], 1)
                te.wait_ge(sem["xev"], 6 * g_ + 2 * q + 2)
                if i >= 1:
                    te.wait_ge(sem["scf"], i)
                for dc in range(8):
                    mm = te.matmul(psc[:, :],
                                   lfmb[:, dc * 256 + q * 128:
                                        dc * 256 + (q + 1) * 128],
                                   rwt_sb[:, dc * NB:(dc + 1) * NB],
                                   start=(dc == 0), stop=(dc == 7))
                mm.then_inc(sem["mmsc"], 1)
                te.wait_ge(sem["hev"], i)
                for dc in range(8):
                    mm = te.matmul(phh[:, :],
                                   xfmb[:, dc * 256 + q * 128:
                                        dc * 256 + (q + 1) * 128],
                                   ut_sb[:, dc * NB:(dc + 1) * NB],
                                   start=(dc == 0), stop=(dc == 7))
                mm.then_inc(sem["mmh"], 1)
                if q == 1:
                    if g_ >= 1:
                        te.wait_ge(sem["epq"], 4 * g_)
                    for oc in range(33):
                        if oc < 32:
                            if oc >= 2:
                                te.wait_ge(sem["gel"], 16 * g_ + oc // 2)
                            for dc in range(8):
                                mm = te.matmul(
                                    ps1[:, (oc % 2) * 256:(oc % 2 + 1) * 256],
                                    w1_sb[dc][:, oc * 128:(oc + 1) * 128],
                                    xfmb[:, dc * 256:(dc + 1) * 256],
                                    start=(dc == 0), stop=(dc == 7))
                            mm.then_inc(sem["mm1"], 1)
                        if oc >= 1:
                            o2 = oc - 1
                            te.wait_ge(sem["gel"], 16 * g_ + o2 // 2 + 1)
                            for j in range(4):
                                q2, oh = divmod(j, 2)
                                mm = te.matmul(
                                    psS[j][:, :],
                                    s1c[:, (o2 % 2) * 256 + q2 * 128:
                                        (o2 % 2) * 256 + (q2 + 1) * 128],
                                    w2_sb[:, o2 * D + oh * 512:
                                          o2 * D + (oh + 1) * 512],
                                    start=(o2 == 0), stop=False)
                            mm.then_inc(sem["s2u"], 1)
                    for qq in range(2):
                        te.wait_ge(sem["hwr"], 2 * g_ + qq + 1)
                        te.wait_ge(sem["xev"], 6 * g_ + 4 + qq)
                        for c in range(4):
                            mm = te.transpose(ptr[:, c * 128:(c + 1) * 128],
                                              hwb[qq][:, c * 128:
                                                      (c + 1) * 128],
                                              identb[:, :])
                        mm.then_inc(sem["hwt"], 1)
                    te.wait_ge(sem["xev"], 6 * g_ + 6)
                    for j in range(4):
                        q2, oh = divmod(j, 2)
                        for vc in range(4):
                            mm = te.matmul(
                                psS[j][:, :],
                                hwfm[:, q2 * 512 + vc * 128:
                                     q2 * 512 + (vc + 1) * 128],
                                vu_sb[:, vc * D + oh * 512:
                                      vc * D + (oh + 1) * 512],
                                start=False, stop=(vc == 3))
                        mm.then_inc(sem["mmS"], 1)

    es.close()
    pool.end()
    return [(sem["gso0"], pool.base["gso0"]),
            (sem["gso1"], pool.base["gso1"])]


def l1ffn_io_tensors(nc, pfx):
    NB = 512
    DFF = 4 * D
    return {
        "w1t": nc.dram_tensor(pfx + "w1t", [D, DFF], F32,
                              kind="ExternalInput"),
        "w2t": nc.dram_tensor(pfx + "w2t", [DFF, D], F32,
                              kind="ExternalInput"),
        "lng": nc.dram_tensor(pfx + "lng", [D], F32, kind="ExternalInput"),
        "lnbd": nc.dram_tensor(pfx + "lnbd", [D], F32, kind="ExternalInput"),
        "rwt": nc.dram_tensor(pfx + "rwt", [D, NB], F32,
                              kind="ExternalInput"),
        "rbv": nc.dram_tensor(pfx + "rbv", [NB], F32, kind="ExternalInput"),
        "rawu": nc.dram_tensor(pfx + "rawu", [NB, D], F32,
                               kind="ExternalInput"),
        "rawv": nc.dram_tensor(pfx + "rawv", [NB, D], F32,
                               kind="ExternalInput"),
        "gamma": nc.dram_tensor(pfx + "gamma", [D], F32,
                                kind="ExternalInput"),
    }


def build_l1ffn():
    nc = bass.Bass("TRN2", target_bir_lowering=False)
    io = l1ffn_io_tensors(nc, "")
    io["x_in"] = nc.dram_tensor("x_in", [L, D], F32, kind="ExternalInput")
    io["ident"] = nc.dram_tensor("ident", [128, 128], F32,
                                 kind="ExternalInput")
    io["x_out"] = nc.dram_tensor("x_out", [L, D], F32, kind="ExternalOutput")
    pool = SemPool(nc)
    with nc.Block() as block:
        emit_l1ffn(nc, block, pool, "", [], io)
    return nc



def build_full():
    nc = bass.Bass("TRN2", target_bir_lowering=False)
    x_in = nc.dram_tensor("x_in", [L, D], F32, kind="ExternalInput")
    ident = nc.dram_tensor("ident", [128, 128], F32, kind="ExternalInput")
    x_out = nc.dram_tensor("x_out", [L, D], F32, kind="ExternalOutput")
    w0 = nc.dram_tensor("f0_w", [D, D], F32, kind="ExternalInput")
    b0 = nc.dram_tensor("f0_b", [D], F32, kind="ExternalInput")
    m0 = mamba_io_tensors(nc, "m0_", False)
    f1 = l1ffn_io_tensors(nc, "f1_")
    m1 = mamba_io_tensors(nc, "m1_", True)
    xh = [nc.dram_tensor(f"xh{j}", [L, D], F32) for j in range(5)]
    pool = SemPool(nc)
    with nc.Block() as block:
        with nc.named_scope("A_l0ffn"):
            g = emit_l0ffn(nc, block, pool, "A", [],
                           {"x_in": x_in, "x_out": xh[0], "w": w0, "b": b0,
                            "ident": ident})
        with nc.named_scope("B_mamba0"):
            g = emit_mamba(nc, block, pool, "B", g,
                           {**m0, "x_in": xh[0], "x_out": xh[1],
                            "ident": ident}, v1=False)
        with nc.named_scope("C_l0ffn"):
            g = emit_l0ffn(nc, block, pool, "C", g,
                           {"x_in": xh[1], "x_out": xh[2], "w": w0, "b": b0,
                            "ident": ident})
        with nc.named_scope("D_l1ffn"):
            g = emit_l1ffn(nc, block, pool, "D", g,
                           {**f1, "x_in": xh[2], "x_out": xh[3],
                            "ident": ident})
        with nc.named_scope("E_mamba1"):
            g = emit_mamba(nc, block, pool, "E", g,
                           {**m1, "x_in": xh[3], "x_out": xh[4],
                            "ident": ident}, v1=True)
        with nc.named_scope("F_l1ffn"):
            g = emit_l1ffn(nc, block, pool, "F", g,
                           {**f1, "x_in": xh[4], "x_out": x_out,
                            "ident": ident})
    return nc


def full_in_map(x, inp):
    m = {"x_in": np.ascontiguousarray(x), "ident": IDENT_NP,
         "f0_w": np.ascontiguousarray(inp["L0_ffn_weight"]),
         "f0_b": np.ascontiguousarray(inp["L0_ffn_bias"])}
    for pfx, sub in (("m0_", mamba_in_map_l0(x, inp)),
                     ("m1_", mamba_in_map_l1(x, inp)),
                     ("f1_", l1ffn_in_map(x, inp))):
        for k, v in sub.items():
            if k not in ("x_in", "ident"):
                m[pfx + k] = v
    return m



_NC_CACHE = {}


def _get_nc(name, builder):
    if name not in _NC_CACHE:
        _NC_CACHE[name] = builder()
    return _NC_CACHE[name]


def run_l0ffn(x_cores, w, b):
    nc = _get_nc("l0ffn", build_l0ffn)
    in_maps = [{"x_in": np.ascontiguousarray(x), "w": w, "b": b,
                "ident": IDENT_NP} for x in x_cores]
    res = run_bass_kernel_spmd(nc, in_maps, core_ids=list(range(len(x_cores))))
    return [np.asarray(r["x_out"]) for r in res.results]


def mamba_in_map_l0(x, inp):
    return {
        "x_in": np.ascontiguousarray(x),
        "w_proj": inp["L0_attn_W_in"],
        "conv_w": inp["L0_attn_conv_w"],
        "conv_b": inp["L0_attn_conv_b"],
        "wx_d": np.ascontiguousarray(inp["L0_attn_W_x"][:, :RDT]),
        "wx_B": np.ascontiguousarray(inp["L0_attn_W_x"][:, RDT:RDT + NST]),
        "wx_C": np.ascontiguousarray(inp["L0_attn_W_x"][:, RDT + NST:]),
        "w_dt": inp["L0_attn_W_dt"],
        "b_dt": inp["L0_attn_b_dt"],
        "a_log": inp["L0_attn_A_log"],
        "d_vec": inp["L0_attn_D"],
        "w_out": inp["L0_attn_W_out"],
        "ident": IDENT_NP,
    }


def mamba_in_map_l1(x, inp):
    xp = np.ascontiguousarray(inp["L1_attn_xp_w"].T)
    return {
        "x_in": np.ascontiguousarray(x),
        "w_proj": np.ascontiguousarray(inp["L1_attn_in_w"].T),
        "conv_w": np.ascontiguousarray(inp["L1_attn_conv_w"].reshape(DIN, 4)),
        "conv_b": inp["L1_attn_conv_b"],
        "wx_d": np.ascontiguousarray(xp[:, :RDT]),
        "wx_B": np.ascontiguousarray(xp[:, RDT:RDT + NST]),
        "wx_C": np.ascontiguousarray(xp[:, RDT + NST:]),
        "w_dt": np.ascontiguousarray(inp["L1_attn_dtp_w"].T),
        "b_dt": inp["L1_attn_dtp_b"],
        "a_log": inp["L1_attn_A_log"],
        "d_vec": inp["L1_attn_D"],
        "w_out": np.ascontiguousarray(inp["L1_attn_out_w"].T),
        "ln_w": inp["L1_attn_ln_w"],
        "ln_b": inp["L1_attn_ln_b"],
        "in_b": inp["L1_attn_in_b"],
        "out_b": inp["L1_attn_out_b"],
        "ident": IDENT_NP,
    }


def l1ffn_in_map(x, inp):
    return {
        "x_in": np.ascontiguousarray(x),
        "w1t": np.ascontiguousarray(inp["L1_ffn_W1"].T),
        "w2t": np.ascontiguousarray(inp["L1_ffn_W2"].T),
        "lng": np.ascontiguousarray(inp["L1_ffn_ln_g"]).reshape(-1),
        "lnbd": np.ascontiguousarray(inp["L1_ffn_ln_b"]).reshape(-1),
        "rwt": np.ascontiguousarray(inp["L1_ffn_router_W"].T),
        "rbv": np.ascontiguousarray(inp["L1_ffn_router_b"]).reshape(-1),
        "rawu": np.ascontiguousarray(inp["L1_ffn_raw_U"]),
        "rawv": np.ascontiguousarray(inp["L1_ffn_raw_V"]),
        "gamma": np.ascontiguousarray(inp["L1_ffn_gamma"]).reshape(-1),
        "ident": IDENT_NP,
    }


def run_l1ffn(x_cores, inp):
    nc = _get_nc("l1ffn", build_l1ffn)
    in_maps = [l1ffn_in_map(x, inp) for x in x_cores]
    res = run_bass_kernel_spmd(nc, in_maps, core_ids=list(range(len(x_cores))))
    return [np.asarray(r["x_out"]) for r in res.results]


def run_mamba(x_cores, inp, v1):
    nc = _get_nc("mamba1" if v1 else "mamba0",
                 lambda: build_mamba(v1))
    mk = mamba_in_map_l1 if v1 else mamba_in_map_l0
    in_maps = [mk(x, inp) for x in x_cores]
    res = run_bass_kernel_spmd(nc, in_maps, core_ids=list(range(len(x_cores))))
    return [np.asarray(r["x_out"]) for r in res.results]


def kernel(**inputs):
    x = np.asarray(inputs["L0_attn_x"], dtype=np.float32)
    nc = _get_nc("full", build_full)
    in_maps = [full_in_map(np.ascontiguousarray(x[c]), inputs)
               for c in range(NCORE)]
    res = run_bass_kernel_spmd(nc, in_maps, core_ids=list(range(NCORE)))
    return np.stack([np.asarray(r["x_out"]) for r in res.results], axis=0)


def kernel_unfused(**inputs):
    x = np.asarray(inputs["L0_attn_x"], dtype=np.float32)
    x_cores = [np.ascontiguousarray(x[c]) for c in range(NCORE)]
    w, b = inputs["L0_ffn_weight"], inputs["L0_ffn_bias"]
    x_cores = run_l0ffn(x_cores, w, b)
    x_cores = run_mamba(x_cores, inputs, v1=False)
    x_cores = run_l0ffn(x_cores, w, b)
    x_cores = run_l1ffn(x_cores, inputs)
    x_cores = run_mamba(x_cores, inputs, v1=True)
    x_cores = run_l1ffn(x_cores, inputs)
    return np.stack(x_cores, axis=0)


# version 20
# speedup vs baseline: 1.1964x; 1.0055x over previous
import numpy as np
import concourse.bass as bass
import concourse.mybir as mybir
from concourse.bass_utils import run_bass_kernel_spmd

F32 = mybir.dt.float32
BF16 = mybir.dt.bfloat16
AF = mybir.ActivationFunctionType
ALU = mybir.AluOpType
AX = mybir.AxisListType.X

L = 4096
D = 1024
DIN = 2048
NST = 16
RDT = 64
MAGIC = 12582912.0
NT128 = L // 128
NCORE = 8

IDENT_NP = np.eye(128, dtype=np.float32)


def bcast_row(dram, row, n, part=128):
    return bass.AP(dram, row * n, [[0, part], [1, n]])



def emit_quant_dve(vector, src, dst_bf16, t, nblk=32, blk=32):
    v3 = lambda h: h[:, :].rearrange("p (a c) -> p a c", c=blk)
    sc_b = lambda h: bass.AP(h, 0, [[nblk, 128], [1, nblk], [0, blk]])
    V = vector
    V.reduce_max(t["sc0"][:, :], v3(src), axis=AX, apply_absolute_value=True)
    V.drain()
    V.tensor_scalar(t["sc"][:, :], t["sc0"][:, :], 1.0 / 7.5, 1e-30,
                    ALU.mult, ALU.max)
    V.drain()
    V.reciprocal(t["si"][:, :], t["sc"][:, :])
    V.drain()
    V.tensor_tensor(v3(t["v"]), v3(src), sc_b(t["si"]), ALU.mult)
    V.drain()
    V.tensor_tensor(t["q4"][:, :], t["v"][:, :], t["v"][:, :], ALU.mult)
    V.drain()
    V.tensor_scalar(t["g2"][:, :], t["q4"][:, :], 4.0, None, ALU.is_ge)
    V.tensor_scalar(t["g4"][:, :], t["q4"][:, :], 16.0, None, ALU.is_ge)
    V.drain()
    V.tensor_scalar(t["w1"][:, :], t["g2"][:, :], -4.0, 8.0, ALU.mult, ALU.add)
    V.tensor_scalar(t["sn"][:, :], t["g2"][:, :], -0.125, -0.125,
                    ALU.mult, ALU.add)
    V.drain()
    V.scalar_tensor_tensor(t["w1"][:, :], t["g4"][:, :], -2.0, t["w1"][:, :],
                           ALU.mult, ALU.add)
    V.scalar_tensor_tensor(t["sn"][:, :], t["g4"][:, :], -0.25, t["sn"][:, :],
                           ALU.mult, ALU.add)
    V.drain()
    V.tensor_tensor(t["tt"][:, :], t["v"][:, :], t["w1"][:, :], ALU.mult)
    V.drain()
    V.tensor_scalar(t["z"][:, :], t["tt"][:, :], -1.0, 0.5, ALU.mult, ALU.add)
    V.drain()
    V.tensor_scalar(t["rr"][:, :], t["z"][:, :], MAGIC, MAGIC,
                    ALU.add, ALU.subtract)
    V.drain()
    V.tensor_tensor(t["gt"][:, :], t["rr"][:, :], t["z"][:, :], ALU.is_gt)
    V.drain()
    V.tensor_tensor(t["ff"][:, :], t["rr"][:, :], t["gt"][:, :], ALU.subtract)
    V.drain()
    V.tensor_tensor(t["tt"][:, :], t["ff"][:, :], t["sn"][:, :], ALU.mult)
    V.drain()
    return V.tensor_tensor(v3(dst_bf16), v3(t["tt"]), sc_b(t["sc"]), ALU.mult)


def emit_ln_stats_dve(v, xt, st, j=0):
    F = xt.shape[1]
    h = F // 2
    v.bn_stats(st["stat12"][:, 0:6], xt[:, 0:h])
    v.bn_stats(st["stat12"][:, 6:12], xt[:, h:F])
    v.drain()
    return v.bn_aggr(st["mv"][:, 2 * j:2 * j + 2], st["stat12"][:, :])


def emit_ln_stats_act(sc, st, j=0):
    sc.activation(st["negm"][:, j:j + 1], st["mv"][:, 2 * j:2 * j + 1],
                  AF.Copy, scale=-1.0)
    sc.activation(st["lnv"][:, :], st["mv"][:, 2 * j + 1:2 * j + 2], AF.Ln,
                  bias=st["eps"][:, :1])
    sc.drain()
    return sc.activation(st["r"][:, j:j + 1], st["lnv"][:, :], AF.Exp,
                         scale=-0.5)


def alloc_quant_scratch(nc, pfx="", es=None):
    if es is None:
        mk = lambda n, sh: nc.alloc_sbuf_tensor(n, sh, F32)
    else:
        mk = lambda n, sh: es.enter_context(nc.sbuf_tensor(n, sh, F32))
    t = {k: mk(pfx + "q_" + k, [128, D])
         for k in ["v", "q4", "g2", "g4", "w1", "sn", "tt",
                   "z", "rr", "gt", "ff"]}
    for k in ["sc0", "sc", "si"]:
        t[k] = mk(pfx + "q_" + k, [128, 32])
    return t


def dve_fence(v, dummy=None):
    return v.drain()


def act_fence(sc, dummy=None):
    return sc.drain()



class SemPool:
    def __init__(self, nc):
        self.nc = nc
        self.sems = {}
        self.name_of = {}
        self.base = {}
        self.cur = {}

    def get(self, name):
        if name not in self.sems:
            s = self.nc.alloc_semaphore(name=name)
            self.sems[name] = s
            self.name_of[id(s)] = name
            self.base[name] = 0
        return self.sems[name]

    def begin(self):
        self.cur = {}

    def end(self):
        for n, v in self.cur.items():
            self.base[n] += v
        self.cur = {}

    def note_inc(self, sem, n):
        nm = self.name_of[id(sem)]
        self.cur[nm] = self.cur.get(nm, 0) + n

    def base_of(self, sem):
        return self.base[self.name_of[id(sem)]]

    def abs_total(self, name):
        return self.base.get(name, 0) + self.cur.get(name, 0)


class InstrWrap:
    def __init__(self, instr, pool):
        self._i, self._p = instr, pool

    def then_inc(self, sem, n):
        self._p.note_inc(sem, n)
        r = self._i.then_inc(sem, n)
        return InstrWrap(r, self._p) if hasattr(r, "then_inc") else r

    def __getattr__(self, k):
        return getattr(self._i, k)


class EngWrap:
    def __init__(self, eng, pool):
        self._e, self._p = eng, pool

    def wait_ge(self, sem, v):
        r = self._e.wait_ge(sem, v + self._p.base_of(sem))
        return InstrWrap(r, self._p) if hasattr(r, "then_inc") else r

    def wait_abs(self, sem, v):
        r = self._e.wait_ge(sem, v)
        return InstrWrap(r, self._p) if hasattr(r, "then_inc") else r

    def __getattr__(self, k):
        a = getattr(self._e, k)
        if not callable(a):
            return a

        def call(*args, **kw):
            r = a(*args, **kw)
            return InstrWrap(r, self._p) if hasattr(r, "then_inc") else r
        return call


class BlockWrap:
    ENGS = ("sync", "gpsimd", "vector", "scalar", "tensor")

    def __init__(self, block, pool, gates):
        self._b, self._p, self._g = block, pool, gates
        self._gated = set()

    def _mk(self, engname):
        dec = getattr(self._b, engname)
        pool, gates, gated = self._p, self._g, self._gated

        def deco(fn):
            def inner(eng):
                w = EngWrap(eng, pool)
                if engname not in gated:
                    gated.add(engname)
                    for s, v in gates:
                        w.wait_abs(s, v)
                return fn(w)
            return dec(inner)
        return deco

    @property
    def sync(self):
        return self._mk("sync")

    @property
    def gpsimd(self):
        return self._mk("gpsimd")

    @property
    def vector(self):
        return self._mk("vector")

    @property
    def scalar(self):
        return self._mk("scalar")

    @property
    def tensor(self):
        return self._mk("tensor")



DEBUG_L0FFN = False


def emit_l0ffn(nc, block, pool, pfx, gates, io):
    from contextlib import ExitStack
    es = ExitStack()
    sbt = lambda n, sh, dt: es.enter_context(nc.sbuf_tensor(pfx + n, sh, dt))
    pst = lambda n, sh, dt: es.enter_context(nc.psum_tensor(pfx + n, sh, dt))
    block = BlockWrap(block, pool, gates)
    pool.begin()
    dbg = {}
    x_in, x_out = io["x_in"], io["x_out"]
    w_in, b_in, ident = io["w"], io["b"], io["ident"]

    wr = [sbt(f"wr{j}", [128, D], F32) for j in range(2)]
    wq = sbt("wq", [128, D], BF16)
    wT = sbt("wT", [128, 8 * D], BF16)
    identb = sbt("identb", [128, 128], BF16)
    xt = [sbt(f"xt{j}", [128, D], F32) for j in range(2)]
    xn = sbt("xn", [128, D], F32)
    xq = sbt("xq", [128, D], BF16)
    xfm = sbt("xfm", [128, D], BF16)
    biasb = sbt("biasb", [128, D], F32)
    ty = sbt("ty", [128, 512], F32)
    xo = [sbt(f"xo{j}", [128, D], F32) for j in range(2)]
    t = alloc_quant_scratch(nc, pfx, es)
    st = {k: sbt("s_" + k, [128, 1], F32) for k in ["lnv", "eps"]}
    st["negm"] = sbt("s_negm", [128, 2], F32)
    st["r"] = sbt("s_r", [128, 2], F32)
    st["mv"] = sbt("s_mv", [128, 4], F32)
    st["stat12"] = sbt("s_stat12", [128, 12], F32)
    fv = sbt("fv", [128, 1], F32)
    fa = sbt("fa", [128, 1], F32)
    dbq = sbt("dbq", [128, D], F32)
    dbs = sbt("dbs", [128, 4], F32)

    ptr = pst("ptr", [128, 1024], BF16)
    py = [pst(f"py{j}", [128, 512], F32) for j in range(2)]

    sem = {n: pool.get(n) for n in
           ["ldw0", "ldw1", "ldi", "ldb", "wq_rdy", "wtr", "wev", "ldx0", "ldx1",
            "v2a", "a2v", "xq_rdy", "xtr", "xev", "mm", "ep", "so0", "so1",
            "cst",
            "dbg", "dbd", "mvr"]}

    ptr3 = bass.AP(ptr, 0, [[1024, 128], [128, 8], [1, 128]])

    if True:
        @block.gpsimd
        def _(g):
            g.dma_start(out=identb[:, :], in_=ident[:, :]).then_inc(sem["ldi"], 16)
            for k in range(8):
                if k >= 2:
                    g.wait_ge(sem["wq_rdy"], k - 1)
                g.dma_start(out=wr[k % 2][:, :],
                            in_=w_in[k * 128:(k + 1) * 128, :]).then_inc(
                                sem[f"ldw{k % 2}"], 16)

        @block.sync
        def _(s):
            s.dma_start(out=biasb[:, :], in_=bcast_row(b_in, 0, D)).then_inc(
                sem["ldb"], 16)
            for i in range(NT128):
                if i >= 2:
                    s.wait_ge(sem["ep"], i - 1)
                s.dma_start(out=xt[i % 2][:, :],
                            in_=x_in[i * 128:(i + 1) * 128, :]).then_inc(
                                sem[f"ldx{i % 2}"], 16)
                if i >= 1:
                    s.wait_ge(sem["ep"], i)
                    s.dma_start(out=x_out[(i - 1) * 128:i * 128, :],
                                in_=xo[(i - 1) % 2][:, :]).then_inc(
                                    sem[f"so{(i - 1) % 2}"], 16)
            s.wait_ge(sem["ep"], NT128)
            s.dma_start(out=x_out[(NT128 - 1) * 128:, :],
                        in_=xo[(NT128 - 1) % 2][:, :]).then_inc(
                            sem[f"so{(NT128 - 1) % 2}"], 16)
            s.wait_ge(sem["so0"], 16 * (NT128 // 2))
            s.wait_ge(sem["so1"], 16 * (NT128 // 2))

        @block.scalar
        def _(sc):
            sc.wait_ge(sem["cst"], 1)
            for i in range(NT128):
                sc.wait_ge(sem["mvr"], i + 1)
                emit_ln_stats_act(sc, st, j=i % 2)
                act_fence(sc).then_inc(sem["a2v"], 1)

        def emit_ep(v, j):
            if j == 0:
                v.wait_ge(sem["ldb"], 16)
            if j >= 2:
                v.wait_ge(sem[f"so{j % 2}"], 16 * (j // 2))
            for h in range(2):
                v.wait_ge(sem["mm"], 2 * j + h + 1)
                v.tensor_tensor(ty[:, :], py[h][:, :],
                                biasb[:, h * 512:(h + 1) * 512], ALU.add)
                v.drain()
                v.scalar_tensor_tensor(
                    xo[j % 2][:, h * 512:(h + 1) * 512], ty[:, :], 0.5,
                    xt[j % 2][:, h * 512:(h + 1) * 512], ALU.mult, ALU.add)
                v.drain()
            dve_fence(v).then_inc(sem["ep"], 1)

        @block.vector
        def _(v):
            v.memset(st["eps"][:, :], 1e-5).then_inc(sem["cst"], 1)
            for k in range(8):
                v.wait_ge(sem[f"ldw{k % 2}"], 16 * (k // 2 + 1))
                emit_quant_dve(v, wr[k % 2], wq, t)
                dve_fence(v).then_inc(sem["wq_rdy"], 1)
                v.wait_ge(sem["wtr"], k + 1)
                v.tensor_copy(
                    bass.AP(wT, k * 128, [[8 * D, 128], [1024, 8], [1, 128]]),
                    ptr3)
                dve_fence(v).then_inc(sem["wev"], 1)
            for i in range(NT128):
                v.wait_ge(sem[f"ldx{i % 2}"], 16 * (i // 2 + 1))
                xc = xt[i % 2]
                if i >= 2:
                    v.wait_ge(sem["a2v"], i - 1)
                emit_ln_stats_dve(v, xc, st, j=i % 2)
                dve_fence(v).then_inc(sem["mvr"], 1)
                v.wait_ge(sem["a2v"], i + 1)
                if DEBUG_L0FFN and i >= 1:
                    v.wait_ge(sem["dbd"], 48 * i)
                v.tensor_scalar(xn[:, :], xc[:, :],
                                st["negm"][:, i % 2:i % 2 + 1],
                                st["r"][:, i % 2:i % 2 + 1],
                                ALU.add, ALU.mult)
                v.drain()
                emit_quant_dve(v, xn, xq, t)
                if DEBUG_L0FFN:
                    v.tensor_copy(dbq[:, :], xq[:, :])
                    v.tensor_copy(dbs[:, 0:1], st["negm"][:, i % 2:i % 2 + 1])
                    v.tensor_copy(dbs[:, 1:2], st["r"][:, i % 2:i % 2 + 1])
                    v.tensor_copy(dbs[:, 2:4], st["mv"][:, 2*(i%2):2*(i%2)+2])
                dve_fence(v).then_inc(sem["xq_rdy"], 1)
                if DEBUG_L0FFN:
                    v.tensor_copy(dbq[:, 0:1], dbq[:, 0:1]).then_inc(
                        sem["dbg"], 1)
                if i >= 1:
                    emit_ep(v, i - 1)
                v.wait_ge(sem["xtr"], i + 1)
                v.tensor_copy(xfm[:, :], ptr[:, :])
                dve_fence(v).then_inc(sem["xev"], 1)
            emit_ep(v, NT128 - 1)

        @block.tensor
        def _(te):
            te.wait_ge(sem["ldi"], 16)
            for k in range(8):
                te.wait_ge(sem["wq_rdy"], k + 1)
                if k >= 1:
                    te.wait_ge(sem["wev"], k)
                for c in range(8):
                    mm = te.transpose(ptr[:, c * 128:(c + 1) * 128],
                                      wq[:, c * 128:(c + 1) * 128],
                                      identb[:, :])
                mm.then_inc(sem["wtr"], 1)
            for i in range(NT128):
                te.wait_ge(sem["xq_rdy"], i + 1)
                te.wait_ge(sem["wev"], 8) if i == 0 else te.wait_ge(
                    sem["xev"], i)
                for c in range(8):
                    mm = te.transpose(ptr[:, c * 128:(c + 1) * 128],
                                      xq[:, c * 128:(c + 1) * 128],
                                      identb[:, :])
                mm.then_inc(sem["xtr"], 1)
                te.wait_ge(sem["xev"], i + 1)
                if i > 0:
                    te.wait_ge(sem["ep"], i)
                for h in range(2):
                    for c in range(8):
                        mm = te.matmul(py[h][:, :],
                                       xfm[:, c * 128:(c + 1) * 128],
                                       wT[:, c * 1024 + h * 512:
                                          c * 1024 + (h + 1) * 512],
                                       start=(c == 0), stop=(c == 7))
                    mm.then_inc(sem["mm"], 1)

        if DEBUG_L0FFN:
            @block.gpsimd
            def _(g):
                for i in range(NT128):
                    g.wait_ge(sem["dbg"], i + 1)
                    g.dma_start(out=dbg["xn"][i * 128:(i + 1) * 128, :],
                                in_=xn[:, :]).then_inc(sem["dbd"], 16)
                    g.dma_start(out=dbg["xq"][i * 128:(i + 1) * 128, :],
                                in_=dbq[:, :]).then_inc(sem["dbd"], 16)
                    g.dma_start(out=dbg["st"].ap()[i, :, :],
                                in_=dbs[:, :]).then_inc(sem["dbd"], 16)
                g.wait_ge(sem["wev"], 8)
                g.dma_start(out=dbg["wT"][:, :], in_=wT[:, :]).then_inc(
                    sem["dbd"], 16)
                g.wait_ge(sem["dbd"], 16 * (3 * NT128 + 1))

    es.close()
    pool.end()
    return [(sem["so0"], pool.base["so0"]), (sem["so1"], pool.base["so1"])]


def build_l0ffn():
    nc = bass.Bass("TRN2", target_bir_lowering=False)
    io = {
        "x_in": nc.dram_tensor("x_in", [L, D], F32, kind="ExternalInput"),
        "w": nc.dram_tensor("w", [D, D], F32, kind="ExternalInput"),
        "b": nc.dram_tensor("b", [D], F32, kind="ExternalInput"),
        "ident": nc.dram_tensor("ident", [128, 128], F32,
                                kind="ExternalInput"),
        "x_out": nc.dram_tensor("x_out", [L, D], F32, kind="ExternalOutput"),
    }
    pool = SemPool(nc)
    with nc.Block() as block:
        emit_l0ffn(nc, block, pool, "", [], io)
    return nc




from contextlib import ExitStack

TC = 256
NCH = L // TC
T5 = 512
NC5 = L // T5


def emit_mamba(nc, block, pool, pfx, gates, io, v1):
    es = ExitStack()
    sbt = lambda *a: es.enter_context(nc.sbuf_tensor(*a))
    block = BlockWrap(block, pool, gates)
    pool.begin()
    DI = DIN
    NDC = DI // 128
    x_in = io["x_in"]
    w_proj = io["w_proj"]
    conv_w = io["conv_w"]
    conv_b = io["conv_b"]
    wx_d, wx_B, wx_C = io["wx_d"], io["wx_B"], io["wx_C"]
    w_dt, b_dt = io["w_dt"], io["b_dt"]
    a_log, d_vec, w_out = io["a_log"], io["d_vec"], io["w_out"]
    ident = io["ident"]
    if v1:
        ln_w, ln_b = io["ln_w"], io["ln_b"]
        in_b, out_b = io["in_b"], io["out_b"]
    x_out = io["x_out"]

    upre_h = nc.dram_tensor(pfx + "upre_h", [DI, L + 4], F32 if v1 else BF16)
    res_h = nc.dram_tensor(pfx + "res_h", [DI, L], F32)
    u_h = nc.dram_tensor(pfx + "u_h", [DI, L], F32)
    del_h = nc.dram_tensor(pfx + "del_h", [DI, L], F32)
    bc_h = nc.dram_tensor(pfx + "bc_h", [2 * NST, L], F32)
    y_h = nc.dram_tensor(pfx + "y_h", [DI, L], F32)
    hid_h = nc.dram_tensor(pfx + "hid_h", [L, D], F32) if v1 else None

    def fmdc(dram, dc, col0, tlen, row0=0, width=L):
        return bass.AP(dram, (row0 + dc * 128) * width + col0,
                       [[width, 128], [1, tlen]])

    def fm_dma(eng, dram, col0, tlen, sbuf, seg, s_sem, ndc=NDC, row0=0,
               store=False, width=L):
        for dc in range(ndc):
            sb = sbuf[:, dc * seg:dc * seg + tlen]
            dr = fmdc(dram, dc, col0, tlen, row0, width)
            if store:
                eng.dma_start(out=dr, in_=sb).then_inc(s_sem, 16)
            else:
                eng.dma_start(out=sb, in_=dr).then_inc(s_sem, 16)

    identb = sbt(pfx + "identb", [128, 128], BF16)
    a_sb = sbt(pfx + "a_sb", [128, NDC * NST], F32)
    dcol = sbt(pfx + "dcol", [128, NDC], F32)
    bdtcol = sbt(pfx + "bdtcol", [128, NDC], F32)
    cbcol = sbt(pfx + "cbcol", [128, NDC], F32)
    wdt_sb = sbt(pfx + "wdt_sb", [RDT, DI], BF16)
    wxd_sb = sbt(pfx + "wxd_sb", [128, NDC * RDT], BF16)
    wxB_sb = sbt(pfx + "wxB_sb", [128, NDC * NST], BF16)
    wxC_sb = sbt(pfx + "wxC_sb", [128, NDC * NST], BF16)
    rf_sb = sbt(pfx + "rf_sb", [128, NDC], F32)
    rft = sbt(pfx + "rft", [128, 1], F32)
    carry = sbt(pfx + "carry", [128, NDC * NST], F32)
    st = {k: sbt(pfx + "s_" + k, [128, 1], F32)
          for k in ["lnv", "eps"]}
    st["negm"] = sbt(pfx + "s_negm", [128, 4], F32)
    st["r"] = sbt(pfx + "s_r", [128, 4], F32)
    st["mv"] = sbt(pfx + "s_mv", [128, 8], F32)
    st["stat12"] = sbt(pfx + "s_stat12", [128, 12], F32)
    zpad4 = sbt(pfx + "zpad4", [128, 4], F32 if v1 else BF16)
    if v1:
        inbcol = sbt(pfx + "inbcol", [128, 2 * NDC], F32)
        cw2col = sbt(pfx + "cw2col", [128, NDC * 4], F32)
    pLN = ExitStack()
    if v1:
        lnw_b = pLN.enter_context(nc.sbuf_tensor(pfx + "lnw_b", [128, D], F32))
        lnb_b = pLN.enter_context(nc.sbuf_tensor(pfx + "lnb_b", [128, D], F32))

    sem = {n: pool.get(n) for n in [
        "ldc", "ldw1",
        "ldx0", "ldx1", "unr", "mvr", "a2v", "sqr", "tr1", "rir", "mm1", "ubr",
        "spre0", "spre1", "sres0", "sres1", "shid",
        "cwld", "upld0", "upld1", "mmsc", "scf", "mmh", "su0", "su1",
        "uev2", "zp", "zpd",
        "ubld0", "ubld1", "hev", "rcr", "spa", "dtmm", "spl", "dcons",
        "dstgr", "sdel0", "sdel1", "sbc0", "sbc1", "spa3",
        "ld4", "ld4p0", "ld4p1", "p4a", "p4c", "p4y", "vdone",
        "sy0", "sy1", "sy2", "sy3",
        "ldw5", "ld5a", "ld5b", "s5a", "ygr", "ymm", "p5e", "ep5", "so5",
    ]}

    if True:
        @block.gpsimd
        def _(g):
            g.dma_start(out=identb[:, :], in_=ident[:, :]).then_inc(sem["ldc"], 16)
            g.dma_start(out=wdt_sb[:, :], in_=w_dt[:, :]).then_inc(sem["ldc"], 16)
            for dc in range(NDC):
                g.dma_start(out=wxd_sb[:, dc * RDT:(dc + 1) * RDT],
                            in_=wx_d[dc * 128:(dc + 1) * 128, :]).then_inc(
                                sem["ldc"], 16)
                g.dma_start(out=wxB_sb[:, dc * NST:(dc + 1) * NST],
                            in_=wx_B[dc * 128:(dc + 1) * 128, :]).then_inc(
                                sem["ldc"], 16)
                g.dma_start(out=wxC_sb[:, dc * NST:(dc + 1) * NST],
                            in_=wx_C[dc * 128:(dc + 1) * 128, :]).then_inc(
                                sem["ldc"], 16)
                g.dma_start(out=a_sb[:, dc * NST:(dc + 1) * NST],
                            in_=a_log[dc * 128:(dc + 1) * 128, :]).then_inc(
                                sem["ldc"], 16)
                g.dma_start(out=dcol[:, dc:dc + 1],
                            in_=bass.AP(d_vec, dc * 128, [[1, 128], [1, 1]])
                            ).then_inc(sem["ldc"], 16)
                g.dma_start(out=bdtcol[:, dc:dc + 1],
                            in_=bass.AP(b_dt, dc * 128, [[1, 128], [1, 1]])
                            ).then_inc(sem["ldc"], 16)
                g.dma_start(out=cbcol[:, dc:dc + 1],
                            in_=bass.AP(conv_b, dc * 128, [[1, 128], [1, 1]])
                            ).then_inc(sem["ldc"], 16)
                if v1:
                    g.dma_start(out=cw2col[:, dc * 4:(dc + 1) * 4],
                                in_=conv_w[dc * 128:(dc + 1) * 128, :]
                                ).then_inc(sem["ldc"], 16)
            if v1:
                g.dma_start(out=lnw_b[:, :], in_=bcast_row(ln_w, 0, D)
                            ).then_inc(sem["ldc"], 16)
                g.dma_start(out=lnb_b[:, :], in_=bcast_row(ln_b, 0, D)
                            ).then_inc(sem["ldc"], 16)
                for h in range(2 * NDC):
                    g.dma_start(out=inbcol[:, h:h + 1],
                                in_=bass.AP(in_b, h * 128, [[1, 128], [1, 1]])
                                ).then_inc(sem["ldc"], 16)

        NLDC = 16 * (2 + NDC * (8 if v1 else 7) + ((2 + 2 * NDC) if v1 else 0))

        @block.sync
        def _(s):
            s.wait_ge(sem["zp"], 1)
            for dc in range(NDC):
                s.dma_start(out=fmdc(upre_h, dc, 0, 4, width=L + 4),
                            in_=zpad4[:, :]).then_inc(sem["zpd"], 16)

        @block.vector
        def _(v):
            v.memset(st["eps"][:, :], 1e-5)
            v.memset(carry[:, :], 0.0)
            v.memset(zpad4[:, :], 0.0)
            v.drain().then_inc(sem["zp"], 1)
            v.wait_ge(sem["ldc"], NLDC)
            v.drain().then_inc(sem["unr"], 1)

        @block.scalar
        def _(sc):
            sc.wait_ge(sem["unr"], 1)
            sc.activation(a_sb[:, :], a_sb[:, :], AF.Exp, scale=1.0)
            sc.drain().then_inc(sem["a2v"], 1)

        @block.vector
        def _(v):
            v.wait_ge(sem["a2v"], 1)
            v.tensor_scalar(a_sb[:, :], a_sb[:, :], -1.0, None, ALU.mult)
            v.drain().then_inc(sem["unr"], 1)

        MVR0, A2V0, VP10 = 0, 1, 2
        APT = 2 if v1 else 1

        _sid = nc.enter_named_scope(pfx + "p1", False)[0]
        p1 = ExitStack()
        wproj_sb = [p1.enter_context(
            nc.sbuf_tensor(pfx + f"wp{dc}", [128, 2 * DI], BF16)) for dc in range(8)]
        xt = [p1.enter_context(nc.sbuf_tensor(pfx + f"xt{j}", [128, D], F32))
              for j in range(2)]
        hidt = p1.enter_context(nc.sbuf_tensor(pfx + "hidt", [128, D], F32))
        hnt = p1.enter_context(nc.sbuf_tensor(pfx + "hnt", [128, D], F32))
        xnb = p1.enter_context(nc.sbuf_tensor(pfx + "sqr", [128, D], BF16))
        xfmb = p1.enter_context(nc.sbuf_tensor(pfx + "xfmb", [128, 8 * T5], BF16))
        NSB = 1 if v1 else 2
        ustg = [p1.enter_context(nc.sbuf_tensor(pfx + f"ustg{j}",
                                                [128, NDC * T5],
                                                F32 if v1 else BF16))
                for j in range(NSB)]
        rstg = [p1.enter_context(nc.sbuf_tensor(pfx + f"rstg{j}",
                                                [128, NDC * T5], F32))
                for j in range(NSB)]
        ptr = p1.enter_context(nc.psum_tensor(pfx + "ptr", [128, 1024], BF16))
        pmm = [p1.enter_context(nc.psum_tensor(pfx + f"pmm{j}", [128, T5], F32))
               for j in range(2)]
        ptr3 = bass.AP(ptr, 0, [[1024, 128], [128, 8], [1, 128]])

        @block.gpsimd
        def _(g):
            for dc in range(8):
                g.dma_start(out=wproj_sb[dc][:, :],
                            in_=w_proj[dc * 128:(dc + 1) * 128, :]).then_inc(
                                sem["ldw1"], 16)
            for tt in range(NC5):
                if v1:
                    for q in range(4):
                        i = 4 * tt + q
                        g.wait_ge(sem["unr"], VP10 + i + 1)
                        g.dma_start(out=hid_h[i * 128:(i + 1) * 128, :],
                                    in_=hidt[:, :]).then_inc(sem["shid"], 16)
                g.wait_ge(sem["ubr"], tt + 1)
                fm_dma(g, upre_h, tt * T5 + 4, T5, ustg[tt % NSB], T5,
                       sem[f"spre{tt % 2}"], store=True, width=L + 4)
                fm_dma(g, res_h, tt * T5, T5, rstg[tt % NSB], T5,
                       sem[f"sres{tt % 2}"], store=True)

        @block.sync
        def _(s):
            for i in range(NT128):
                if i >= 2:
                    s.wait_ge(sem["unr"], VP10 + i - 1)
                s.dma_start(out=xt[i % 2][:, :],
                            in_=x_in[i * 128:(i + 1) * 128, :]).then_inc(
                                sem[f"ldx{i % 2}"], 16)

        @block.scalar
        def _(sc):
            for tt in range(NC5):
                for q in range(4):
                    i = 4 * tt + q
                    sc.wait_ge(sem["mvr"], MVR0 + APT * i + 1)
                    emit_ln_stats_act(sc, st, j=i % 2)
                    act_fence(sc).then_inc(sem["a2v"], 1)
                    if v1:
                        sc.wait_ge(sem["mvr"], MVR0 + 2 * i + 2)
                        emit_ln_stats_act(sc, st, j=2 + i % 2)
                        act_fence(sc).then_inc(sem["a2v"], 1)
                if tt >= NSB:
                    j_ = tt - NSB
                    sc.wait_ge(sem[f"spre{j_ % 2}"],
                               16 * NDC * (j_ // 2 + 1))
                    sc.wait_ge(sem[f"sres{j_ % 2}"],
                               16 * NDC * (j_ // 2 + 1))
                for oc in range(2 * NDC):
                    sc.wait_ge(sem["mm1"], 32 * tt + oc + 1)
                    dst = (ustg[tt % NSB][:, oc * T5:(oc + 1) * T5]
                           if oc < NDC else
                           rstg[tt % NSB][:, (oc - NDC) * T5:
                                          (oc - NDC + 1) * T5])
                    if v1:
                        sc.activation(dst, pmm[oc % 2][:, :], AF.Identity,
                                      bias=inbcol[:, oc:oc + 1])
                    else:
                        sc.activation(dst, pmm[oc % 2][:, :], AF.Copy)
                    sc.drain().then_inc(sem["uev2"], 1)
                sc.drain().then_inc(sem["ubr"], 1)

        @block.vector
        def _(v):
            for i in range(NT128):
                tt, q = divmod(i, 4)
                v.wait_ge(sem[f"ldx{i % 2}"], 16 * (i // 2 + 1))
                xc = xt[i % 2]
                if i >= 2:
                    v.wait_ge(sem["a2v"], A2V0 + APT * (i - 1))
                emit_ln_stats_dve(v, xc, st, j=i % 2)
                dve_fence(v).then_inc(sem["mvr"], 1)
                v.wait_ge(sem["a2v"], A2V0 + APT * i + 1)
                if v1:
                    if i >= 1:
                        v.wait_ge(sem["shid"], 16 * i)
                    v.tensor_scalar(hidt[:, :], xc[:, :],
                                    st["negm"][:, i % 2:i % 2 + 1],
                                    st["r"][:, i % 2:i % 2 + 1],
                                    ALU.add, ALU.mult)
                    dve_fence(v).then_inc(sem["unr"], 1)
                    emit_ln_stats_dve(v, hidt, st, j=2 + i % 2)
                    dve_fence(v).then_inc(sem["mvr"], 1)
                    v.wait_ge(sem["a2v"], A2V0 + 2 * i + 2)
                    v.tensor_scalar(hnt[:, :], hidt[:, :],
                                    st["negm"][:, 2 + i % 2:3 + i % 2],
                                    st["r"][:, 2 + i % 2:3 + i % 2],
                                    ALU.add, ALU.mult)
                    v.drain()
                    v.tensor_tensor(hnt[:, :], hnt[:, :], lnw_b[:, :], ALU.mult)
                    v.drain()
                    v.tensor_tensor(xnb[:, :], hnt[:, :], lnb_b[:, :], ALU.add)
                else:
                    v.tensor_scalar(xnb[:, :], xc[:, :],
                                    st["negm"][:, i % 2:i % 2 + 1],
                                    st["r"][:, i % 2:i % 2 + 1],
                                    ALU.add, ALU.mult)
                    dve_fence(v).then_inc(sem["unr"], 1)
                dve_fence(v).then_inc(sem["sqr"], 1)
                v.wait_ge(sem["tr1"], i + 1)
                if i >= 4:
                    v.wait_ge(sem["mm1"], 32 * (i // 4))
                v.tensor_copy(
                    bass.AP(xfmb, q * 128,
                            [[8 * T5, 128], [T5, 8], [1, 128]]), ptr3)
                dve_fence(v).then_inc(sem["rir"], 1)

        @block.tensor
        def _(te):
            te.wait_ge(sem["ldc"], NLDC)
            te.wait_ge(sem["ldw1"], 16 * 8)
            for tt in range(NC5):
                for q in range(4):
                    i = 4 * tt + q
                    te.wait_ge(sem["sqr"], i + 1)
                    if i > 0:
                        te.wait_ge(sem["rir"], i)
                    for c in range(8):
                        mm = te.transpose(ptr[:, c * 128:(c + 1) * 128],
                                          xnb[:, c * 128:(c + 1) * 128],
                                          identb[:, :])
                    mm.then_inc(sem["tr1"], 1)
                te.wait_ge(sem["rir"], 4 * (tt + 1))
                for oc in range(2 * NDC):
                    gidx = 32 * tt + oc
                    if gidx >= 2:
                        te.wait_ge(sem["uev2"], gidx - 1)
                    for dc in range(8):
                        mm = te.matmul(pmm[oc % 2][:, :],
                                       wproj_sb[dc][:, oc * 128:(oc + 1) * 128],
                                       xfmb[:, dc * T5:(dc + 1) * T5],
                                       start=(dc == 0), stop=(dc == 7))
                    mm.then_inc(sem["mm1"], 1)

        p1.close()
        pLN.close()
        nc.leave_named_scope(pfx + "p1", _sid, False)

        _sid = nc.enter_named_scope(pfx + "p2", False)[0]
        p2 = ExitStack()
        P1S = 16 * NDC * (NC5 // 2)
        UPW = 16 * NDC * (T5 + 4)
        if v1:
            up2 = [p2.enter_context(nc.sbuf_tensor(
                pfx + f"up2_{j}", [128, NDC * (T5 + 4)], F32)) for j in range(2)]
            cacc = [p2.enter_context(nc.sbuf_tensor(
                pfx + f"cacc{j}", [128, T5], F32)) for j in range(2)]
            tapa = p2.enter_context(nc.sbuf_tensor(pfx + "tapa", [128, T5], F32))
            tapb = p2.enter_context(nc.sbuf_tensor(pfx + "tapb", [128, T5], F32))
            ustg2 = [p2.enter_context(nc.sbuf_tensor(
                pfx + f"ustg2_{j}", [128, NDC * T5], F32)) for j in range(2)]

            @block.sync
            def _(s):
                for cc in range(NC5):
                    if cc == 0:
                        s.wait_ge(sem["spre0"], P1S)
                        s.wait_ge(sem["spre1"], P1S)
                        s.wait_ge(sem["sres0"], P1S)
                        s.wait_ge(sem["sres1"], P1S)
                        s.wait_ge(sem["shid"], 16 * NT128)
                    if cc >= 2:
                        s.wait_ge(sem["mmh"], NDC * (cc - 1))
                    buf = up2[cc % 2]
                    if cc == 0:
                        s.wait_ge(sem["zpd"], 16 * NDC)
                    fm_dma(s, upre_h, cc * T5, T5 + 4, buf, T5 + 4,
                           sem[f"upld{cc % 2}"], width=L + 4)

            @block.vector
            def _(v):
                v.wait_ge(sem["spre0"], P1S)
                v.wait_ge(sem["spre1"], P1S)
                v.wait_ge(sem["sres0"], P1S)
                v.wait_ge(sem["sres1"], P1S)
                v.wait_ge(sem["shid"], 16 * NT128)
                v.drain().then_inc(sem["mmsc"], 1)
                for cc in range(NC5):
                    buf = up2[cc % 2]
                    v.wait_ge(sem[f"upld{cc % 2}"], 16 * NDC * (cc // 2 + 1))
                    for dc in range(NDC):
                        idx = NDC * cc + dc
                        if idx >= 2:
                            v.wait_ge(sem["scf"], idx - 1)
                        base = dc * (T5 + 4)
                        ca = cacc[dc % 2]
                        v.tensor_scalar(tapa[:, :],
                                        buf[:, base + 1:base + 1 + T5],
                                        cw2col[:, dc * 4:dc * 4 + 1], None,
                                        ALU.mult)
                        v.tensor_scalar(tapb[:, :],
                                        buf[:, base + 2:base + 2 + T5],
                                        cw2col[:, dc * 4 + 1:dc * 4 + 2], None,
                                        ALU.mult)
                        v.drain()
                        v.scalar_tensor_tensor(
                            tapa[:, :], buf[:, base + 3:base + 3 + T5],
                            cw2col[:, dc * 4 + 2:dc * 4 + 3], tapa[:, :],
                            ALU.mult, ALU.add)
                        v.scalar_tensor_tensor(
                            tapb[:, :], buf[:, base + 4:base + 4 + T5],
                            cw2col[:, dc * 4 + 3:dc * 4 + 4], tapb[:, :],
                            ALU.mult, ALU.add)
                        v.drain()
                        v.tensor_tensor(ca[:, :], tapa[:, :], tapb[:, :],
                                        ALU.add)
                        v.drain().then_inc(sem["mmh"], 1)

            @block.scalar
            def _(sc):
                for cc in range(NC5):
                    if cc >= 2:
                        sc.wait_ge(sem[f"su{cc % 2}"], 16 * NDC * (cc // 2))
                    for dc in range(NDC):
                        idx = NDC * cc + dc
                        sc.wait_ge(sem["mmh"], idx + 1)
                        sc.activation(ustg2[cc % 2][:, dc * T5:(dc + 1) * T5],
                                      cacc[dc % 2][:, :], AF.Silu,
                                      bias=cbcol[:, dc:dc + 1])
                        sc.drain().then_inc(sem["scf"], 1)
                    sc.drain()
                    fm_dma(sc, u_h, cc * T5, T5, ustg2[cc % 2], T5,
                           sem[f"su{cc % 2}"], store=True)
            SU_T = 16 * NDC * (NC5 // 2)
        else:
            cw_sb = [p2.enter_context(nc.sbuf_tensor(
                pfx + f"cw{t_}_{ic}", [128, 8 * 128], BF16))
                for t_ in range(4) for ic in range(NDC)]
            up2 = [p2.enter_context(nc.sbuf_tensor(
                pfx + f"up2_{j}", [128, NDC * (T5 + 4)], BF16)) for j in range(2)]
            ustg2 = [p2.enter_context(nc.sbuf_tensor(
                pfx + f"ustg2_{j}", [128, 8 * T5], F32)) for j in range(2)]
            pcv = [p2.enter_context(nc.psum_tensor(pfx + f"pcv{j}", [128, T5], F32))
                   for j in range(2)]

            @block.gpsimd
            def _(g):
                for h2 in range(2):
                    if h2 == 0:
                        for p_ in range(2):
                            g.wait_ge(sem[f"spre{p_}"], P1S)
                            g.wait_ge(sem[f"sres{p_}"], P1S)
                    else:
                        g.wait_ge(sem["mmsc"], 1 + 64)
                    for t_ in range(4):
                        for ic in range(NDC):
                            g.dma_start(
                                out=cw_sb[t_ * NDC + ic][:, :],
                                in_=conv_w.ap()[t_, ic * 128:(ic + 1) * 128,
                                                h2 * 1024:(h2 + 1) * 1024]
                                ).then_inc(sem["cwld"], 16)

            @block.sync
            def _(s):
                for h2 in range(2):
                    for cc in range(NC5):
                        idx = h2 * NC5 + cc
                        if idx == 0:
                            for p_ in range(2):
                                s.wait_ge(sem[f"spre{p_}"], P1S)
                                s.wait_ge(sem[f"sres{p_}"], P1S)
                        if idx >= 2:
                            s.wait_ge(sem["mmsc"], 1 + 8 * (idx - 1))
                        buf = up2[idx % 2]
                        if idx == 0:
                            s.wait_ge(sem["zpd"], 16 * NDC)
                        fm_dma(s, upre_h, cc * T5, T5 + 4, buf[:, :],
                               T5 + 4, sem[f"upld{idx % 2}"], width=L + 4)

            @block.vector
            def _(v):
                v.wait_ge(sem["spre0"], P1S)
                v.wait_ge(sem["spre1"], P1S)
                v.wait_ge(sem["sres0"], P1S)
                v.wait_ge(sem["sres1"], P1S)
                v.drain().then_inc(sem["mmsc"], 1)

            @block.tensor
            def _(te):
                te.wait_ge(sem["mmsc"], 1)
                for h2 in range(2):
                    te.wait_ge(sem["cwld"], 16 * 64 * (h2 + 1))
                    for cc in range(NC5):
                        idx = h2 * NC5 + cc
                        te.wait_ge(sem[f"upld{idx % 2}"],
                                   16 * NDC * (idx // 2 + 1))
                        buf = up2[idx % 2]
                        for oc8 in range(8):
                            gidx = 8 * idx + oc8
                            if gidx >= 2:
                                te.wait_ge(sem["scf"], gidx - 1)
                            first = True
                            for t_ in range(4):
                                for ic in range(NDC):
                                    mm = te.matmul(
                                        pcv[oc8 % 2][:, :],
                                        cw_sb[t_ * NDC + ic][:,
                                                             oc8 * 128:
                                                             (oc8 + 1) * 128],
                                        buf[:, ic * (T5 + 4) + 1 + t_:
                                            ic * (T5 + 4) + 1 + t_ + T5],
                                        start=first,
                                        stop=(t_ == 3 and ic == NDC - 1))
                                    first = False
                            mm.then_inc(sem["mmsc"], 1)

            @block.scalar
            def _(sc):
                for h2 in range(2):
                    for cc in range(NC5):
                        idx = h2 * NC5 + cc
                        if idx >= 2:
                            sc.wait_ge(sem[f"su{idx % 2}"], 16 * 8 * (idx // 2))
                        for oc8 in range(8):
                            gidx = 8 * idx + oc8
                            sc.wait_ge(sem["mmsc"], 1 + gidx + 1)
                            oc = h2 * 8 + oc8
                            sc.activation(
                                ustg2[idx % 2][:, oc8 * T5:(oc8 + 1) * T5],
                                pcv[oc8 % 2][:, :], AF.Silu,
                                bias=cbcol[:, oc:oc + 1])
                            sc.drain().then_inc(sem["scf"], 1)
                        sc.drain()
                        fm_dma(sc, u_h, cc * T5, T5, ustg2[idx % 2], T5,
                               sem[f"su{idx % 2}"], ndc=8, row0=h2 * 1024,
                               store=True)
            SU_T = 16 * 8 * NC5

        p2.close()
        nc.leave_named_scope(pfx + "p2", _sid, False)

        _sid = nc.enter_named_scope(pfx + "p3", False)[0]
        p3 = ExitStack()
        ub3 = [p3.enter_context(nc.sbuf_tensor(pfx + f"ub3_{j}", [128, NDC * T5],
                                               BF16)) for j in range(2)]
        draw = p3.enter_context(nc.sbuf_tensor(pfx + "draw", [RDT, T5], BF16))
        bcstB = [p3.enter_context(nc.sbuf_tensor(pfx + f"bcstB{j}", [NST, T5], F32))
                 for j in range(2)]
        bcstC = [p3.enter_context(nc.sbuf_tensor(pfx + f"bcstC{j}", [NST, T5], F32))
                 for j in range(2)]
        dstg = [p3.enter_context(nc.sbuf_tensor(pfx + f"dstg{j}", [128, NDC * T5],
                                                F32)) for j in range(2)]
        sp1 = [p3.enter_context(nc.sbuf_tensor(pfx + f"sp1_{j}", [128, T5], F32))
               for j in range(2)]
        sp2 = [p3.enter_context(nc.sbuf_tensor(pfx + f"sp2_{j}", [128, T5], F32))
               for j in range(2)]
        pxd = p3.enter_context(nc.psum_tensor(pfx + "pxd", [64, T5], F32))
        pxb = p3.enter_context(nc.psum_tensor(pfx + "pxb", [16, T5], F32))
        pxc = p3.enter_context(nc.psum_tensor(pfx + "pxc", [16, T5], F32))
        pdt = [p3.enter_context(nc.psum_tensor(pfx + f"pdt{j}", [128, T5], F32))
               for j in range(2)]

        @block.gpsimd
        def _(g):
            for cc in range(NC5):
                if cc == 0:
                    g.wait_ge(sem["su0"], SU_T)
                    g.wait_ge(sem["su1"], SU_T)
                if cc >= 2:
                    g.wait_ge(sem["hev"], cc - 1)
                fm_dma(g, u_h, cc * T5, T5, ub3[cc % 2], T5,
                       sem[f"ubld{cc % 2}"])

        @block.tensor
        def _(te):
            for cc in range(NC5):
                te.wait_ge(sem[f"ubld{cc % 2}"], 16 * NDC * (cc // 2 + 1))
                if cc >= 1:
                    te.wait_ge(sem["spa3"], cc)
                ub = ub3[cc % 2]
                for dc in range(NDC):
                    te.matmul(pxd[:, :], wxd_sb[:, dc * RDT:(dc + 1) * RDT],
                              ub[:, dc * T5:(dc + 1) * T5],
                              start=(dc == 0), stop=(dc == NDC - 1))
                for dc in range(NDC):
                    te.matmul(pxb[:, :], wxB_sb[:, dc * NST:(dc + 1) * NST],
                              ub[:, dc * T5:(dc + 1) * T5],
                              start=(dc == 0), stop=(dc == NDC - 1))
                for dc in range(NDC):
                    mm = te.matmul(pxc[:, :],
                                   wxC_sb[:, dc * NST:(dc + 1) * NST],
                                   ub[:, dc * T5:(dc + 1) * T5],
                                   start=(dc == 0), stop=(dc == NDC - 1))
                mm.then_inc(sem["hev"], 1)
                te.wait_ge(sem["rcr"], cc + 1)
                for dc in range(NDC):
                    idx = NDC * cc + dc
                    if idx >= 2:
                        te.wait_ge(sem["spl"], idx - 1)
                    te.matmul(pdt[dc % 2][:, :],
                              wdt_sb[:, dc * 128:(dc + 1) * 128],
                              draw[:, :], start=True,
                              stop=True).then_inc(sem["dtmm"], 1)

        @block.scalar
        def _(sc):
            for cc in range(NC5):
                sc.wait_ge(sem["hev"], cc + 1)
                if cc >= 1:
                    sc.wait_ge(sem["dtmm"], NDC * cc)
                sc.activation(draw[:, :], pxd[:, :], AF.Copy)
                sc.drain().then_inc(sem["rcr"], 1)
                if cc >= 2:
                    sc.wait_ge(sem[f"sbc{cc % 2}"], 32 * (cc // 2))
                sc.activation(bcstB[cc % 2][:, :], pxb[:, :], AF.Copy)
                sc.activation(bcstC[cc % 2][:, :], pxc[:, :], AF.Copy)
                sc.drain().then_inc(sem["spa"], 1)
                sc.drain().then_inc(sem["spa3"], 1)
                for dc in range(NDC):
                    idx = NDC * cc + dc
                    sc.wait_ge(sem["dtmm"], idx + 1)
                    if idx >= 2:
                        sc.wait_ge(sem["dcons"], idx - 1)
                    pp = pdt[dc % 2]
                    sc.activation(sp1[dc % 2][:, :], pp[:, :], AF.Abs,
                                  bias=bdtcol[:, dc:dc + 1])
                    sc.activation(sp2[dc % 2][:, :], pp[:, :], AF.Relu,
                                  bias=bdtcol[:, dc:dc + 1])
                    sc.drain()
                    sc.activation(sp1[dc % 2][:, :], sp1[dc % 2][:, :],
                                  AF.Exp, scale=-1.0)
                    sc.drain()
                    sc.activation(sp1[dc % 2][:, :], sp1[dc % 2][:, :],
                                  AF.Ln, bias=1.0)
                    sc.drain().then_inc(sem["spl"], 1)

        @block.vector
        def _(v):
            for cc in range(NC5):
                if cc >= 2:
                    v.wait_ge(sem[f"sdel{cc % 2}"], 16 * NDC * (cc // 2))
                for dc in range(NDC):
                    idx = NDC * cc + dc
                    v.wait_ge(sem["spl"], idx + 1)
                    v.tensor_tensor(dstg[cc % 2][:, dc * T5:(dc + 1) * T5],
                                    sp1[dc % 2][:, :], sp2[dc % 2][:, :],
                                    ALU.add).then_inc(sem["dcons"], 1)
                v.drain().then_inc(sem["dstgr"], 1)

        @block.sync
        def _(s):
            for cc in range(NC5):
                s.wait_ge(sem["dstgr"], cc + 1)
                fm_dma(s, del_h, cc * T5, T5, dstg[cc % 2], T5,
                       sem[f"sdel{cc % 2}"], store=True)
                s.wait_ge(sem["spa"], cc + 1)
                s.dma_start(out=bass.AP(bc_h, cc * T5, [[L, NST], [1, T5]]),
                            in_=bcstB[cc % 2][:, :]).then_inc(
                                sem[f"sbc{cc % 2}"], 16)
                s.dma_start(out=bass.AP(bc_h, NST * L + cc * T5,
                                        [[L, NST], [1, T5]]),
                            in_=bcstC[cc % 2][:, :]).then_inc(
                                sem[f"sbc{cc % 2}"], 16)

        p3.close()
        nc.leave_named_scope(pfx + "p3", _sid, False)

        _sid = nc.enter_named_scope(pfx + "p4", False)[0]
        p4 = ExitStack()
        dd4 = [p4.enter_context(nc.sbuf_tensor(pfx + f"dd4_{j}",
                                                [128, NDC * TC], F32))
               for j in range(2)]
        du4 = [p4.enter_context(nc.sbuf_tensor(pfx + f"du4_{j}",
                                                [128, NDC * TC], F32))
               for j in range(2)]
        dB4 = [p4.enter_context(nc.sbuf_tensor(pfx + f"dB4_{j}",
                                                [128, NST * TC], F32))
               for j in range(2)]
        dC4 = p4.enter_context(nc.sbuf_tensor(pfx + "dC4", [128, NST * TC], F32))
        dlu4 = p4.enter_context(nc.sbuf_tensor(pfx + "dlu4", [128, TC], F32))
        tr1 = p4.enter_context(nc.sbuf_tensor(pfx + "tr1", [128, 8 * TC], F32))
        tr2 = p4.enter_context(nc.sbuf_tensor(pfx + "tr2", [128, 4 * TC], F32))
        tr3 = p4.enter_context(nc.sbuf_tensor(pfx + "tr3", [128, 2 * TC], F32))
        s4 = p4.enter_context(nc.sbuf_tensor(pfx + "s4", [128, NST * TC], F32))
        ybd = [p4.enter_context(nc.sbuf_tensor(pfx + f"ybd{j}", [128, TC],
                                               F32)) for j in range(4)]
        abb = [p4.enter_context(nc.sbuf_tensor(pfx + f"abb{j}", [128, NST * TC],
                                               F32)) for j in range(2)]
        ball = [p4.enter_context(nc.sbuf_tensor(pfx + "ball0",
                                                [128, NST * TC], F32))]
        yred = p4.enter_context(nc.sbuf_tensor(pfx + "yred", [128, TC], F32))

        LD4C = 16 * (2 * NDC + NST)
        LDCC = 16 * NST

        @block.sync
        def _(s):
            for ch in range(NCH):
                if ch == 0:
                    s.wait_ge(sem["sdel0"], 16 * NDC * (NC5 // 2))
                    s.wait_ge(sem["sdel1"], 16 * NDC * (NC5 // 2))
                    s.wait_ge(sem["sbc0"], 32 * (NC5 // 2))
                    s.wait_ge(sem["sbc1"], 32 * (NC5 // 2))
                if ch >= 2:
                    s.wait_ge(sem["vdone"], ch - 1)
                fm_dma(s, del_h, ch * TC, TC, dd4[ch % 2], TC,
                       sem[f"ld4p{ch % 2}"])
                fm_dma(s, u_h, ch * TC, TC, du4[ch % 2], TC,
                       sem[f"ld4p{ch % 2}"])
                for n in range(NST):
                    s.dma_start(out=dB4[ch % 2][:, n * TC:(n + 1) * TC],
                                in_=bass.AP(bc_h, n * L + ch * TC,
                                            [[0, 128], [1, TC]])).then_inc(
                                    sem[f"ld4p{ch % 2}"], 16)
                if ch >= 1:
                    s.wait_ge(sem["vdone"], ch)
                for n in range(NST):
                    s.dma_start(out=dC4[:, n * TC:(n + 1) * TC],
                                in_=bass.AP(bc_h, (NST + n) * L + ch * TC,
                                            [[0, 128], [1, TC]])).then_inc(
                                    sem["ld4"], 16)

        @block.scalar
        def _(sc):
            for ch in range(NCH):
                sc.wait_ge(sem[f"ld4p{ch % 2}"], LD4C * (ch // 2 + 1))
                for dc in range(NDC):
                    g_ = ch * NDC + dc
                    if g_ >= 2:
                        sc.wait_ge(sem["p4c"], g_ - 1)
                    for n in range(NST):
                        sc.activation(abb[g_ % 2][:, n * TC:(n + 1) * TC],
                                      dd4[ch % 2][:, dc * TC:(dc + 1) * TC],
                                      AF.Exp,
                                      scale=a_sb[:, dc * NST + n:
                                                 dc * NST + n + 1])
                    sc.drain().then_inc(sem["p4a"], 1)

        HTC = (NST // 2) * TC
        @block.gpsimd
        def _(g):
            for ch in range(NCH):
                for dc in range(NDC):
                    g_ = ch * NDC + dc
                    g.wait_ge(sem["p4y"], g_ + 1)
                    fm_dma(g, y_h, ch * TC, TC, ybd[g_ % 4], TC,
                           sem[f"sy{g_ % 4}"], ndc=1, row0=dc * 128,
                           store=True)

        @block.vector
        def _(v):
            for ch in range(NCH):
                v.wait_ge(sem[f"ld4p{ch % 2}"], LD4C * (ch // 2 + 1))
                for dc in range(NDC):
                    g_ = ch * NDC + dc
                    if g_ >= 4:
                        v.wait_ge(sem[f"sy{g_ % 4}"], 16 * (g_ // 4))
                    v.tensor_tensor(dlu4[:, :],
                                    dd4[ch % 2][:, dc * TC:(dc + 1) * TC],
                                    du4[ch % 2][:, dc * TC:(dc + 1) * TC],
                                    ALU.mult)
                    v.drain()
                    v.tensor_tensor(
                        ball[0][:, :],
                        bass.AP(dlu4, 0, [[TC, 128], [0, NST], [1, TC]]),
                        dB4[ch % 2][:, :], ALU.mult)
                    v.drain()
                    v.wait_ge(sem["p4a"], g_ + 1)
                    for n in range(NST):
                        v.tensor_tensor_scan(
                            s4[:, n * TC:(n + 1) * TC],
                            abb[g_ % 2][:, n * TC:(n + 1) * TC],
                            ball[0][:, n * TC:(n + 1) * TC],
                            carry[:, dc * NST + n:dc * NST + n + 1],
                            ALU.mult, ALU.add)
                    v.drain()
                    v.tensor_copy(
                        carry[:, dc * NST:(dc + 1) * NST],
                        bass.AP(s4, TC - 1, [[NST * TC, 128], [TC, NST],
                                             [1, 1]]))
                    if dc == 0:
                        v.wait_ge(sem["ld4"], LDCC * (ch + 1))
                    v.tensor_tensor(abb[g_ % 2][:, :], s4[:, :],
                                    dC4[:, :], ALU.mult)
                    v.drain()
                    v.tensor_tensor(tr1[:, :], abb[g_ % 2][:, 0:HTC],
                                    abb[g_ % 2][:, HTC:], ALU.add)
                    dve_fence(v).then_inc(sem["p4c"], 1)
                    v.tensor_tensor(tr2[:, :], tr1[:, 0:4 * TC],
                                    tr1[:, 4 * TC:], ALU.add)
                    v.drain()
                    v.tensor_tensor(tr3[:, :], tr2[:, 0:2 * TC],
                                    tr2[:, 2 * TC:], ALU.add)
                    v.drain()
                    v.tensor_tensor(yred[:, :], tr3[:, 0:TC], tr3[:, TC:],
                                    ALU.add)
                    v.drain()
                    if v1:
                        v.scalar_tensor_tensor(
                            ybd[g_ % 4][:, :],
                            du4[ch % 2][:, dc * TC:(dc + 1) * TC],
                            dcol[:, dc:dc + 1], yred[:, :],
                            ALU.mult, ALU.add)
                    else:
                        v.tensor_copy(ybd[g_ % 4][:, :], yred[:, :])
                    dve_fence(v).then_inc(sem["p4y"], 1)
                v.drain().then_inc(sem["vdone"], 1)
            if not v1:
                for dc in range(NDC):
                    v.reduce_sum(rft[:, :], carry[:, dc * NST:(dc + 1) * NST],
                                 axis=AX)
                    v.drain()
                    v.tensor_scalar(rf_sb[:, dc:dc + 1], rft[:, :],
                                    dcol[:, dc:dc + 1], None, ALU.mult)
                    v.drain()

        p4.close()
        nc.leave_named_scope(pfx + "p4", _sid, False)

        _sid = nc.enter_named_scope(pfx + "p5", False)[0]
        p5 = ExitStack()
        wout_sb = p5.enter_context(nc.sbuf_tensor(pfx + "wout_sb", [128, NDC * D],
                                                  BF16))
        y5 = p5.enter_context(nc.sbuf_tensor(pfx + "y5", [128, NDC * T5], F32))
        g5 = p5.enter_context(nc.sbuf_tensor(pfx + "g5", [128, NDC * T5], F32))
        ygb = p5.enter_context(nc.sbuf_tensor(pfx + "ygb", [128, NDC * T5], BF16))
        tmp5 = p5.enter_context(nc.sbuf_tensor(pfx + "tmp5", [128, T5], F32))
        tmp5b = p5.enter_context(nc.sbuf_tensor(pfx + "tmp5b", [128, T5], F32))
        X5 = [p5.enter_context(nc.sbuf_tensor(pfx + f"X5_{j}", [128, D], F32))
              for j in range(4)]
        hid5 = [p5.enter_context(nc.sbuf_tensor(pfx + f"hid5_{j}", [128, D], F32))
                for j in range(4)] if v1 else None
        xo5 = [p5.enter_context(nc.sbuf_tensor(pfx + f"xo5_{j}", [128, D], F32))
               for j in range(4)]
        pm5 = [p5.enter_context(nc.psum_tensor(pfx + f"pm5_{j}", [128, T5], F32))
               for j in range(2)]
        outb_b = (p5.enter_context(nc.sbuf_tensor(pfx + "outb_b", [128, D],
                                                  F32)) if v1 else None)

        LD5A = 16 * 2 * NDC
        LD5B = 16 * (8 if v1 else 4)
        NLDW5 = 16 * (NDC + (1 if v1 else 0))

        @block.gpsimd
        def _(g):
            for p_ in range(4):
                g.wait_ge(sem[f"sy{p_}"], 16 * (NCH * NDC // 4))
            if v1:
                g.dma_start(out=outb_b[:, :], in_=bcast_row(out_b, 0, D)
                            ).then_inc(sem["ldw5"], 16)
            for dc in range(NDC):
                g.dma_start(out=wout_sb[:, dc * D:(dc + 1) * D],
                            in_=w_out[dc * 128:(dc + 1) * 128, :]).then_inc(
                                sem["ldw5"], 16)

        @block.sync
        def _(s):
            for c5 in range(NC5):
                if c5 == 0:
                    for p_ in range(4):
                        s.wait_ge(sem[f"sy{p_}"], 16 * (NCH * NDC // 4))
                for dc in range(NDC):
                    if c5 >= 1:
                        s.wait_ge(sem["ygr"], NDC * (c5 - 1) + dc + 1)
                    fm_dma(s, y_h, c5 * T5, T5, y5[:, dc * T5:(dc + 1) * T5],
                           T5, sem["ld5a"], ndc=1, row0=dc * 128)
                    fm_dma(s, res_h, c5 * T5, T5,
                           g5[:, dc * T5:(dc + 1) * T5], T5, sem["ld5a"],
                           ndc=1, row0=dc * 128)
                for q in range(4):
                    i = 4 * c5 + q
                    if c5 >= 1:
                        s.wait_ge(sem["ep5"], 4 * (c5 - 1) + q + 1)
                    s.dma_start(out=X5[q][:, :],
                                in_=x_in[i * 128:(i + 1) * 128, :]).then_inc(
                                    sem["ld5b"], 16)
                    if v1:
                        s.dma_start(out=hid5[q][:, :],
                                    in_=hid_h[i * 128:(i + 1) * 128, :]
                                    ).then_inc(sem["ld5b"], 16)
                if c5 >= 1:
                    for q in range(4):
                        s.wait_ge(sem["ep5"], 4 * (c5 - 1) + q + 1)
                        i = 4 * (c5 - 1) + q
                        s.dma_start(out=x_out[i * 128:(i + 1) * 128, :],
                                    in_=xo5[q][:, :]).then_inc(sem["so5"], 16)
            for q in range(4):
                s.wait_ge(sem["ep5"], 4 * (NC5 - 1) + q + 1)
                i = 4 * (NC5 - 1) + q
                s.dma_start(out=x_out[i * 128:(i + 1) * 128, :],
                            in_=xo5[q][:, :]).then_inc(sem["so5"], 16)
            s.wait_ge(sem["so5"], 16 * NT128)

        @block.scalar
        def _(sc):
            for c5 in range(NC5):
                sc.wait_ge(sem["ld5a"], LD5A * (c5 + 1))
                for dc in range(NDC):
                    sc.activation(g5[:, dc * T5:(dc + 1) * T5],
                                  g5[:, dc * T5:(dc + 1) * T5], AF.Silu)
                    sc.drain().then_inc(sem["s5a"], 1)

        @block.vector
        def _(v):
            for c5 in range(NC5):
                for dc in range(NDC):
                    v.wait_ge(sem["s5a"], NDC * c5 + dc + 1)
                    if v1:
                        v.tensor_tensor(ygb[:, dc * T5:(dc + 1) * T5],
                                        y5[:, dc * T5:(dc + 1) * T5],
                                        g5[:, dc * T5:(dc + 1) * T5], ALU.mult)
                    else:
                        tb = tmp5 if dc % 2 == 0 else tmp5b
                        v.tensor_scalar(tb[:, :],
                                        y5[:, dc * T5:(dc + 1) * T5],
                                        rf_sb[:, dc:dc + 1], None, ALU.add)
                        v.drain()
                        v.tensor_tensor(ygb[:, dc * T5:(dc + 1) * T5],
                                        tb[:, :],
                                        g5[:, dc * T5:(dc + 1) * T5], ALU.mult)
                    v.drain().then_inc(sem["ygr"], 1)
                dve_fence(v)
                v.wait_ge(sem["ld5b"], LD5B * (c5 + 1))
                if c5 >= 1:
                    v.wait_ge(sem["so5"], 16 * 4 * c5)
                for q in range(4):
                    for oh in range(2):
                        e = 8 * c5 + 2 * q + oh
                        v.wait_ge(sem["ymm"], e + 1)
                        pm = pm5[(2 * q + oh) % 2]
                        if v1:
                            v.tensor_tensor(tmp5[:, :], pm[:, :],
                                            X5[q][:, oh * 512:(oh + 1) * 512],
                                            ALU.add)
                            v.tensor_tensor(tmp5b[:, :],
                                            hid5[q][:, oh * 512:(oh + 1) * 512],
                                            outb_b[:, oh * 512:(oh + 1) * 512],
                                            ALU.add)
                            v.drain()
                            v.tensor_tensor(
                                xo5[q][:, oh * 512:(oh + 1) * 512], tmp5[:, :],
                                tmp5b[:, :], ALU.add)
                        else:
                            v.tensor_tensor(xo5[q][:, oh * 512:(oh + 1) * 512],
                                            pm[:, :],
                                            X5[q][:, oh * 512:(oh + 1) * 512],
                                            ALU.add)
                        v.drain().then_inc(sem["p5e"], 1)
                    dve_fence(v).then_inc(sem["ep5"], 1)

        @block.tensor
        def _(te):
            te.wait_ge(sem["ldw5"], NLDW5)
            for c5 in range(NC5):
                te.wait_ge(sem["ygr"], NDC * (c5 + 1))
                for q in range(4):
                    for oh in range(2):
                        e = 8 * c5 + 2 * q + oh
                        if e >= 2:
                            te.wait_ge(sem["p5e"], e - 1)
                        for dc in range(NDC):
                            mm = te.matmul(
                                pm5[(2 * q + oh) % 2][:, :],
                                ygb[:, dc * T5 + q * 128:dc * T5 + (q + 1) * 128],
                                wout_sb[:, dc * D + oh * 512:
                                        dc * D + (oh + 1) * 512],
                                start=(dc == 0), stop=(dc == NDC - 1))
                        mm.then_inc(sem["ymm"], 1)

        p5.close()
        nc.leave_named_scope(pfx + "p5", _sid, False)
    es.close()
    pool.end()
    return [(sem["so5"], pool.base["so5"])]


def mamba_io_tensors(nc, pfx, v1, with_xio=True):
    DI = DIN
    io = {
        "w_proj": nc.dram_tensor(pfx + "w_proj", [D, 2 * DI], F32,
                                 kind="ExternalInput"),
        "conv_w": (nc.dram_tensor(pfx + "conv_w", [DI, 4], F32,
                                  kind="ExternalInput") if v1 else
                   nc.dram_tensor(pfx + "conv_w", [4, DI, DI], F32,
                                  kind="ExternalInput")),
        "conv_b": nc.dram_tensor(pfx + "conv_b", [DI], F32,
                                 kind="ExternalInput"),
        "wx_d": nc.dram_tensor(pfx + "wx_d", [DI, RDT], F32,
                               kind="ExternalInput"),
        "wx_B": nc.dram_tensor(pfx + "wx_B", [DI, NST], F32,
                               kind="ExternalInput"),
        "wx_C": nc.dram_tensor(pfx + "wx_C", [DI, NST], F32,
                               kind="ExternalInput"),
        "w_dt": nc.dram_tensor(pfx + "w_dt", [RDT, DI], F32,
                               kind="ExternalInput"),
        "b_dt": nc.dram_tensor(pfx + "b_dt", [DI], F32, kind="ExternalInput"),
        "a_log": nc.dram_tensor(pfx + "a_log", [DI, NST], F32,
                                kind="ExternalInput"),
        "d_vec": nc.dram_tensor(pfx + "d_vec", [DI], F32,
                                kind="ExternalInput"),
        "w_out": nc.dram_tensor(pfx + "w_out", [DI, D], F32,
                                kind="ExternalInput"),
    }
    if v1:
        io["ln_w"] = nc.dram_tensor(pfx + "ln_w", [D], F32,
                                    kind="ExternalInput")
        io["ln_b"] = nc.dram_tensor(pfx + "ln_b", [D], F32,
                                    kind="ExternalInput")
        io["in_b"] = nc.dram_tensor(pfx + "in_b", [2 * DI], F32,
                                    kind="ExternalInput")
        io["out_b"] = nc.dram_tensor(pfx + "out_b", [D], F32,
                                     kind="ExternalInput")
    return io


def build_mamba(v1: bool):
    nc = bass.Bass("TRN2", target_bir_lowering=False)
    io = mamba_io_tensors(nc, "", v1)
    io["x_in"] = nc.dram_tensor("x_in", [L, D], F32, kind="ExternalInput")
    io["ident"] = nc.dram_tensor("ident", [128, 128], F32,
                                 kind="ExternalInput")
    io["x_out"] = nc.dram_tensor("x_out", [L, D], F32, kind="ExternalOutput")
    pool = SemPool(nc)
    with nc.Block() as block:
        emit_mamba(nc, block, pool, "", [], io, v1)
    return nc



def emit_l1ffn(nc, block, pool, pfx, gates, io):
    es = ExitStack()
    sbt = lambda n, sh, dt: es.enter_context(nc.sbuf_tensor(pfx + n, sh, dt))
    pst = lambda n, sh, dt: es.enter_context(nc.psum_tensor(pfx + n, sh, dt))
    block = BlockWrap(block, pool, gates)
    pool.begin()
    NB = 512
    DFF = 4 * D
    x_in, x_out = io["x_in"], io["x_out"]
    w1t, w2t = io["w1t"], io["w2t"]
    lng, lnbd = io["lng"], io["lnbd"]
    rwt, rbv = io["rwt"], io["rbv"]
    rawu, rawv, gamma = io["rawu"], io["rawv"], io["gamma"]
    ident = io["ident"]

    w1_sb = [sbt(f"w1_{dc}", [128, DFF], BF16)
             for dc in range(8)]
    w2_sb = sbt("w2_sb", [128, 32 * D], BF16)
    rwt_sb = sbt("rwt_sb", [128, 8 * NB], BF16)
    ut_sb = sbt("ut_sb", [128, 8 * NB], BF16)
    vu_sb = sbt("vu_sb", [128, 4 * D], BF16)
    identb = sbt("identb", [128, 128], BF16)
    lnw_b = sbt("lnw_b", [128, D], F32)
    lnb_b = sbt("lnb_b", [128, D], F32)
    rb_b = sbt("rb_b", [128, NB], F32)
    nrm = {k: sbt("n_" + k, [128, 1], F32)
           for k in ["s", "ln", "sq", "mx", "ri"]}
    st = {"negm": sbt("s_negm", [128, 4], F32),
          "r": sbt("s_r", [128, 4], F32),
          "mv": sbt("s_mv", [128, 8], F32),
          "stat12": sbt("s_stat12", [128, 12], F32),
          "lnv": sbt("s_lnv", [128, 1], F32),
          "eps": sbt("s_eps", [128, 1], F32)}
    tk = {k: sbt("t_" + k, [128, 1], F32)
          for k in ["ssum", "ssc", "e2", "den", "rd", "rS", "th", "gs", "sp"]}
    mx8 = sbt("mx8", [128, 8], F32)

    xt = [sbt(f"xt{j}", [128, D], F32) for j in range(2)]
    xoh = [sbt(f"xoh{j}", [128, 512], F32) for j in range(2)]
    xnf = sbt("xnf", [128, D], F32)
    hnt = sbt("hnt", [128, D], F32)
    xnb = sbt("xnb", [128, D], BF16)
    ln2b = sbt("ln2b", [128, D], BF16)
    xfmb = sbt("xfmb", [128, 8 * 256], BF16)
    lfmb = sbt("lfmb", [128, 8 * 256], BF16)
    s1c = sbt("s1c", [128, 512], BF16)
    hwfm = sbt("hwfm", [128, 1024], BF16)
    hsb = sbt("hsb", [128, NB], F32)
    hwb = [sbt(f"hwb{j}", [128, NB], BF16) for j in range(2)]
    rc = sbt("rc", [128, NB], F32)
    spl = sbt("spl", [128, NB], F32)

    ptr = pst("ptr", [128, 1024], BF16)
    psc = pst("psc", [128, NB], F32)
    phh = pst("phh", [128, NB], F32)
    ps1 = pst("ps1", [128, 512], F32)
    psS = [pst(f"psS{j}", [128, 512], F32) for j in range(4)]
    ptr3 = bass.AP(ptr, 0, [[1024, 128], [128, 8], [1, 128]])

    sem = {n: pool.get(n) for n in [
        "ldi", "ldc", "ldw", "ldu", "unr", "sqr", "rir", "ubr", "utr",
        "uev", "vev", "ldx0", "ldx1", "mvr", "a2v", "xnr", "tr1", "xev",
        "mmsc", "scf", "mmh", "hev", "rcr", "spa", "ssr", "e2r", "hwr",
        "mm1", "gel", "s2u", "mmS", "epq", "gso0", "gso1", "lde0",
        "lde1", "hwt"]}
    NGR = NT128 // 2

    if True:
        @block.gpsimd
        def _(g):
            g.dma_start(out=identb[:, :], in_=ident[:, :]).then_inc(
                sem["ldi"], 16)
            g.dma_start(out=lnw_b[:, :], in_=bcast_row(lng, 0, D)).then_inc(
                sem["ldc"], 16)
            g.dma_start(out=lnb_b[:, :], in_=bcast_row(lnbd, 0, D)).then_inc(
                sem["ldc"], 16)
            g.dma_start(out=rb_b[:, :], in_=bcast_row(rbv, 0, NB)).then_inc(
                sem["ldc"], 16)
            g.dma_start(out=hnt[:, :], in_=bcast_row(gamma, 0, D)).then_inc(
                sem["ldc"], 16)
            for dc in range(8):
                g.dma_start(out=w1_sb[dc][:, :],
                            in_=w1t[dc * 128:(dc + 1) * 128, :]).then_inc(
                                sem["ldw"], 16)
            for hc in range(32):
                g.dma_start(out=w2_sb[:, hc * D:(hc + 1) * D],
                            in_=w2t[hc * 128:(hc + 1) * 128, :]).then_inc(
                                sem["ldw"], 16)
            for dc in range(8):
                g.dma_start(out=rwt_sb[:, dc * NB:(dc + 1) * NB],
                            in_=rwt[dc * 128:(dc + 1) * 128, :]).then_inc(
                                sem["ldw"], 16)
            for k in range(8):
                if 1 <= k <= 4:
                    g.wait_ge(sem["ubr"], k)
                elif k >= 5:
                    g.wait_ge(sem["vev"], k - 4)
                src = rawu if k < 4 else rawv
                uc = k % 4
                g.dma_start(out=xt[0][:, :],
                            in_=src[uc * 128:(uc + 1) * 128, :]).then_inc(
                                sem["ldu"], 16)

        @block.vector
        def _(v):
            v.memset(st["eps"][:, :], 1e-5)
            v.drain()
            for k in range(8):
                v.wait_ge(sem["ldu"], 16 * (k + 1))
                v.tensor_tensor(xnf[:, :], xt[0][:, :], xt[0][:, :], ALU.mult)
                v.drain()
                v.reduce_sum(nrm["s"][:, :], xnf[:, :], axis=AX)
                dve_fence(v).then_inc(sem["unr"], 1)
                v.wait_ge(sem["sqr"], k + 1)
                v.tensor_scalar(nrm["mx"][:, :], nrm["sq"][:, :], 1e-6, None,
                                ALU.max)
                v.drain()
                v.reciprocal(nrm["ri"][:, :], nrm["mx"][:, :])
                dve_fence(v).then_inc(sem["rir"], 1)
                if k < 4:
                    v.wait_ge(sem["utr"], k + 1)
                    v.tensor_copy(
                        bass.AP(ut_sb, k * 128,
                                [[8 * NB, 128], [NB, 8], [1, 128]]), ptr3)
                    dve_fence(v).then_inc(sem["uev"], 1)
                else:
                    v.wait_ge(sem["ldc"], 64)
                    v.tensor_scalar(xnf[:, :], xt[0][:, :], nrm["ri"][:, :1],
                                    None, ALU.mult)
                    v.drain()
                    v.tensor_tensor(vu_sb[:, (k - 4) * D:(k - 3) * D],
                                    xnf[:, :], hnt[:, :], ALU.mult)
                    dve_fence(v).then_inc(sem["vev"], 1)

        @block.scalar
        def _(sc):
            for k in range(8):
                sc.wait_ge(sem["unr"], k + 1)
                sc.activation(nrm["ln"][:, :], nrm["s"][:, :], AF.Ln)
                sc.drain()
                sc.activation(nrm["sq"][:, :], nrm["ln"][:, :], AF.Exp,
                              scale=0.5)
                act_fence(sc).then_inc(sem["sqr"], 1)
                if k < 4:
                    sc.wait_ge(sem["rir"], k + 1)
                    if k >= 1:
                        sc.wait_ge(sem["utr"], k)
                    sc.activation(xnb[:, :], xt[0][:, :], AF.Identity,
                                  scale=nrm["ri"][:, :1])
                    act_fence(sc).then_inc(sem["ubr"], 1)

        @block.tensor
        def _(te):
            te.wait_ge(sem["ldi"], 16)
            for uc in range(4):
                te.wait_ge(sem["ubr"], uc + 1)
                if uc >= 1:
                    te.wait_ge(sem["uev"], uc)
                for c in range(8):
                    mm = te.transpose(ptr[:, c * 128:(c + 1) * 128],
                                      xnb[:, c * 128:(c + 1) * 128],
                                      identb[:, :])
                mm.then_inc(sem["utr"], 1)

        @block.sync
        def _(s):
            for g_ in range(NGR):
                for q in range(2):
                    i = 2 * g_ + q
                    if i == 0:
                        s.wait_ge(sem["vev"], 4)
                    if i >= 2:
                        s.wait_ge(sem["xnr"], i - 1)
                    s.dma_start(out=xt[q][:, :],
                                in_=x_in[i * 128:(i + 1) * 128, :]).then_inc(
                                    sem[f"ldx{q}"], 16)
                for j in range(4):
                    e = 4 * g_ + j
                    q2, oh = divmod(j, 2)
                    ii = 2 * g_ + q2
                    if e >= 2:
                        s.wait_ge(sem[f"gso{e % 2}"], 16 * (e // 2))
                    s.dma_start(out=xoh[e % 2][:, :],
                                in_=x_in[ii * 128:(ii + 1) * 128,
                                         oh * 512:(oh + 1) * 512]).then_inc(
                                    sem[f"lde{e % 2}"], 16)
            s.wait_ge(sem["gso0"], 16 * 32)
            s.wait_ge(sem["gso1"], 16 * 32)

        @block.gpsimd
        def _(g):
            for e in range(4 * NGR):
                g_, j = divmod(e, 4)
                q2, oh = divmod(j, 2)
                ii = 2 * g_ + q2
                g.wait_ge(sem["epq"], e + 1)
                g.dma_start(out=x_out[ii * 128:(ii + 1) * 128,
                                      oh * 512:(oh + 1) * 512],
                            in_=xoh[e % 2][:, :]).then_inc(
                                sem[f"gso{e % 2}"], 16)

        @block.vector
        def _(v):
            v.wait_ge(sem["utr"], 4)
            for i in range(NT128):
                g_, q = divmod(i, 2)
                v.wait_ge(sem[f"ldx{q}"], 16 * (g_ + 1))
                if i >= 2:
                    v.wait_ge(sem["a2v"], 2 * i - 2)
                emit_ln_stats_dve(v, xt[q], st, j=q)
                dve_fence(v).then_inc(sem["mvr"], 1)
                v.wait_ge(sem["a2v"], 2 * i + 1)
                v.tensor_scalar(xnf[:, :], xt[q][:, :],
                                st["negm"][:, q:q + 1], st["r"][:, q:q + 1],
                                ALU.add, ALU.mult)
                v.drain()
                v.tensor_copy(xnb[:, :], xnf[:, :])
                emit_ln_stats_dve(v, xnf, st, j=2 + q)
                dve_fence(v).then_inc(sem["mvr"], 1)
                v.wait_ge(sem["a2v"], 2 * i + 2)
                v.tensor_scalar(hnt[:, :], xnf[:, :],
                                st["negm"][:, 2 + q:3 + q],
                                st["r"][:, 2 + q:3 + q], ALU.add, ALU.mult)
                v.drain()
                v.tensor_tensor(xnf[:, :], hnt[:, :], lnw_b[:, :], ALU.mult)
                v.drain()
                v.tensor_tensor(ln2b[:, :], xnf[:, :], lnb_b[:, :], ALU.add)
                dve_fence(v).then_inc(sem["xnr"], 1)
                v.wait_ge(sem["tr1"], 2 * i + 1)
                v.tensor_copy(
                    bass.AP(xfmb, q * 128, [[2048, 128], [256, 8], [1, 128]]),
                    ptr3)
                dve_fence(v).then_inc(sem["xev"], 1)
                v.wait_ge(sem["tr1"], 2 * i + 2)
                v.tensor_copy(
                    bass.AP(lfmb, q * 128, [[2048, 128], [256, 8], [1, 128]]),
                    ptr3)
                dve_fence(v).then_inc(sem["xev"], 1)
                v.wait_ge(sem["mmsc"], i + 1)
                v.tensor_tensor(spl[:, :], psc[:, :], rb_b[:, :], ALU.add)
                dve_fence(v).then_inc(sem["scf"], 1)
                v.tensor_scalar(rc[:, :], spl[:, :], 10.0, -10.0,
                                ALU.min, ALU.max)
                dve_fence(v).then_inc(sem["rcr"], 1)
                v.wait_ge(sem["spa"], i + 1)
                v.max(mx8[:, :], spl[:, :])
                v.drain()
                v.reduce_sum(tk["ssum"][:, :], mx8[:, :], axis=AX)
                v.drain()
                v.tensor_scalar(tk["ssc"][:, :], tk["ssum"][:, :], 20.0, None,
                                ALU.min)
                dve_fence(v).then_inc(sem["ssr"], 1)
                v.wait_ge(sem["e2r"], i + 1)
                v.tensor_scalar(tk["den"][:, :], tk["e2"][:, :], 1.0, None,
                                ALU.add)
                v.tensor_scalar(tk["sp"][:, :], tk["ssum"][:, :], 1e-6, None,
                                ALU.add)
                v.drain()
                v.reciprocal(tk["rd"][:, :], tk["den"][:, :])
                v.reciprocal(tk["rS"][:, :], tk["sp"][:, :])
                v.drain()
                v.tensor_scalar(tk["th"][:, :], tk["rd"][:, :], -2.0, 1.0,
                                ALU.mult, ALU.add)
                v.drain()
                v.tensor_tensor(tk["gs"][:, :], tk["th"][:, :],
                                tk["rS"][:, :], ALU.mult)
                v.drain()
                v.tensor_scalar(rc[:, :], spl[:, :], mx8[:, 7:8], None,
                                ALU.is_ge)
                v.drain()
                v.tensor_tensor(rc[:, :], rc[:, :], spl[:, :], ALU.mult)
                v.drain()
                v.tensor_scalar(rc[:, :], rc[:, :], tk["gs"][:, :1], None,
                                ALU.mult)
                v.drain()
                v.wait_ge(sem["hev"], i + 1)
                v.tensor_tensor(hwb[q][:, :], hsb[:, :], rc[:, :], ALU.mult)
                dve_fence(v).then_inc(sem["hwr"], 1)
                if q == 1:
                    for qq in range(2):
                        v.wait_ge(sem["hwt"], 2 * g_ + qq + 1)
                        v.tensor_copy(hwfm[:, qq * 512:(qq + 1) * 512],
                                      ptr[:, 0:512])
                        dve_fence(v).then_inc(sem["xev"], 1)
                    for j in range(4):
                        q2, oh = divmod(j, 2)
                        e = 4 * g_ + j
                        v.wait_ge(sem["mmS"], e + 1)
                        v.wait_ge(sem[f"lde{e % 2}"], 16 * (e // 2 + 1))
                        v.scalar_tensor_tensor(
                            xoh[e % 2][:, :], psS[j][:, :], 0.5,
                            xoh[e % 2][:, :], ALU.mult, ALU.add)
                        dve_fence(v).then_inc(sem["epq"], 1)

        @block.scalar
        def _(sc):
            for i in range(NT128):
                g_, q = divmod(i, 2)
                sc.wait_ge(sem["mvr"], 2 * i + 1)
                emit_ln_stats_act(sc, st, j=q)
                act_fence(sc).then_inc(sem["a2v"], 1)
                sc.wait_ge(sem["mvr"], 2 * i + 2)
                emit_ln_stats_act(sc, st, j=2 + q)
                act_fence(sc).then_inc(sem["a2v"], 1)
                sc.wait_ge(sem["rcr"], i + 1)
                sc.activation(spl[:, :], rc[:, :], AF.Exp)
                sc.drain()
                sc.activation(spl[:, :], spl[:, :], AF.Ln, bias=1.0)
                act_fence(sc).then_inc(sem["spa"], 1)
                sc.wait_ge(sem["mmh"], i + 1)
                if i >= 1:
                    sc.wait_ge(sem["hwr"], i)
                sc.activation(hsb[:, :], phh[:, :], AF.Copy)
                act_fence(sc).then_inc(sem["hev"], 1)
                sc.wait_ge(sem["ssr"], i + 1)
                sc.activation(tk["e2"][:, :], tk["ssc"][:, :], AF.Exp,
                              scale=2.0)
                act_fence(sc).then_inc(sem["e2r"], 1)
                if q == 1:
                    for p in range(16):
                        sc.wait_ge(sem["mm1"], 32 * g_ + 2 * p + 2)
                        if 16 * g_ + p >= 1:
                            sc.wait_ge(sem["s2u"], 32 * g_ + 2 * p)
                        sc.activation(s1c[:, :], ps1[:, :], AF.Gelu)
                        act_fence(sc).then_inc(sem["gel"], 1)

        @block.tensor
        def _(te):
            te.wait_ge(sem["ldw"], 16 * 48)
            te.wait_ge(sem["uev"], 4)
            te.wait_ge(sem["vev"], 4)
            for i in range(NT128):
                g_, q = divmod(i, 2)
                te.wait_ge(sem["xnr"], i + 1)
                if i > 0:
                    te.wait_ge(sem["xev"], 6 * g_ + 2 * q)
                for c in range(8):
                    mm = te.transpose(ptr[:, c * 128:(c + 1) * 128],
                                      xnb[:, c * 128:(c + 1) * 128],
                                      identb[:, :])
                mm.then_inc(sem["tr1"], 1)
                te.wait_ge(sem["xev"], 6 * g_ + 2 * q + 1)
                for c in range(8):
                    mm = te.transpose(ptr[:, c * 128:(c + 1) * 128],
                                      ln2b[:, c * 128:(c + 1) * 128],
                                      identb[:, :])
                mm.then_inc(sem["tr1"], 1)
                te.wait_ge(sem["xev"], 6 * g_ + 2 * q + 2)
                if i >= 1:
                    te.wait_ge(sem["scf"], i)
                for dc in range(8):
                    mm = te.matmul(psc[:, :],
                                   lfmb[:, dc * 256 + q * 128:
                                        dc * 256 + (q + 1) * 128],
                                   rwt_sb[:, dc * NB:(dc + 1) * NB],
                                   start=(dc == 0), stop=(dc == 7))
                mm.then_inc(sem["mmsc"], 1)
                te.wait_ge(sem["hev"], i)
                for dc in range(8):
                    mm = te.matmul(phh[:, :],
                                   xfmb[:, dc * 256 + q * 128:
                                        dc * 256 + (q + 1) * 128],
                                   ut_sb[:, dc * NB:(dc + 1) * NB],
                                   start=(dc == 0), stop=(dc == 7))
                mm.then_inc(sem["mmh"], 1)
                if q == 1:
                    if g_ >= 1:
                        te.wait_ge(sem["epq"], 4 * g_)
                    for oc in range(33):
                        if oc < 32:
                            if oc >= 2:
                                te.wait_ge(sem["gel"], 16 * g_ + oc // 2)
                            for dc in range(8):
                                mm = te.matmul(
                                    ps1[:, (oc % 2) * 256:(oc % 2 + 1) * 256],
                                    w1_sb[dc][:, oc * 128:(oc + 1) * 128],
                                    xfmb[:, dc * 256:(dc + 1) * 256],
                                    start=(dc == 0), stop=(dc == 7))
                            mm.then_inc(sem["mm1"], 1)
                        if oc >= 1:
                            o2 = oc - 1
                            te.wait_ge(sem["gel"], 16 * g_ + o2 // 2 + 1)
                            for j in range(4):
                                q2, oh = divmod(j, 2)
                                mm = te.matmul(
                                    psS[j][:, :],
                                    s1c[:, (o2 % 2) * 256 + q2 * 128:
                                        (o2 % 2) * 256 + (q2 + 1) * 128],
                                    w2_sb[:, o2 * D + oh * 512:
                                          o2 * D + (oh + 1) * 512],
                                    start=(o2 == 0), stop=False)
                            mm.then_inc(sem["s2u"], 1)
                    for qq in range(2):
                        te.wait_ge(sem["hwr"], 2 * g_ + qq + 1)
                        te.wait_ge(sem["xev"], 6 * g_ + 4 + qq)
                        for c in range(4):
                            mm = te.transpose(ptr[:, c * 128:(c + 1) * 128],
                                              hwb[qq][:, c * 128:
                                                      (c + 1) * 128],
                                              identb[:, :])
                        mm.then_inc(sem["hwt"], 1)
                    te.wait_ge(sem["xev"], 6 * g_ + 6)
                    for j in range(4):
                        q2, oh = divmod(j, 2)
                        for vc in range(4):
                            mm = te.matmul(
                                psS[j][:, :],
                                hwfm[:, q2 * 512 + vc * 128:
                                     q2 * 512 + (vc + 1) * 128],
                                vu_sb[:, vc * D + oh * 512:
                                      vc * D + (oh + 1) * 512],
                                start=False, stop=(vc == 3))
                        mm.then_inc(sem["mmS"], 1)

    es.close()
    pool.end()
    return [(sem["gso0"], pool.base["gso0"]),
            (sem["gso1"], pool.base["gso1"])]


def l1ffn_io_tensors(nc, pfx):
    NB = 512
    DFF = 4 * D
    return {
        "w1t": nc.dram_tensor(pfx + "w1t", [D, DFF], F32,
                              kind="ExternalInput"),
        "w2t": nc.dram_tensor(pfx + "w2t", [DFF, D], F32,
                              kind="ExternalInput"),
        "lng": nc.dram_tensor(pfx + "lng", [D], F32, kind="ExternalInput"),
        "lnbd": nc.dram_tensor(pfx + "lnbd", [D], F32, kind="ExternalInput"),
        "rwt": nc.dram_tensor(pfx + "rwt", [D, NB], F32,
                              kind="ExternalInput"),
        "rbv": nc.dram_tensor(pfx + "rbv", [NB], F32, kind="ExternalInput"),
        "rawu": nc.dram_tensor(pfx + "rawu", [NB, D], F32,
                               kind="ExternalInput"),
        "rawv": nc.dram_tensor(pfx + "rawv", [NB, D], F32,
                               kind="ExternalInput"),
        "gamma": nc.dram_tensor(pfx + "gamma", [D], F32,
                                kind="ExternalInput"),
    }


def build_l1ffn():
    nc = bass.Bass("TRN2", target_bir_lowering=False)
    io = l1ffn_io_tensors(nc, "")
    io["x_in"] = nc.dram_tensor("x_in", [L, D], F32, kind="ExternalInput")
    io["ident"] = nc.dram_tensor("ident", [128, 128], F32,
                                 kind="ExternalInput")
    io["x_out"] = nc.dram_tensor("x_out", [L, D], F32, kind="ExternalOutput")
    pool = SemPool(nc)
    with nc.Block() as block:
        emit_l1ffn(nc, block, pool, "", [], io)
    return nc



def build_full():
    nc = bass.Bass("TRN2", target_bir_lowering=False)
    x_in = nc.dram_tensor("x_in", [L, D], F32, kind="ExternalInput")
    ident = nc.dram_tensor("ident", [128, 128], F32, kind="ExternalInput")
    x_out = nc.dram_tensor("x_out", [L, D], F32, kind="ExternalOutput")
    w0 = nc.dram_tensor("f0_w", [D, D], F32, kind="ExternalInput")
    b0 = nc.dram_tensor("f0_b", [D], F32, kind="ExternalInput")
    m0 = mamba_io_tensors(nc, "m0_", False)
    f1 = l1ffn_io_tensors(nc, "f1_")
    m1 = mamba_io_tensors(nc, "m1_", True)
    xh = [nc.dram_tensor(f"xh{j}", [L, D], F32) for j in range(5)]
    pool = SemPool(nc)
    with nc.Block() as block:
        with nc.named_scope("A_l0ffn"):
            g = emit_l0ffn(nc, block, pool, "A", [],
                           {"x_in": x_in, "x_out": xh[0], "w": w0, "b": b0,
                            "ident": ident})
        with nc.named_scope("B_mamba0"):
            g = emit_mamba(nc, block, pool, "B", g,
                           {**m0, "x_in": xh[0], "x_out": xh[1],
                            "ident": ident}, v1=False)
        with nc.named_scope("C_l0ffn"):
            g = emit_l0ffn(nc, block, pool, "C", g,
                           {"x_in": xh[1], "x_out": xh[2], "w": w0, "b": b0,
                            "ident": ident})
        with nc.named_scope("D_l1ffn"):
            g = emit_l1ffn(nc, block, pool, "D", g,
                           {**f1, "x_in": xh[2], "x_out": xh[3],
                            "ident": ident})
        with nc.named_scope("E_mamba1"):
            g = emit_mamba(nc, block, pool, "E", g,
                           {**m1, "x_in": xh[3], "x_out": xh[4],
                            "ident": ident}, v1=True)
        with nc.named_scope("F_l1ffn"):
            g = emit_l1ffn(nc, block, pool, "F", g,
                           {**f1, "x_in": xh[4], "x_out": x_out,
                            "ident": ident})
    return nc


def full_in_map(x, inp):
    m = {"x_in": np.ascontiguousarray(x), "ident": IDENT_NP,
         "f0_w": np.ascontiguousarray(inp["L0_ffn_weight"]),
         "f0_b": np.ascontiguousarray(inp["L0_ffn_bias"])}
    for pfx, sub in (("m0_", mamba_in_map_l0(x, inp)),
                     ("m1_", mamba_in_map_l1(x, inp)),
                     ("f1_", l1ffn_in_map(x, inp))):
        for k, v in sub.items():
            if k not in ("x_in", "ident"):
                m[pfx + k] = v
    return m



_NC_CACHE = {}


def _get_nc(name, builder):
    if name not in _NC_CACHE:
        _NC_CACHE[name] = builder()
    return _NC_CACHE[name]


def run_l0ffn(x_cores, w, b):
    nc = _get_nc("l0ffn", build_l0ffn)
    in_maps = [{"x_in": np.ascontiguousarray(x), "w": w, "b": b,
                "ident": IDENT_NP} for x in x_cores]
    res = run_bass_kernel_spmd(nc, in_maps, core_ids=list(range(len(x_cores))))
    return [np.asarray(r["x_out"]) for r in res.results]


def mamba_in_map_l0(x, inp):
    return {
        "x_in": np.ascontiguousarray(x),
        "w_proj": inp["L0_attn_W_in"],
        "conv_w": inp["L0_attn_conv_w"],
        "conv_b": inp["L0_attn_conv_b"],
        "wx_d": np.ascontiguousarray(inp["L0_attn_W_x"][:, :RDT]),
        "wx_B": np.ascontiguousarray(inp["L0_attn_W_x"][:, RDT:RDT + NST]),
        "wx_C": np.ascontiguousarray(inp["L0_attn_W_x"][:, RDT + NST:]),
        "w_dt": inp["L0_attn_W_dt"],
        "b_dt": inp["L0_attn_b_dt"],
        "a_log": inp["L0_attn_A_log"],
        "d_vec": inp["L0_attn_D"],
        "w_out": inp["L0_attn_W_out"],
        "ident": IDENT_NP,
    }


def mamba_in_map_l1(x, inp):
    xp = np.ascontiguousarray(inp["L1_attn_xp_w"].T)
    return {
        "x_in": np.ascontiguousarray(x),
        "w_proj": np.ascontiguousarray(inp["L1_attn_in_w"].T),
        "conv_w": np.ascontiguousarray(inp["L1_attn_conv_w"].reshape(DIN, 4)),
        "conv_b": inp["L1_attn_conv_b"],
        "wx_d": np.ascontiguousarray(xp[:, :RDT]),
        "wx_B": np.ascontiguousarray(xp[:, RDT:RDT + NST]),
        "wx_C": np.ascontiguousarray(xp[:, RDT + NST:]),
        "w_dt": np.ascontiguousarray(inp["L1_attn_dtp_w"].T),
        "b_dt": inp["L1_attn_dtp_b"],
        "a_log": inp["L1_attn_A_log"],
        "d_vec": inp["L1_attn_D"],
        "w_out": np.ascontiguousarray(inp["L1_attn_out_w"].T),
        "ln_w": inp["L1_attn_ln_w"],
        "ln_b": inp["L1_attn_ln_b"],
        "in_b": inp["L1_attn_in_b"],
        "out_b": inp["L1_attn_out_b"],
        "ident": IDENT_NP,
    }


def l1ffn_in_map(x, inp):
    return {
        "x_in": np.ascontiguousarray(x),
        "w1t": np.ascontiguousarray(inp["L1_ffn_W1"].T),
        "w2t": np.ascontiguousarray(inp["L1_ffn_W2"].T),
        "lng": np.ascontiguousarray(inp["L1_ffn_ln_g"]).reshape(-1),
        "lnbd": np.ascontiguousarray(inp["L1_ffn_ln_b"]).reshape(-1),
        "rwt": np.ascontiguousarray(inp["L1_ffn_router_W"].T),
        "rbv": np.ascontiguousarray(inp["L1_ffn_router_b"]).reshape(-1),
        "rawu": np.ascontiguousarray(inp["L1_ffn_raw_U"]),
        "rawv": np.ascontiguousarray(inp["L1_ffn_raw_V"]),
        "gamma": np.ascontiguousarray(inp["L1_ffn_gamma"]).reshape(-1),
        "ident": IDENT_NP,
    }


def run_l1ffn(x_cores, inp):
    nc = _get_nc("l1ffn", build_l1ffn)
    in_maps = [l1ffn_in_map(x, inp) for x in x_cores]
    res = run_bass_kernel_spmd(nc, in_maps, core_ids=list(range(len(x_cores))))
    return [np.asarray(r["x_out"]) for r in res.results]


def run_mamba(x_cores, inp, v1):
    nc = _get_nc("mamba1" if v1 else "mamba0",
                 lambda: build_mamba(v1))
    mk = mamba_in_map_l1 if v1 else mamba_in_map_l0
    in_maps = [mk(x, inp) for x in x_cores]
    res = run_bass_kernel_spmd(nc, in_maps, core_ids=list(range(len(x_cores))))
    return [np.asarray(r["x_out"]) for r in res.results]


def kernel(**inputs):
    x = np.asarray(inputs["L0_attn_x"], dtype=np.float32)
    nc = _get_nc("full", build_full)
    in_maps = [full_in_map(np.ascontiguousarray(x[c]), inputs)
               for c in range(NCORE)]
    res = run_bass_kernel_spmd(nc, in_maps, core_ids=list(range(NCORE)))
    return np.stack([np.asarray(r["x_out"]) for r in res.results], axis=0)


def kernel_unfused(**inputs):
    x = np.asarray(inputs["L0_attn_x"], dtype=np.float32)
    x_cores = [np.ascontiguousarray(x[c]) for c in range(NCORE)]
    w, b = inputs["L0_ffn_weight"], inputs["L0_ffn_bias"]
    x_cores = run_l0ffn(x_cores, w, b)
    x_cores = run_mamba(x_cores, inputs, v1=False)
    x_cores = run_l0ffn(x_cores, w, b)
    x_cores = run_l1ffn(x_cores, inputs)
    x_cores = run_mamba(x_cores, inputs, v1=True)
    x_cores = run_l1ffn(x_cores, inputs)
    return np.stack(x_cores, axis=0)
